# version 13
# baseline (speedup 1.0000x reference)
import numpy as np
import ml_dtypes
from contextlib import ExitStack

import concourse.bass as bass
import concourse.mybir as mybir
from concourse.bass_utils import run_bass_kernel_spmd


F32 = mybir.dt.float32
BF16 = mybir.dt.bfloat16
AF = mybir.ActivationFunctionType
ALU = mybir.AluOpType
AX = mybir.AxisListType

D = 1024
H = 4096
E3 = 3 * D
EPS = 1e-5
N_CORES = 8
NW1 = 32
NW2 = 32


class Sem:
    def __init__(self, nc, es, name):
        self.h = es.enter_context(nc.semaphore(name))
        self.name = name
        self.n = 0

    def inc(self, inst, k=1):
        inst.then_inc(self.h, k)
        self.n += k
        return inst


class DSem:

    def __init__(self, nc, es, name, nslots=2):
        self.s = [Sem(nc, es, f"{name}{i}") for i in range(nslots)]
        self.nslots = nslots
        self.name = name

    def h(self, slot):
        return self.s[slot % self.nslots].h

    def inc(self, inst, slot, k=16):
        inst.then_inc(self.s[slot % self.nslots].h, k)
        self.s[slot % self.nslots].n += k
        return inst

    def totals(self):
        return [x.n for x in self.s]


def build_program(T):
    NT = T // 128
    NG = NT // 4
    nc = bass.Bass("TRN2", target_bir_lowering=False)

    x_d = nc.dram_tensor("x", [T, D], F32, kind="ExternalInput").ap()
    wqkv_d = nc.dram_tensor("wqkv", [D, E3], F32, kind="ExternalInput").ap()
    bqkv_d = nc.dram_tensor("bqkv", [1, E3], F32, kind="ExternalInput").ap()
    wout_d = nc.dram_tensor("wout", [D, D], F32, kind="ExternalInput").ap()
    bout_d = nc.dram_tensor("bout", [1, D], F32, kind="ExternalInput").ap()
    w1_d = nc.dram_tensor("w1", [H, D], F32, kind="ExternalInput").ap()
    b1f_d = nc.dram_tensor("b1f", [1, H], F32, kind="ExternalInput").ap()
    g1_d = nc.dram_tensor("g1", [1, D], F32, kind="ExternalInput").ap()
    b1l_d = nc.dram_tensor("b1l", [1, D], F32, kind="ExternalInput").ap()
    w2_d = nc.dram_tensor("w2", [D, H], F32, kind="ExternalInput").ap()
    g2_d = nc.dram_tensor("g2", [1, H], F32, kind="ExternalInput").ap()
    b2l_d = nc.dram_tensor("b2l", [1, H], F32, kind="ExternalInput").ap()
    b2_d = nc.dram_tensor("b2", [1, D], F32, kind="ExternalInput").ap()
    out_d = nc.dram_tensor("out", [T, D], F32, kind="ExternalOutput").ap()

    h1_d = nc.dram_tensor("h1scr", [T, D], BF16, kind="Internal").ap()
    qh_d = nc.dram_tensor("qhscr", [T, D], BF16, kind="Internal").ap()
    wq1_d = nc.dram_tensor("wq1scr", [H, D], BF16, kind="Internal").ap()
    wq2_d = nc.dram_tensor("wq2scr", [D, H], BF16, kind="Internal").ap()

    onesb_d = nc.inline_tensor(np.ones((128, 1), np.float32).astype(ml_dtypes.bfloat16))
    onesf_d = nc.inline_tensor(np.ones((128, 1), np.float32))
    onesrow_d = nc.inline_tensor(np.ones((1, 128), np.float32))
    ident_d = nc.inline_tensor(np.eye(128, dtype=np.float32).astype(ml_dtypes.bfloat16))

    es = ExitStack()

    def sbR(name, shape, dtype):
        return es.enter_context(nc.sbuf_tensor(name, shape, dtype, side="right"))

    woutp = sbR("woutp", [128, 8 * D], BF16)
    boutb = sbR("boutb", [128, D], F32)
    beta2b = sbR("beta2b", [128, D], F32)
    cs2b = sbR("cs2b", [128, D], F32)
    beta1c = sbR("beta1c", [128, 32], F32)
    g1c = sbR("g1c", [128, 8], F32)
    g2c = sbR("g2c", [128, 32], F32)
    b1lc = sbR("b1lc", [128, 8], BF16)
    b2lc = sbR("b2lc", [128, 32], BF16)
    b1fc = sbR("b1fc", [128, 32], F32)
    g1m1c = sbR("g1m1c", [128, 8], F32)
    g2m2c = sbR("g2m2c", [128, 32], F32)
    onesb = sbR("onesb", [128, 1], BF16)
    onesf = sbR("onesf", [128, 1], F32)
    onesrow = sbR("onesrow", [1, 128], F32)
    identb = sbR("identb", [128, 128], BF16)
    mcols = sbR("mcols", [128, 6], F32)
    mrow = sbR("mrow", [1, 2], F32)
    wpart = sbR("wpart", [128, 64], F32)
    kvT = sbR("kvT", [128, 8], F32)

    allsems = []

    def S(name):
        s = Sem(nc, es, name)
        allsems.append(s)
        return s

    ax = DSem(nc, es, "ax"); asq = S("asq"); avar = S("avar"); asd = S("asd")
    astat = S("astat"); ah1 = S("ah1")
    ah1w = DSem(nc, es, "ah1w"); ah1t = DSem(nc, es, "ah1t", nslots=4)
    aqkv = S("aqkv"); aevac = S("aevac")
    aqs = S("aqs"); aqr = S("aqr"); aqh = S("aqh"); aqw = DSem(nc, es, "aqw")
    aks = S("aks"); akr = S("akr"); akh = S("akh"); akv = S("akv")
    wqkvs = S("wqkvs"); wouts = S("wouts"); smld = S("smld")
    wld = DSem(nc, es, "wld", nslots=5)
    gcons = S("gcons"); dcons = S("dcons"); gmld = S("gmld")
    mpre = S("mpre"); mtot = S("mtot"); mval = S("mval"); mbc = S("mbc")
    m1r = S("m1r"); m2r = S("m2r")
    dvw = DSem(nc, es, "dvw", nslots=4); scq = S("scq"); bhlv = S("bhlv")
    abc = S("abc"); binit = S("binit")
    kvc = S("kvc"); kvw = S("kvw")
    bar = S("bar")
    w1t = S("w1t"); w2t = S("w2t")
    bb1 = S("bb1"); bb1ev = S("bb1ev"); w1s = S("w1s")
    bb2 = S("bb2"); bb2r = S("bb2r"); w2s = S("w2s")
    bcs = S("bcs"); bcsr = S("bcsr"); bpb = S("bpb"); bprep = S("bprep")
    bx = DSem(nc, es, "bx"); bqt = S("bqt")
    batt = S("batt"); bxp = S("bxp")
    bsqx = S("bsqx"); bvar = S("bvar"); bsd = S("bsd"); bst = S("bst")
    bz1 = S("bz1"); bzt = S("bzt"); bze = S("bze")
    by1 = S("by1"); bh4 = S("bh4"); bsq = S("bsq"); bsq4 = S("bsq4")
    bsumrd = S("bsumrd"); by2 = S("by2"); bsum = S("bsum")
    bvar2 = S("bvar2"); bsd2 = S("bsd2")
    bfin = S("bfin"); bow = S("bow")

    esA = ExitStack()

    def sbA(name, shape, dtype, **kw):
        return esA.enter_context(nc.sbuf_tensor(name, shape, dtype, **kw))

    wqkvp = sbA("wqkvp", [128, 8 * E3], BF16)
    xbufA = sbA("xbufA", [128, 2 * D], F32)
    h1buf = sbA("h1buf", [128, 2 * D], BF16)
    h1T = sbA("h1T", [128, 2 * 4096], BF16, is_transpose_dest=True)
    qkvb = sbA("qkvb", [128, E3], F32)
    qhat = sbA("qhat", [128, 2 * D], BF16)
    khat = sbA("khat", [128, D], F32)
    kvtmp = sbA("kvtmp", [128, D], F32)
    kv_acc = sbA("kv_acc", [128, D], F32)
    wstg = sbA("wstg", [128, 5 * 1024], F32)
    qod = sbA("qod", [128, 2 * 1024], BF16)
    qog = sbA("qog", [128, 2 * 1024], BF16)
    qtd = sbA("qtd", [128, 2048], BF16)
    qtg = sbA("qtg", [128, 2048], BF16)
    sqscrA = sbA("sqscrA", [128, D], F32)
    stA = sbA("stA", [128, 32], F32)
    rowsA = sbA("rowsA", [1, E3], F32)
    boutrow = sbA("boutrow", [1, D], F32)

    psA = ExitStack()
    pq = [psA.enter_context(nc.psum_tensor(f"pqkv{i}", [128, 512], F32))
          for i in range(4)]
    pone = psA.enter_context(nc.psum_tensor("pone", [1, 8], F32))
    pmb = psA.enter_context(nc.psum_tensor("pmb", [128, 2], F32))
    pkv = psA.enter_context(nc.psum_tensor("pkv", [128, 8], F32))

    esA.close()
    psA.close()

    esB = ExitStack()

    def sbB(name, shape, dtype, **kw):
        return esB.enter_context(nc.sbuf_tensor(name, shape, dtype, **kw))

    w1p = sbB("w1p", [128, 8 * H], BF16, is_transpose_dest=True)
    w2p = sbB("w2p", [128, 32 * D], BF16, is_transpose_dest=True)
    xbufB = sbB("xbufB", [128, 2 * D], F32)
    xring = sbB("xring", [128, 3 * D], F32)
    qT = sbB("qT", [128, 1024], BF16, is_transpose_dest=True)
    z1buf = sbB("z1buf", [128, D], BF16)
    z1T = sbB("z1T", [128, 2048], BF16, is_transpose_dest=True)
    h4buf = sbB("h4buf", [128, 2 * H], BF16)
    h4sq = sbB("h4sq", [128, 4 * 128], BF16)
    outbuf = sbB("outbuf", [128, D], F32)
    stB = sbB("stB", [128, 32], F32)
    b2rowB = outbuf[0:1, 0:D]

    psB = ExitStack()
    patt = psB.enter_context(nc.psum_tensor("patt", [128, 1024], F32))
    psP = ExitStack()
    pcol = [psP.enter_context(nc.psum_tensor(f"pcol{i}", [128, 1], F32))
            for i in range(2)]
    prow = psP.enter_context(nc.psum_tensor("prow", [1, D], F32))
    pbc = [psP.enter_context(nc.psum_tensor(f"pbc{i}", [128, 512], F32))
           for i in range(2)]
    psP.close()
    py1 = [psB.enter_context(nc.psum_tensor(f"py1_{i}", [128, 256], F32))
           for i in range(2)]
    psumSQ = psB.enter_context(nc.psum_tensor("psumSQ", [128, 64], F32))
    pzT = psB.enter_context(nc.psum_tensor("pzT", [128, 512], F32))
    py2 = psB.enter_context(nc.psum_tensor("py2", [128, 1024], F32))

    wplan = ([("w1m", c) for c in range(NW1)] + [("w2m", c) for c in range(NW2)]
             + [("w1q", c) for c in range(NW1)] + [("w2q", c) for c in range(NW2)])
    cons_engine = []
    for kind, ch in wplan:
        if kind in ("w1m", "w2m"):
            cons_engine.append("dv")
        else:
            cons_engine.append("dv")
    gp_ord = {}
    dv_ord = {}
    ng = nd = 0
    for i, e in enumerate(cons_engine):
        if e == "gp":
            ng += 1
            gp_ord[i] = ng
        else:
            nd += 1
            dv_ord[i] = nd
    NLOADS = len(wplan)
    dvq_loads = [i for i, e in enumerate(cons_engine) if e == "dv"]
    NDV = len(dvq_loads)
    NGP_Q = sum(1 for i, e in enumerate(cons_engine)
                if e == "gp" and wplan[i][0] == "w2q")

    def w_src(kind, ch):
        if kind in ("w1m", "w1q"):
            return w1_d[ch * 128:(ch + 1) * 128, :]
        row, col = ch // 4, ch % 4
        return w2_d[row * 128:(row + 1) * 128, col * 1024:(col + 1) * 1024]

    def wslot(i):
        s = i % 5
        return wstg[:, s * 1024:(s + 1) * 1024]

    dvq_of_load = {}
    for i in dvq_loads:
        kind, ch = wplan[i]
        if kind in ("w1m", "w2m"):
            continue
        dvq_of_load[i] = len(dvq_of_load)
    dv_cons_of_load = {i: k + 1 for k, i in enumerate(dvq_loads)}

    sc_emit_tile = []

    def qring_a(kq):
        base = qod if kq % 4 < 2 else qtd
        s = kq % 2
        return base[:, s * 1024:(s + 1) * 1024]

    def qring_b(kq):
        base = qog if kq % 4 < 2 else qtg
        s = kq % 2
        return base[:, s * 1024:(s + 1) * 1024]

    with es, esB, psB, nc.Block() as block:

        @block.sync
        def _(sy):
            def h1t_tile(jt):
                gt = jt // 4
                st = jt % 4
                sy.wait_ge(ah1w.h(jt % 2), 16 * (jt // 2 + 1))
                if jt >= 4:
                    sy.wait_ge(ah1t.h(jt % 4), 128 * (jt // 4))
                if gt >= 2 and st == 0:
                    sy.wait_ge(aqkv.h, 24 * (gt - 1))
                for c in range(8):
                    ah1t.inc(sy.dma_start(
                        out=h1T[:, (gt % 2) * 4096 + c * 512 + st * 128:
                                (gt % 2) * 4096 + c * 512 + (st + 1) * 128],
                        in_=h1_d[jt * 128:(jt + 1) * 128, c * 128:(c + 1) * 128],
                        transpose=True), jt)

            for g in range(NG):
                for s in range(4):
                    j = 4 * g + s
                    if j >= 2:
                        sy.wait_ge(ah1.h, j - 1)
                    ax.inc(sy.dma_start(
                        out=xbufA[:, (j % 2) * D:(j % 2 + 1) * D],
                        in_=x_d[j * 128:(j + 1) * 128, :]), j)
                    if j - 2 >= 0:
                        h1t_tile(j - 2)
            for jt in (4 * NG - 2, 4 * NG - 1):
                h1t_tile(jt)
            sy.wait_ge(ax.h(0), ax.totals()[0])
            sy.wait_ge(ax.h(1), ax.totals()[1])
            for sl in range(4):
                sy.wait_ge(ah1t.h(sl), ah1t.totals()[sl])
            bar.inc(sy.nop())
            sy.wait_ge(bar.h, 5)
            for sl in range(4):
                sy.wait_ge(dvw.h(sl), 16 * 8)
            for c in range(8):
                w1t.inc(sy.dma_start(out=w1p[:, c * H:(c + 1) * H],
                                     in_=wq1_d[:, c * 128:(c + 1) * 128],
                                     transpose=True), 16)
            for sl in range(4):
                sy.wait_ge(dvw.h(sl), 16 * 16)
            for r in range(32):
                w2t.inc(sy.dma_start(out=w2p[:, r * D:(r + 1) * D],
                                     in_=wq2_d[:, r * 128:(r + 1) * 128],
                                     transpose=True), 16)
            for j in range(NT):
                if j >= 2:
                    sy.wait_ge(bsqx.h, j - 1)
                bx.inc(sy.dma_start(out=xbufB[:, (j % 2) * D:(j % 2 + 1) * D],
                                    in_=x_d[j * 128:(j + 1) * 128, :]), j)
                if j >= 1:
                    sy.wait_ge(batt.h, 2 * j)
                for c in range(8):
                    bqt.inc(sy.dma_start(
                        out=qT[:, c * 128:(c + 1) * 128],
                        in_=qh_d[j * 128:(j + 1) * 128, c * 128:(c + 1) * 128],
                        transpose=True), 16)
            sy.wait_ge(bow.h, 16 * NT)

        @block.scalar
        def _(sc):
            smld.inc(sc.dma_start(out=rowsA[:, :], in_=bqkv_d), 16)
            smld.inc(sc.dma_start(out=boutrow[:, :], in_=bout_d), 16)
            smld.inc(sc.dma_start(out=onesb[:, :], in_=onesb_d.ap()), 16)
            smld.inc(sc.dma_start(out=onesf[:, :], in_=onesf_d.ap()), 16)
            smld.inc(sc.dma_start(out=onesrow[:, :], in_=onesrow_d.ap()), 16)
            smld.inc(sc.dma_start(out=identb[:, :], in_=ident_d.ap()), 16)


            def scA_front(j):
                P = (j % 2) * 16
                xs = xbufA[:, (j % 2) * D:(j % 2 + 1) * D]
                sc.wait_ge(ax.h(j % 2), 16 * (j // 2 + 1))
                asq.inc(sc.activation(out=sqscrA[:, :], in_=xs, func=AF.Square,
                                      accum_out=stA[:, P + 1:P + 2]))
                sc.wait_ge(avar.h, j + 1)
                asd.inc(sc.activation(out=stA[:, P + 5:P + 6],
                                      in_=stA[:, P + 4:P + 5], func=AF.Sqrt))
                sc.wait_ge(astat.h, j + 1)
                if j >= 2:
                    sc.wait_ge(ah1w.h(j % 2), 16 * (j // 2))
                ah1.inc(sc.activation(out=h1buf[:, (j % 2) * D:(j % 2 + 1) * D],
                                      in_=xs, func=AF.Identity,
                                      scale=stA[:, P + 6:P + 7],
                                      bias=stA[:, P + 7:P + 8]))
                sc.wait_ge(ah1.h, j + 1)
                ah1w.inc(sc.dma_start(out=h1_d[j * 128:(j + 1) * 128, :],
                                      in_=h1buf[:, (j % 2) * D:(j % 2 + 1) * D]), j)

            def scA_back(u):
                P = (u % 2) * 16
                if u >= 1:
                    sc.wait_ge(akv.h, u)
                for e in range(6):
                    i = u * 6 + e
                    sc.wait_ge(aqkv.h, i + 1)
                    if e == 0 and u >= 1:
                        sc.wait_ge(aqh.h, u)
                    aevac.inc(sc.activation(out=qkvb[:, e * 512:(e + 1) * 512],
                                            in_=pq[i % 4][:, :], func=AF.Copy))
                sc.drain()
                sc.activation(out=sqscrA[:, :], in_=qkvb[:, 0:D], func=AF.Square,
                              accum_out=stA[:, P + 8:P + 9])
                sc.drain()
                aqs.inc(sc.activation(out=stA[:, P + 9:P + 10],
                                      in_=stA[:, P + 8:P + 9], func=AF.Sqrt))
                sc.activation(out=sqscrA[:, :], in_=qkvb[:, D:2 * D],
                              func=AF.Square, accum_out=stA[:, P + 10:P + 11])
                sc.drain()
                aks.inc(sc.activation(out=stA[:, P + 11:P + 12],
                                      in_=stA[:, P + 10:P + 11], func=AF.Sqrt))
                sc.wait_ge(aqr.h, u + 1)
                if u >= 2:
                    sc.wait_ge(aqw.h(u % 2), 16 * (u // 2))
                aqh.inc(sc.activation(out=qhat[:, (u % 2) * D:(u % 2 + 1) * D],
                                      in_=qkvb[:, 0:D], func=AF.Identity,
                                      scale=stA[:, P + 12:P + 13]))
                sc.wait_ge(aqh.h, u + 1)
                aqw.inc(sc.dma_start(out=qh_d[u * 128:(u + 1) * 128, :],
                                     in_=qhat[:, (u % 2) * D:(u % 2 + 1) * D]), u)
                sc.wait_ge(akr.h, u + 1)
                if u >= 1:
                    sc.wait_ge(akv.h, u)
                akh.inc(sc.activation(out=khat[:, :], in_=qkvb[:, D:2 * D],
                                      func=AF.Identity, scale=stA[:, P + 13:P + 14]))

            scq_pos = [0]
            quant_loads = sorted(dvq_of_load)
            first_w2q = [True]

            def sc_consume(n, u):
                for _ in range(n):
                    k0 = scq_pos[0]
                    if k0 >= len(quant_loads):
                        return
                    i = quant_loads[k0]
                    kind, ch = wplan[i]
                    if kind == "w1q" and u < min(10, NT - 1):
                        return
                    if kind == "w2q" and u < min(18, NT - 1):
                        return
                    scq_pos[0] = k0 + 1
                    kq = k0
                    sc_emit_tile.append(u)
                    if kq == 0:
                        sc.wait_ge(m1r.h, 1)
                    if kind == "w2q" and first_w2q[0]:
                        sc.wait_ge(m2r.h, 1)
                        first_w2q[0] = False
                    sc.wait_ge(wld.h(i % 5), 16 * (i // 5 + 1))
                    if kq >= 4:
                        sc.wait_ge(dvw.h(kq - 4), 16 * ((kq - 4) // 4 + 1))
                    scol = mcols[:, 1:2] if kind == "w1q" else mcols[:, 3:4]
                    abuf = qring_a(kq)
                    bbuf = qring_b(kq)
                    sc.activation(out=abuf, in_=wslot(i), func=AF.Sign,
                                  scale=scol, bias=mcols[:, 5:6])
                    scq.inc(sc.activation(out=bbuf, in_=wslot(i), func=AF.Sign,
                                          scale=scol, bias=mcols[:, 4:5]))

            for j in range(NT):
                scA_front(j)
                if j >= 3:
                    scA_back(j - 3)
                    sc_consume(4, j - 3)
            for u in range(max(0, NT - 3), NT):
                scA_back(u)
                sc_consume(4, u)
            sc_consume(10 ** 9, NT + 99)
            sc.wait_ge(ah1w.h(0), ah1w.totals()[0])
            sc.wait_ge(ah1w.h(1), ah1w.totals()[1])
            sc.wait_ge(aqw.h(0), aqw.totals()[0])
            sc.wait_ge(aqw.h(1), aqw.totals()[1])
            sc.wait_ge(smld.h, 16 * 6)
            bar.inc(sc.drain())
            sc.wait_ge(bar.h, 5)
            smld.inc(sc.dma_start(out=b2rowB, in_=b2_d), 16)

            def sc_front(t):
                P = (t % 2) * 16
                xr = xring[:, (t % 3) * D:(t % 3 + 1) * D]
                xs = xbufB[:, (t % 2) * D:(t % 2 + 1) * D]
                sc.wait_ge(bxp.h, t + 1)
                sc.activation(out=xs[:, 0:512], in_=xr[:, 0:512], func=AF.Square,
                              accum_out=stB[:, P + 1:P + 2])
                sc.activation(out=xs[:, 512:1024], in_=xr[:, 512:1024],
                              func=AF.Square, accum_out=stB[:, P + 14:P + 15])
                sc.drain()
                bsqx.inc(sc.activation(out=stB[:, P + 1:P + 2],
                                       in_=stB[:, P + 14:P + 15],
                                       func=AF.Identity,
                                       bias=stB[:, P + 1:P + 2]))
                sc.wait_ge(bvar.h, t + 1)
                bsd.inc(sc.activation(out=stB[:, P + 5:P + 6],
                                      in_=stB[:, P + 4:P + 5], func=AF.Sqrt))
                sc.wait_ge(bst.h, t + 1)
                if t >= 1:
                    sc.wait_ge(bzt.h, 2 * t)
                bz1.inc(sc.activation(out=z1buf[:, :], in_=xr, func=AF.Identity,
                                      scale=stB[:, P + 6:P + 7],
                                      bias=stB[:, P + 7:P + 8]))

            def sc_tail(j):
                P = (j % 2) * 16
                sc.wait_ge(bvar2.h, j + 1)
                bsd2.inc(sc.activation(out=stB[:, P + 10:P + 11],
                                       in_=stB[:, P + 9:P + 10], func=AF.Sqrt))
                sc.wait_ge(bfin.h, j + 1)
                bow.inc(sc.dma_start(out=out_d[j * 128:(j + 1) * 128, :],
                                     in_=outbuf[:, :]), 16)

            sc_front(0)
            sc_front(1)
            for p in range(NT // 2):
                j0, j1 = 2 * p, 2 * p + 1
                if j0 + 2 < NT:
                    sc_front(j0 + 2)
                if p >= 1:
                    sc.wait_ge(bsq4.h, 32 * j0)
                for m in range(32):
                    sc.wait_ge(by1.h, 32 * p + m + 1)
                    bh4.inc(sc.activation(
                        out=h4buf[:, m * 256:(m + 1) * 256],
                        in_=py1[m % 2][:, :], func=AF.Gelu,
                        bias=beta1c[:, m:m + 1]))
                sc_tail(j0)
                if j1 + 2 < NT:
                    sc_front(j1 + 2)
                sc_tail(j1)
            sc.wait_ge(bow.h, 16 * NT)

        @block.gpsimd
        def _(gp):
            for c in range(8):
                wqkvs.inc(gp.dma_start(out=wqkvp[:, c * E3:(c + 1) * E3],
                                       in_=wqkv_d[c * 128:(c + 1) * 128, :]), 16)
            for c in range(8):
                wouts.inc(gp.dma_start(out=woutp[:, c * D:(c + 1) * D],
                                       in_=wout_d[c * 128:(c + 1) * 128, :]), 16)
            with nc.allow_non_contiguous_dma("tiny columnize loads"):
                gmld.inc(gp.dma_start(
                    out=g1c[:, :], in_=g1_d.rearrange("o (c p) -> (o p) c", p=128)), 16)
                gmld.inc(gp.dma_start(
                    out=b1lc[:, :], in_=b1l_d.rearrange("o (c p) -> (o p) c", p=128)), 16)
                gmld.inc(gp.dma_start(
                    out=g2c[:, :], in_=g2_d.rearrange("o (c p) -> (o p) c", p=128)), 16)
                gmld.inc(gp.dma_start(
                    out=b2lc[:, :], in_=b2l_d.rearrange("o (c p) -> (o p) c", p=128)), 16)
                gmld.inc(gp.dma_start(
                    out=b1fc[:, :], in_=b1f_d.rearrange("o (c p) -> (o p) c", p=128)), 16)
            def dv_write(ii):
                kq = dvq_of_load[ii]
                gp.wait_ge(dcons.h, dv_cons_of_load[ii])
                kk, cc = wplan[ii]
                srcb = qring_a(kq)
                if kk == "w1q":
                    dvw.inc(gp.dma_start(
                        out=wq1_d[cc * 128:(cc + 1) * 128, :], in_=srcb), kq)
                else:
                    row, col = cc // 4, cc % 4
                    dvw.inc(gp.dma_start(
                        out=wq2_d[row * 128:(row + 1) * 128,
                                  col * 1024:(col + 1) * 1024], in_=srcb), kq)

            for i, (kind, ch) in enumerate(wplan):
                if i >= 5 and cons_engine[i - 5] == "dv":
                    gp.wait_ge(dcons.h, dv_cons_of_load[i - 5])
                wld.inc(gp.dma_start(out=wslot(i), in_=w_src(kind, ch)), i)
                if i >= 2:
                    if (i - 2) in dvq_of_load:
                        dv_write(i - 2)
            for ii in (NLOADS - 2, NLOADS - 1):
                if ii in dvq_of_load:
                    dv_write(ii)
            gp.wait_ge(wqkvs.h, 128)
            gp.wait_ge(wouts.h, 128)
            gp.wait_ge(gmld.h, 16 * 5)
            for sl in range(5):
                gp.wait_ge(wld.h(sl), wld.totals()[sl])
            for sl in range(4):
                gp.wait_ge(dvw.h(sl), dvw.totals()[sl])
            bar.inc(gp.drain())
            gp.wait_ge(bar.h, 5)

        p_mtot1 = (min(1, NG - 1), 3)
        p_mbc1 = (min(2, NG - 1), 1) if NG >= 3 else p_mtot1
        p_mtot2 = (min(3, NG - 1), 3) if NG >= 4 else p_mbc1
        p_mbc2 = (min(4, NG - 1), 1) if NG >= 5 else p_mtot2

        @block.tensor
        def _(te):
            te.wait_ge(smld.h, 16 * 6)
            for hh in range(2):
                abc.inc(te.matmul(out=pq[hh][:, :], lhsT=onesrow[:, :],
                                  rhs=boutrow[0:1, hh * 512:(hh + 1) * 512],
                                  start=True, stop=True))
            te.wait_ge(binit.h, 2)
            te.wait_ge(wqkvs.h, 128)
            for g in range(NG):
                for s in range(4):
                    te.wait_ge(ah1t.h(s), 128 * (g + 1))
                    for e in range(6):
                        i = (4 * g + s) * 6 + e
                        if i >= 4:
                            te.wait_ge(aevac.h, i - 3)
                        te.matmul(out=pq[i % 4][:, :], lhsT=onesrow[:, :],
                                  rhs=rowsA[0:1, e * 512:(e + 1) * 512],
                                  start=True, stop=False)
                        for c in range(8):
                            mm = te.matmul(
                                out=pq[i % 4][:, :],
                                lhsT=h1T[:, (g % 2) * 4096 + c * 512 + s * 128:
                                         (g % 2) * 4096 + c * 512 + (s + 1) * 128],
                                rhs=wqkvp[:, c * E3 + e * 512:c * E3 + (e + 1) * 512],
                                start=False, stop=(c == 7))
                        aqkv.inc(mm)
                    if (g, s) == p_mtot1:
                        te.wait_ge(mpre.h, 1)
                        mtot.inc(te.matmul(out=pone[:, 0:1],
                                           lhsT=stA[:, 30:31], rhs=onesf[:, :],
                                           start=True, stop=True))
                    if (g, s) == p_mbc1:
                        te.wait_ge(mval.h, 1)
                        mbc.inc(te.matmul(out=pmb[:, 0:1], lhsT=onesrow[:, :],
                                          rhs=mrow[0:1, 0:1],
                                          start=True, stop=True))
                    if (g, s) == p_mtot2:
                        te.wait_ge(mpre.h, 2)
                        mtot.inc(te.matmul(out=pone[:, 1:2],
                                           lhsT=stA[:, 31:32], rhs=onesf[:, :],
                                           start=True, stop=True))
                    if (g, s) == p_mbc2:
                        te.wait_ge(mval.h, 2)
                        mbc.inc(te.matmul(out=pmb[:, 1:2], lhsT=onesrow[:, :],
                                          rhs=mrow[0:1, 1:2],
                                          start=True, stop=True))
            te.wait_ge(akv.h, NT)
            te.wait_ge(aevac.h, 6 * NT)
            for c in range(8):
                kvc.inc(te.matmul(out=pkv[:, c:c + 1],
                                  lhsT=kv_acc[:, c * 128:(c + 1) * 128],
                                  rhs=onesf[:, :], start=True, stop=True))
            bar.inc(te.drain())
            te.wait_ge(bar.h, 5)
            te.wait_ge(bhlv.h, 1)
            te.wait_ge(w1t.h, 128)
            for m in range(32):
                if m >= 2:
                    te.wait_ge(bb1ev.h, m - 1)
                for c in range(8):
                    mm = te.matmul(out=pcol[m % 2][:, :],
                                   lhsT=w1p[:, c * H + m * 128:c * H + (m + 1) * 128],
                                   rhs=b1lc[:, c:c + 1],
                                   start=(c == 0), stop=(c == 7))
                bb1.inc(mm)
            te.wait_ge(w2t.h, 16 * 32)
            for r in range(32):
                for hh in range(2):
                    mm = te.matmul(out=prow[0:1, hh * 512:(hh + 1) * 512],
                                   lhsT=b2lc[:, r:r + 1],
                                   rhs=w2p[:, r * D + hh * 512:r * D + (hh + 1) * 512],
                                   start=(r == 0), stop=(r == 31))
            bb2.inc(mm)
            te.wait_ge(bb2r.h, 1)
            for hh in range(2):
                bpb.inc(te.matmul(out=pbc[hh][:, :], lhsT=onesrow[:, :],
                                  rhs=outbuf[0:1, hh * 512:(hh + 1) * 512],
                                  start=True, stop=True))
            te.wait_ge(w2s.h, 1)
            for r in range(32):
                for hh in range(2):
                    mm = te.matmul(out=prow[0:1, hh * 512:(hh + 1) * 512],
                                   lhsT=onesb[:, :],
                                   rhs=w2p[:, r * D + hh * 512:r * D + (hh + 1) * 512],
                                   start=(r == 0), stop=(r == 31))
            bcs.inc(mm)
            te.wait_ge(bcsr.h, 1)
            for hh in range(2):
                bpb.inc(te.matmul(out=pbc[hh][:, :], lhsT=onesrow[:, :],
                                  rhs=outbuf[0:1, hh * 512:(hh + 1) * 512],
                                  start=True, stop=True))
            def attn_mm(t):
                te.wait_ge(bqt.h, 128 * (t + 1))
                te.wait_ge(kvw.h, 1)
                if t >= 1:
                    te.wait_ge(bxp.h, t)
                for hh in range(2):
                    for c in range(8):
                        mm = te.matmul(
                            out=patt[:, hh * 512:(hh + 1) * 512],
                            lhsT=qT[:, c * 128:(c + 1) * 128],
                            rhs=woutp[:, c * D + hh * 512:c * D + (hh + 1) * 512],
                            start=(c == 0), stop=(c == 7))
                    batt.inc(mm)

            def y2_tile(j):
                p = j // 2
                te.wait_ge(bprep.h, 1)
                if j >= 1:
                    te.wait_ge(bfin.h, j)
                    te.wait_ge(bsumrd.h, j)
                for m in range(32):
                    q = 32 * j + m
                    te.wait_ge(bh4.h, 32 * p + m + 1)
                    h4m = h4buf[:, m * 256 + (j % 2) * 128:m * 256 + (j % 2) * 128 + 128]
                    for hh in range(2):
                        mm = te.matmul(out=py2[:, hh * 512:(hh + 1) * 512],
                                       lhsT=h4m,
                                       rhs=w2p[:, m * D + hh * 512:m * D + (hh + 1) * 512],
                                       start=(m == 0), stop=(m == 31))
                        if hh == 1 and m == 31:
                            by2.inc(mm)
                    te.matmul(out=psumSQ[:, m:m + 1], lhsT=h4m, rhs=onesb[:, :],
                              start=True, stop=True)
                    te.wait_ge(bsq.h, q + 1)
                    bsq4.inc(te.matmul(out=psumSQ[:, 32 + m:33 + m],
                                       lhsT=h4sq[:, (m % 4) * 128:(m % 4 + 1) * 128],
                                       rhs=onesb[:, :], start=True, stop=True))

            def zT_round(t, r):
                g = 2 * t + r
                if r == 0:
                    te.wait_ge(bz1.h, t + 1)
                if g >= 1:
                    te.wait_ge(bze.h, g)
                for cc in range(4):
                    c = 4 * r + cc
                    mm = te.matmul(out=pzT[:, cc * 128:(cc + 1) * 128],
                                   lhsT=z1buf[:, c * 128:(c + 1) * 128],
                                   rhs=identb[:, :], start=True, stop=True)
                bzt.inc(mm)

            attn_mm(0)
            attn_mm(1)
            for t in (0, 1):
                zT_round(t, 0)
                zT_round(t, 1)
            for p in range(NT // 2):
                j0, j1 = 2 * p, 2 * p + 1
                if j0 + 2 < NT:
                    attn_mm(j0 + 2)
                te.wait_ge(bze.h, 4 * p + 4)
                te.wait_ge(w1s.h, 1)
                for m in range(32):
                    if 32 * p + m >= 2:
                        te.wait_ge(bh4.h, 32 * p + m - 1)
                    for c in range(8):
                        mm = te.matmul(
                            out=py1[m % 2][:, :],
                            lhsT=w1p[:, c * H + m * 128:c * H + (m + 1) * 128],
                            rhs=z1T[:, c * 256:(c + 1) * 256],
                            start=(c == 0), stop=(c == 7))
                    by1.inc(mm)
                if j1 + 2 < NT:
                    attn_mm(j1 + 2)
                if j0 + 2 < NT:
                    zT_round(j0 + 2, 0)
                    zT_round(j0 + 2, 1)
                y2_tile(j0)
                y2_tile(j1)
                if j1 + 2 < NT:
                    zT_round(j1 + 2, 0)
                    zT_round(j1 + 2, 1)

        @block.vector
        def _(dv):
            dv.memset(kv_acc[:, :], 0.0)
            dv.memset(mcols[:, 4:5], 0.5)
            dv.memset(mcols[:, 5:6], -0.5)
            dv.wait_ge(gmld.h, 16 * 5)
            dv.tensor_scalar(out=b1lc[:, :], in0=b1lc[:, :], scalar1=0.5,
                             scalar2=None, op0=ALU.mult)
            bhlv.inc(dv.tensor_scalar(out=b2lc[:, :], in0=b2lc[:, :],
                                      scalar1=0.5, scalar2=None, op0=ALU.mult))
            for hh in range(2):
                dv.wait_ge(abc.h, hh + 1)
                binit.inc(dv.tensor_copy(out=boutb[:, hh * 512:(hh + 1) * 512],
                                         in_=pq[hh][:, :]))
            dvq_pos = [0]

            def dv_consume(n, jcur, limit=None, max_i=None):
                for _ in range(n):
                    k0 = dvq_pos[0]
                    if k0 >= NDV or (limit is not None and k0 >= limit):
                        return
                    if max_i is not None and dvq_loads[k0] >= max_i:
                        return
                    i0 = dvq_loads[k0]
                    kq0 = dvq_of_load.get(i0)
                    if kq0 is not None and (kq0 >= len(sc_emit_tile)
                                            or sc_emit_tile[kq0] > jcur - 3):
                        return
                    dvq_pos[0] = k0 + 1
                    i = dvq_loads[k0]
                    kind, ch = wplan[i]
                    if kind in ("w1m", "w2m"):
                        dv.wait_ge(wld.h(i % 5), 16 * (i // 5 + 1))
                        col = ch if kind == "w1m" else NW1 + ch
                        dcons.inc(dv.tensor_reduce(
                            out=wpart[:, col:col + 1], in_=wslot(i),
                            axis=AX.X, op=ALU.add, apply_absolute_value=True))
                        continue
                    kq = dvq_of_load[i]
                    dst = qring_a(kq)
                    srcb = qring_b(kq)
                    dv.wait_ge(scq.h, kq + 1)
                    dcons.inc(dv.tensor_tensor(out=dst, in0=dst, in1=srcb,
                                               op=ALU.add))

            def dvA_front(j):
                P = (j % 2) * 16
                xs = xbufA[:, (j % 2) * D:(j % 2 + 1) * D]
                dv.wait_ge(ax.h(j % 2), 16 * (j // 2 + 1))
                if j >= 2:
                    dv.wait_ge(ah1.h, j - 1)
                    dv.wait_ge(asd.h, j - 1)
                dv.reduce_sum(out=stA[:, P + 0:P + 1], in_=xs, axis=AX.X,
                              negate=True)
                dv.drain()
                dv.tensor_scalar(out=stA[:, P + 2:P + 3], in0=stA[:, P + 0:P + 1],
                                 scalar1=1.0 / D, scalar2=None, op0=ALU.mult)
                dv.wait_ge(asq.h, j + 1)
                dv.tensor_scalar(out=stA[:, P + 3:P + 4], in0=stA[:, P + 1:P + 2],
                                 scalar1=1.0 / D, scalar2=EPS,
                                 op0=ALU.mult, op1=ALU.add)
                dv.drain()
                dv.tensor_tensor(out=stA[:, P + 4:P + 5], in0=stA[:, P + 2:P + 3],
                                 in1=stA[:, P + 2:P + 3], op=ALU.mult)
                dv.drain()
                avar.inc(dv.tensor_tensor(out=stA[:, P + 4:P + 5],
                                          in0=stA[:, P + 3:P + 4],
                                          in1=stA[:, P + 4:P + 5],
                                          op=ALU.subtract))
                dv.wait_ge(asd.h, j + 1)
                dv.reciprocal(out=stA[:, P + 6:P + 7], in_=stA[:, P + 5:P + 6])
                dv.drain()
                astat.inc(dv.tensor_tensor(out=stA[:, P + 7:P + 8],
                                           in0=stA[:, P + 2:P + 3],
                                           in1=stA[:, P + 6:P + 7], op=ALU.mult))
            def dvA_back(u):
                P = (u % 2) * 16
                dv.wait_ge(aqs.h, u + 1)
                aqr.inc(dv.reciprocal(out=stA[:, P + 12:P + 13],
                                      in_=stA[:, P + 9:P + 10]))
                dv.wait_ge(aks.h, u + 1)
                akr.inc(dv.reciprocal(out=stA[:, P + 13:P + 14],
                                      in_=stA[:, P + 11:P + 12]))
                dv.wait_ge(akh.h, u + 1)
                dv.tensor_tensor(out=kvtmp[:, :], in0=khat[:, :],
                                 in1=qkvb[:, 2 * D:3 * D], op=ALU.mult)
                dv.drain()
                akv.inc(dv.tensor_tensor(out=kv_acc[:, :], in0=kv_acc[:, :],
                                         in1=kvtmp[:, :], op=ALU.add))
            def dvA_extra(u):
                if u == min(5, NT - 1):
                    assert dvq_pos[0] >= NW1 or NT < 32, dvq_pos[0]
                    dv.drain()
                    mpre.inc(dv.reduce_sum(out=stA[:, 30:31],
                                           in_=wpart[:, 0:NW1], axis=AX.X))
                if u == min(8, NT - 1):
                    dv.wait_ge(mtot.h, 1)
                    mval.inc(dv.tensor_scalar(out=mrow[0:1, 0:1],
                                              in0=pone[0:1, 0:1],
                                              scalar1=1.0 / (H * D),
                                              scalar2=EPS,
                                              op0=ALU.mult, op1=ALU.max))
                if u == min(10, NT - 1):
                    dv.wait_ge(mbc.h, 1)
                    dv.tensor_copy(out=mcols[:, 0:1], in_=pmb[:, 0:1])
                    dv.drain()
                    m1r.inc(dv.reciprocal(out=mcols[:, 1:2],
                                          in_=mcols[:, 0:1]))
                    dv.drain()
                if u == min(12, NT - 1):
                    assert dvq_pos[0] >= NW1 + NW2 or NT < 32, dvq_pos[0]
                    dv.drain()
                    mpre.inc(dv.reduce_sum(out=stA[:, 31:32],
                                           in_=wpart[:, NW1:NW1 + NW2],
                                           axis=AX.X))
                if u == min(16, NT - 1):
                    dv.wait_ge(mtot.h, 2)
                    mval.inc(dv.tensor_scalar(out=mrow[0:1, 1:2],
                                              in0=pone[0:1, 1:2],
                                              scalar1=1.0 / (H * D),
                                              scalar2=EPS,
                                              op0=ALU.mult, op1=ALU.max))
                if u == min(18, NT - 1):
                    dv.wait_ge(mbc.h, 2)
                    dv.tensor_copy(out=mcols[:, 2:3], in_=pmb[:, 1:2])
                    dv.drain()
                    m2r.inc(dv.reciprocal(out=mcols[:, 3:4],
                                          in_=mcols[:, 2:3]))
                    dv.drain()

            for j in range(NT):
                dvA_front(j)
                if j >= 3:
                    dvA_back(j - 3)
                    dvA_extra(j - 3)
                if NT >= 32:
                    dv_consume(6, j)
                else:
                    dv_consume(64 // NT + 1, j, limit=NW1 + NW2)
            for u in range(max(0, NT - 3), NT):
                dvA_back(u)
                dvA_extra(u)
            dv_consume(10 ** 9, NT + 999)
            dv.wait_ge(kvc.h, 8)
            dv.tensor_copy(out=kvT[:, :], in_=pkv[:, :])
            dv.drain()
            dv.wait_ge(wouts.h, 128)
            for c in range(8):
                dv.tensor_scalar(out=woutp[:, c * D:(c + 1) * D],
                                 in0=woutp[:, c * D:(c + 1) * D],
                                 scalar1=kvT[:, c:c + 1], scalar2=None,
                                 op0=ALU.mult)
            kvw.inc(dv.drain())
            bar.inc(dv.drain())
            dv.wait_ge(bar.h, 5)
            dv.tensor_scalar(out=g1m1c[:, :], in0=g1c[:, :],
                             scalar1=mcols[:, 0:1], scalar2=0.5, op0=ALU.mult,
                             op1=ALU.mult)
            dv.tensor_scalar(out=g2m2c[:, :], in0=g2c[:, :],
                             scalar1=mcols[:, 2:3], scalar2=0.5, op0=ALU.mult,
                             op1=ALU.mult)
            for m in range(32):
                dv.wait_ge(bb1.h, m + 1)
                bb1ev.inc(dv.scalar_tensor_tensor(
                    out=beta1c[:, m:m + 1], in0=pcol[m % 2][:, :],
                    scalar=mcols[:, 0:1], in1=b1fc[:, m:m + 1],
                    op0=ALU.mult, op1=ALU.add))
            dv.drain()
            for c in range(8):
                dv.tensor_scalar(out=w1p[:, c * H:(c + 1) * H],
                                 in0=w1p[:, c * H:(c + 1) * H],
                                 scalar1=g1m1c[:, c:c + 1], scalar2=None,
                                 op0=ALU.mult)
            w1s.inc(dv.drain())
            dv.wait_ge(bb2.h, 1)
            dv.wait_ge(smld.h, 16 * 7)
            bb2r.inc(dv.scalar_tensor_tensor(
                out=b2rowB, in0=prow[0:1, :], scalar=mcols[0:1, 2:3],
                in1=b2rowB, op0=ALU.mult, op1=ALU.add))
            dv.drain()
            for r in range(32):
                dv.tensor_scalar(out=w2p[:, r * D:(r + 1) * D],
                                 in0=w2p[:, r * D:(r + 1) * D],
                                 scalar1=g2m2c[:, r:r + 1], scalar2=None,
                                 op0=ALU.mult)
            w2s.inc(dv.drain())
            dv.wait_ge(bpb.h, 2)
            dv.tensor_copy(out=beta2b[:, 0:512], in_=pbc[0][:, :])
            dv.tensor_copy(out=beta2b[:, 512:1024], in_=pbc[1][:, :])
            dv.wait_ge(bcs.h, 1)
            bcsr.inc(dv.tensor_copy(out=b2rowB, in_=prow[0:1, :]))
            dv.wait_ge(bpb.h, 4)
            dv.tensor_copy(out=cs2b[:, 0:512], in_=pbc[0][:, :])
            bprep.inc(dv.tensor_copy(out=cs2b[:, 512:1024], in_=pbc[1][:, :]))
            def dv_front(t):
                P = (t % 2) * 16
                xs = xbufB[:, (t % 2) * D:(t % 2 + 1) * D]
                xr = xring[:, (t % 3) * D:(t % 3 + 1) * D]
                dv.wait_ge(bx.h(t % 2), 16 * (t // 2 + 1))
                dv.tensor_tensor(out=xs, in0=xs, in1=boutb[:, :], op=ALU.add)
                dv.drain()
                dv.wait_ge(batt.h, 2 * t + 2)
                if t >= 2:
                    dv.wait_ge(bz1.h, t - 1)
                    dv.wait_ge(bsd.h, t - 1)
                bxp.inc(dv.tensor_tensor(out=xr, in0=patt[:, :], in1=xs,
                                         op=ALU.add))
                dv.drain()
                dv.reduce_sum(out=stB[:, P + 0:P + 1], in_=xr, axis=AX.X,
                              negate=True)
                dv.drain()
                dv.tensor_scalar(out=stB[:, P + 2:P + 3], in0=stB[:, P + 0:P + 1],
                                 scalar1=1.0 / D, scalar2=None, op0=ALU.mult)
                dv.wait_ge(bsqx.h, t + 1)
                dv.tensor_scalar(out=stB[:, P + 3:P + 4], in0=stB[:, P + 1:P + 2],
                                 scalar1=(1.0 + EPS) / D, scalar2=EPS * EPS,
                                 op0=ALU.mult, op1=ALU.add)
                dv.drain()
                dv.tensor_tensor(out=stB[:, P + 4:P + 5], in0=stB[:, P + 2:P + 3],
                                 in1=stB[:, P + 2:P + 3], op=ALU.mult)
                dv.drain()
                bvar.inc(dv.scalar_tensor_tensor(
                    out=stB[:, P + 4:P + 5], in0=stB[:, P + 4:P + 5],
                    scalar=-(1.0 + EPS), in1=stB[:, P + 3:P + 4],
                    op0=ALU.mult, op1=ALU.add))
                dv.wait_ge(bsd.h, t + 1)
                dv.reciprocal(out=stB[:, P + 6:P + 7], in_=stB[:, P + 5:P + 6])
                dv.drain()
                bst.inc(dv.tensor_tensor(out=stB[:, P + 7:P + 8],
                                         in0=stB[:, P + 2:P + 3],
                                         in1=stB[:, P + 6:P + 7], op=ALU.mult))

            def dv_h4sq(j):
                p = j // 2
                if j >= 2:
                    dv.wait_ge(bsq4.h, 32 * (j - 1))
                for m in range(32):
                    q = 32 * j + m
                    dv.wait_ge(bh4.h, 32 * p + m + 1)
                    if q >= 4:
                        dv.wait_ge(bsq4.h, q - 3)
                    h4m = h4buf[:, m * 256 + (j % 2) * 128:m * 256 + (j % 2) * 128 + 128]
                    bsq.inc(dv.tensor_tensor(
                        out=h4sq[:, (m % 4) * 128:(m % 4 + 1) * 128],
                        in0=h4m, in1=h4m, op=ALU.mult))

            def dv_tail(j):
                P = (j % 2) * 16
                xr = xring[:, (j % 3) * D:(j % 3 + 1) * D]
                dv.wait_ge(bsq4.h, 32 * (j + 1))
                if j >= 2:
                    dv.wait_ge(bsd2.h, j - 1)
                dv.reduce_sum(out=stB[:, P + 8:P + 9], in_=psumSQ[:, 0:32],
                              axis=AX.X, negate=True)
                bsumrd.inc(dv.reduce_sum(out=stB[:, P + 9:P + 10],
                                         in_=psumSQ[:, 32:64], axis=AX.X))
                dv.drain()
                dv.tensor_scalar(out=stB[:, P + 8:P + 9],
                                 in0=stB[:, P + 8:P + 9],
                                 scalar1=1.0 / H, scalar2=None, op0=ALU.mult)
                dv.tensor_scalar(out=stB[:, P + 9:P + 10],
                                 in0=stB[:, P + 9:P + 10],
                                 scalar1=1.0 / H, scalar2=EPS,
                                 op0=ALU.mult, op1=ALU.add)
                dv.drain()
                dv.tensor_tensor(out=stB[:, P + 11:P + 12],
                                 in0=stB[:, P + 8:P + 9],
                                 in1=stB[:, P + 8:P + 9], op=ALU.mult)
                dv.drain()
                bvar2.inc(dv.tensor_tensor(out=stB[:, P + 9:P + 10],
                                           in0=stB[:, P + 9:P + 10],
                                           in1=stB[:, P + 11:P + 12],
                                           op=ALU.subtract))
                dv.wait_ge(bsd2.h, j + 1)
                dv.reciprocal(out=stB[:, P + 12:P + 13],
                              in_=stB[:, P + 10:P + 11])
                dv.drain()
                dv.tensor_tensor(out=stB[:, P + 13:P + 14],
                                 in0=stB[:, P + 8:P + 9],
                                 in1=stB[:, P + 12:P + 13], op=ALU.mult)
                dv.drain()
                dv.wait_ge(by2.h, j + 1)
                if j >= 1:
                    dv.wait_ge(bow.h, 16 * j)
                dv.scalar_tensor_tensor(out=outbuf[:, :], in0=py2[:, :],
                                        scalar=stB[:, P + 12:P + 13],
                                        in1=beta2b[:, :], op0=ALU.mult,
                                        op1=ALU.add)
                dv.scalar_tensor_tensor(out=xr, in0=cs2b[:, :],
                                        scalar=stB[:, P + 13:P + 14], in1=xr,
                                        op0=ALU.mult, op1=ALU.add)
                dv.drain()
                bfin.inc(dv.tensor_tensor(out=outbuf[:, :], in0=outbuf[:, :],
                                          in1=xr, op=ALU.add))

            def zT_evac(t, r):
                g = 2 * t + r
                dv.wait_ge(bzt.h, g + 1)
                for cc in range(4):
                    c = 4 * r + cc
                    cp = dv.tensor_copy(
                        out=z1T[:, c * 256 + (t % 2) * 128:c * 256 + (t % 2) * 128 + 128],
                        in_=pzT[:, cc * 128:(cc + 1) * 128])
                bze.inc(cp)

            dv_front(0)
            dv_front(1)
            for t in (0, 1):
                zT_evac(t, 0)
                zT_evac(t, 1)
            for p in range(NT // 2):
                j0, j1 = 2 * p, 2 * p + 1
                if j0 + 2 < NT:
                    dv_front(j0 + 2)
                    zT_evac(j0 + 2, 0)
                    zT_evac(j0 + 2, 1)
                dv_h4sq(j0)
                dv_tail(j0)
                if j1 + 2 < NT:
                    dv_front(j1 + 2)
                dv_h4sq(j1)
                if j1 + 2 < NT:
                    zT_evac(j1 + 2, 0)
                    zT_evac(j1 + 2, 1)
                dv_tail(j1)

    expect = {
        "asq": NT, "avar": NT, "asd": NT, "astat": NT, "ah1": NT,
        "aqkv": 6 * NT, "aevac": 6 * NT,
        "aqs": NT, "aqr": NT, "aqh": NT,
        "aks": NT, "akr": NT, "akh": NT, "akv": NT,
        "wqkvs": 128, "wouts": 128, "smld": 16 * 7,
        "gcons": ng, "dcons": nd, "gmld": 16 * 5, "scq": 64, "bhlv": 1,
        "mpre": 2, "mtot": 2, "mval": 2, "mbc": 2, "m1r": 1, "m2r": 1,
        "abc": 2, "binit": 2, "kvc": 8, "kvw": 1, "bar": 5,
        "w1t": 128, "w2t": 512,
        "bb1": 32, "bb1ev": 32, "w1s": 1, "bb2": 1, "bb2r": 1, "w2s": 1,
        "bcs": 1, "bcsr": 1, "bpb": 4, "bprep": 1,
        "batt": 2 * NT, "bxp": NT,
        "bsqx": NT, "bvar": NT, "bsd": NT, "bst": NT, "bz1": NT,
        "by1": 16 * NT, "bh4": 16 * NT, "bsq": 32 * NT, "bsq4": 32 * NT,
        "bsumrd": NT, "by2": NT, "bvar2": NT, "bsd2": NT,
        "bfin": NT, "bow": 16 * NT,
        "bzt": 2 * NT, "bze": 2 * NT,
    }
    dexpect = {"ax": 16 * NT, "ah1w": 16 * NT, "ah1t": 512 * NG,
               "aqw": 16 * NT, "wld": 16 * NLOADS,
               "dvw": 16 * (NW1 + NW2),
               "bx": 16 * NT}
    for ds, tot in [(ax, dexpect["ax"]), (ah1w, dexpect["ah1w"]),
                    (ah1t, dexpect["ah1t"]), (aqw, dexpect["aqw"]),
                    (wld, dexpect["wld"]), (dvw, dexpect["dvw"]),
                    (bx, dexpect["bx"])]:
        assert sum(ds.totals()) == tot, \
            f"dsem {ds.name}: {ds.totals()} != {tot}"
    for s in allsems:
        if s.name in expect:
            assert s.n == expect[s.name], \
                f"sem {s.name}: emitted {s.n}, expected {expect[s.name]}"
    return nc


def _shard_inputs(inputs):
    x = np.ascontiguousarray(np.asarray(inputs["L0_attn_x"], dtype=np.float32))
    common = {
        "wqkv": np.ascontiguousarray(np.asarray(inputs["L0_attn_Wqkv"], np.float32)),
        "bqkv": np.ascontiguousarray(np.asarray(inputs["L0_attn_bqkv"], np.float32).reshape(1, E3)),
        "wout": np.ascontiguousarray(np.asarray(inputs["L0_attn_Wout"], np.float32)),
        "bout": np.ascontiguousarray(np.asarray(inputs["L0_attn_bout"], np.float32).reshape(1, D)),
        "w1": np.ascontiguousarray(np.asarray(inputs["L0_ffn_w1"], np.float32)),
        "b1f": np.ascontiguousarray(np.asarray(inputs["L0_ffn_b1"], np.float32).reshape(1, H)),
        "g1": np.ascontiguousarray(np.asarray(inputs["L0_ffn_ln1_g"], np.float32).reshape(1, D)),
        "b1l": np.ascontiguousarray(np.asarray(inputs["L0_ffn_ln1_b"], np.float32).reshape(1, D)),
        "w2": np.ascontiguousarray(np.asarray(inputs["L0_ffn_w2"], np.float32)),
        "g2": np.ascontiguousarray(np.asarray(inputs["L0_ffn_ln2_g"], np.float32).reshape(1, H)),
        "b2l": np.ascontiguousarray(np.asarray(inputs["L0_ffn_ln2_b"], np.float32).reshape(1, H)),
        "b2": np.ascontiguousarray(np.asarray(inputs["L0_ffn_b2"], np.float32).reshape(1, D)),
    }
    per_core = []
    for b in range(x.shape[0]):
        m = {"x": np.ascontiguousarray(x[b])}
        m.update(common)
        per_core.append(m)
    return per_core


def kernel(**inputs):
    per_core = _shard_inputs(inputs)
    T = per_core[0]["x"].shape[0]
    nc = build_program(T)
    res = run_bass_kernel_spmd(nc, per_core, core_ids=list(range(len(per_core))))
    return np.stack([r["out"] for r in res.results], axis=0)


# version 15
# speedup vs baseline: 1.0526x; 1.0526x over previous
import numpy as np
import ml_dtypes
from contextlib import ExitStack

import concourse.bass as bass
import concourse.mybir as mybir
from concourse.bass_utils import run_bass_kernel_spmd


F32 = mybir.dt.float32
BF16 = mybir.dt.bfloat16
AF = mybir.ActivationFunctionType
ALU = mybir.AluOpType
AX = mybir.AxisListType

D = 1024
H = 4096
E3 = 3 * D
EPS = 1e-5
N_CORES = 8
NW1 = 32
NW2 = 32


class Sem:
    def __init__(self, nc, es, name):
        self.h = es.enter_context(nc.semaphore(name))
        self.name = name
        self.n = 0

    def inc(self, inst, k=1):
        inst.then_inc(self.h, k)
        self.n += k
        return inst


class DSem:

    def __init__(self, nc, es, name, nslots=2):
        self.s = [Sem(nc, es, f"{name}{i}") for i in range(nslots)]
        self.nslots = nslots
        self.name = name

    def h(self, slot):
        return self.s[slot % self.nslots].h

    def inc(self, inst, slot, k=16):
        inst.then_inc(self.s[slot % self.nslots].h, k)
        self.s[slot % self.nslots].n += k
        return inst

    def totals(self):
        return [x.n for x in self.s]


def build_program(T):
    NT = T // 128
    NG = NT // 4
    nc = bass.Bass("TRN2", target_bir_lowering=False)

    x_d = nc.dram_tensor("x", [T, D], F32, kind="ExternalInput").ap()
    wqkv_d = nc.dram_tensor("wqkv", [D, E3], F32, kind="ExternalInput").ap()
    bqkv_d = nc.dram_tensor("bqkv", [1, E3], F32, kind="ExternalInput").ap()
    wout_d = nc.dram_tensor("wout", [D, D], F32, kind="ExternalInput").ap()
    bout_d = nc.dram_tensor("bout", [1, D], F32, kind="ExternalInput").ap()
    w1_d = nc.dram_tensor("w1", [H, D], F32, kind="ExternalInput").ap()
    b1f_d = nc.dram_tensor("b1f", [1, H], F32, kind="ExternalInput").ap()
    g1_d = nc.dram_tensor("g1", [1, D], F32, kind="ExternalInput").ap()
    b1l_d = nc.dram_tensor("b1l", [1, D], F32, kind="ExternalInput").ap()
    w2_d = nc.dram_tensor("w2", [D, H], F32, kind="ExternalInput").ap()
    g2_d = nc.dram_tensor("g2", [1, H], F32, kind="ExternalInput").ap()
    b2l_d = nc.dram_tensor("b2l", [1, H], F32, kind="ExternalInput").ap()
    b2_d = nc.dram_tensor("b2", [1, D], F32, kind="ExternalInput").ap()
    out_d = nc.dram_tensor("out", [T, D], F32, kind="ExternalOutput").ap()

    h1_d = nc.dram_tensor("h1scr", [T, D], BF16, kind="Internal").ap()
    qh_d = nc.dram_tensor("qhscr", [T, D], BF16, kind="Internal").ap()
    wq1_d = nc.dram_tensor("wq1scr", [H, D], BF16, kind="Internal").ap()
    wq2_d = nc.dram_tensor("wq2scr", [D, H], BF16, kind="Internal").ap()

    onesb_d = nc.inline_tensor(np.ones((128, 1), np.float32).astype(ml_dtypes.bfloat16))
    onesf_d = nc.inline_tensor(np.ones((128, 1), np.float32))
    onesrow_d = nc.inline_tensor(np.ones((1, 128), np.float32))
    ident_d = nc.inline_tensor(np.eye(128, dtype=np.float32).astype(ml_dtypes.bfloat16))

    es = ExitStack()

    def sbR(name, shape, dtype):
        return es.enter_context(nc.sbuf_tensor(name, shape, dtype, side="right"))

    woutp = sbR("woutp", [128, 8 * D], BF16)
    boutb = sbR("boutb", [128, D], F32)
    beta2b = sbR("beta2b", [128, D], F32)
    cs2b = sbR("cs2b", [128, D], F32)
    beta1c = sbR("beta1c", [128, 32], F32)
    g1c = sbR("g1c", [128, 8], F32)
    g2c = sbR("g2c", [128, 32], F32)
    b1lc = sbR("b1lc", [128, 8], BF16)
    b2lc = sbR("b2lc", [128, 32], BF16)
    b1fc = sbR("b1fc", [128, 32], F32)
    g1m1c = sbR("g1m1c", [128, 8], F32)
    g2m2c = sbR("g2m2c", [128, 32], F32)
    onesb = sbR("onesb", [128, 1], BF16)
    onesf = sbR("onesf", [128, 1], F32)
    onesrow = sbR("onesrow", [1, 128], F32)
    identb = sbR("identb", [128, 128], BF16)
    mcols = sbR("mcols", [128, 6], F32)
    mrow = sbR("mrow", [1, 2], F32)
    wpart = sbR("wpart", [128, 64], F32)
    kvT = sbR("kvT", [128, 8], F32)

    allsems = []

    def S(name):
        s = Sem(nc, es, name)
        allsems.append(s)
        return s

    ax = DSem(nc, es, "ax"); asq = S("asq"); avar = S("avar"); asd = S("asd")
    astat = S("astat"); ah1 = S("ah1")
    ah1w = DSem(nc, es, "ah1w"); ah1t = DSem(nc, es, "ah1t", nslots=4)
    aqkv = S("aqkv"); aevac = S("aevac")
    aqs = S("aqs"); aqr = S("aqr"); aqh = S("aqh"); aqw = DSem(nc, es, "aqw")
    aks = S("aks"); akr = S("akr"); akh = S("akh"); akv = S("akv")
    wqkvs = S("wqkvs"); wouts = S("wouts"); smld = S("smld")
    wld = DSem(nc, es, "wld", nslots=5)
    gcons = S("gcons"); dcons = S("dcons"); gmld = S("gmld")
    mpre = S("mpre"); mtot = S("mtot"); mval = S("mval"); mbc = S("mbc")
    m1r = S("m1r"); m2r = S("m2r")
    dvw = DSem(nc, es, "dvw", nslots=4); scq = S("scq"); bhlv = S("bhlv")
    abc = S("abc"); binit = S("binit")
    kvc = S("kvc"); kvw = S("kvw")
    bar = S("bar")
    w1t = S("w1t"); w2t = S("w2t")
    bb1 = S("bb1"); bb1ev = S("bb1ev"); w1s = S("w1s")
    bb2 = S("bb2"); bb2r = S("bb2r"); w2s = S("w2s")
    bcs = S("bcs"); bcsr = S("bcsr"); bpb = S("bpb"); bprep = S("bprep")
    bx = DSem(nc, es, "bx"); bqt = S("bqt")
    batt = S("batt"); bxp = S("bxp")
    bsqx = S("bsqx"); bvar = S("bvar"); bsd = S("bsd"); bst = S("bst")
    bz1 = S("bz1"); bzt = S("bzt"); bze = S("bze")
    by1 = S("by1"); bh4 = S("bh4"); bsq = S("bsq"); bsq4 = S("bsq4")
    bsumrd = S("bsumrd"); by2 = S("by2"); bsum = S("bsum")
    bvar2 = S("bvar2"); bsd2 = S("bsd2")
    bfin = S("bfin"); bow = S("bow")

    esA = ExitStack()

    def sbA(name, shape, dtype, **kw):
        return esA.enter_context(nc.sbuf_tensor(name, shape, dtype, **kw))

    wqkvp = sbA("wqkvp", [128, 8 * E3], BF16)
    bqkvb = sbA("bqkvb", [128, E3], F32)
    xbufA = sbA("xbufA", [128, 2 * D], F32)
    h1buf = sbA("h1buf", [128, 2 * D], BF16)
    h1T = sbA("h1T", [128, 2 * 4096], BF16, is_transpose_dest=True)
    qkvb = sbA("qkvb", [128, E3], F32)
    qhat = sbA("qhat", [128, 2 * D], BF16)
    khat = sbA("khat", [128, D], F32)
    kvtmp = sbA("kvtmp", [128, D], F32)
    kv_acc = sbA("kv_acc", [128, D], F32)
    wstg = sbA("wstg", [128, 5 * 1024], F32)
    qod = sbA("qod", [128, 2 * 1024], BF16)
    qog = sbA("qog", [128, 2 * 1024], BF16)
    qtd = sbA("qtd", [128, 2048], BF16)
    qtg = sbA("qtg", [128, 2048], BF16)
    sqscrA = sbA("sqscrA", [128, D], F32)
    stA = sbA("stA", [128, 32], F32)
    rowsA = sbA("rowsA", [1, E3], F32)
    boutrow = sbA("boutrow", [1, D], F32)

    psA = ExitStack()
    pq = [psA.enter_context(nc.psum_tensor(f"pqkv{i}", [128, 512], F32))
          for i in range(4)]
    pone = psA.enter_context(nc.psum_tensor("pone", [1, 8], F32))
    pmb = psA.enter_context(nc.psum_tensor("pmb", [128, 2], F32))
    pkv = psA.enter_context(nc.psum_tensor("pkv", [128, 8], F32))

    esA.close()
    psA.close()

    esB = ExitStack()

    def sbB(name, shape, dtype, **kw):
        return esB.enter_context(nc.sbuf_tensor(name, shape, dtype, **kw))

    w1p = sbB("w1p", [128, 8 * H], BF16, is_transpose_dest=True)
    w2p = sbB("w2p", [128, 32 * D], BF16, is_transpose_dest=True)
    xbufB = sbB("xbufB", [128, 2 * D], F32)
    xring = sbB("xring", [128, 3 * D], F32)
    qT = sbB("qT", [128, 1024], BF16, is_transpose_dest=True)
    z1buf = sbB("z1buf", [128, D], BF16)
    z1T = sbB("z1T", [128, 2048], BF16, is_transpose_dest=True)
    h4buf = sbB("h4buf", [128, 2 * H], BF16)
    h4sq = sbB("h4sq", [128, 4 * 128], BF16)
    outbuf = sbB("outbuf", [128, D], F32)
    stB = sbB("stB", [128, 32], F32)
    b2rowB = outbuf[0:1, 0:D]

    psB = ExitStack()
    patt = psB.enter_context(nc.psum_tensor("patt", [128, 1024], F32))
    psP = ExitStack()
    pcol = [psP.enter_context(nc.psum_tensor(f"pcol{i}", [128, 1], F32))
            for i in range(2)]
    prow = psP.enter_context(nc.psum_tensor("prow", [1, D], F32))
    pbc = [psP.enter_context(nc.psum_tensor(f"pbc{i}", [128, 512], F32))
           for i in range(2)]
    psP.close()
    py1 = [psB.enter_context(nc.psum_tensor(f"py1_{i}", [128, 256], F32))
           for i in range(2)]
    psumSQ = psB.enter_context(nc.psum_tensor("psumSQ", [128, 64], F32))
    pzT = psB.enter_context(nc.psum_tensor("pzT", [128, 512], F32))
    py2 = psB.enter_context(nc.psum_tensor("py2", [128, 1024], F32))

    wplan = ([("w1m", c) for c in range(NW1)] + [("w2m", c) for c in range(NW2)]
             + [("w1q", c) for c in range(NW1)] + [("w2q", c) for c in range(NW2)])
    cons_engine = []
    for kind, ch in wplan:
        if kind in ("w1m", "w2m"):
            cons_engine.append("dv")
        else:
            cons_engine.append("dv")
    gp_ord = {}
    dv_ord = {}
    ng = nd = 0
    for i, e in enumerate(cons_engine):
        if e == "gp":
            ng += 1
            gp_ord[i] = ng
        else:
            nd += 1
            dv_ord[i] = nd
    NLOADS = len(wplan)
    dvq_loads = [i for i, e in enumerate(cons_engine) if e == "dv"]
    NDV = len(dvq_loads)
    NGP_Q = sum(1 for i, e in enumerate(cons_engine)
                if e == "gp" and wplan[i][0] == "w2q")

    def w_src(kind, ch):
        if kind in ("w1m", "w1q"):
            return w1_d[ch * 128:(ch + 1) * 128, :]
        row, col = ch // 4, ch % 4
        return w2_d[row * 128:(row + 1) * 128, col * 1024:(col + 1) * 1024]

    def wslot(i):
        s = i % 5
        return wstg[:, s * 1024:(s + 1) * 1024]

    dvq_of_load = {}
    for i in dvq_loads:
        kind, ch = wplan[i]
        if kind in ("w1m", "w2m"):
            continue
        dvq_of_load[i] = len(dvq_of_load)
    dv_cons_of_load = {i: k + 1 for k, i in enumerate(dvq_loads)}

    sc_emit_tile = []

    def qring_a(kq):
        base = qod if kq % 4 < 2 else qtd
        s = kq % 2
        return base[:, s * 1024:(s + 1) * 1024]

    def qring_b(kq):
        base = qog if kq % 4 < 2 else qtg
        s = kq % 2
        return base[:, s * 1024:(s + 1) * 1024]

    with es, esB, psB, nc.Block() as block:

        @block.sync
        def _(sy):
            def h1t_half(jt):
                gt = jt // 4
                st = jt % 4
                hg = jt // 2
                sy.wait_ge(ah1w.h(0), 16 * (jt // 2 + 1))
                sy.wait_ge(ah1w.h(1), 16 * (jt // 2 + 1))
                if hg >= 4:
                    sy.wait_ge(ah1t.h(hg % 4), 128 * (hg // 4))
                if gt >= 2 and st == 0:
                    sy.wait_ge(aqkv.h, 24 * (gt - 1))
                for c in range(8):
                    ah1t.inc(sy.dma_start(
                        out=h1T[:, (gt % 2) * 4096 + c * 512 + st * 128:
                                (gt % 2) * 4096 + c * 512 + (st + 2) * 128],
                        in_=h1_d[jt * 128:(jt + 2) * 128, c * 128:(c + 1) * 128],
                        transpose=True), hg)

            for g in range(NG):
                for s in range(4):
                    j = 4 * g + s
                    if j >= 2:
                        sy.wait_ge(ah1.h, j - 1)
                    ax.inc(sy.dma_start(
                        out=xbufA[:, (j % 2) * D:(j % 2 + 1) * D],
                        in_=x_d[j * 128:(j + 1) * 128, :]), j)
                    if j >= 3 and j % 2 == 1:
                        h1t_half(j - 3)
            h1t_half(4 * NG - 2)
            sy.wait_ge(ax.h(0), ax.totals()[0])
            sy.wait_ge(ax.h(1), ax.totals()[1])
            for sl in range(4):
                sy.wait_ge(ah1t.h(sl), ah1t.totals()[sl])
            bar.inc(sy.nop())
            sy.wait_ge(bar.h, 5)
            for sl in range(4):
                sy.wait_ge(dvw.h(sl), 16 * 8)
            for c in range(8):
                w1t.inc(sy.dma_start(out=w1p[:, c * H:(c + 1) * H],
                                     in_=wq1_d[:, c * 128:(c + 1) * 128],
                                     transpose=True), 16)
            for sl in range(4):
                sy.wait_ge(dvw.h(sl), 16 * 16)
            for r in range(32):
                w2t.inc(sy.dma_start(out=w2p[:, r * D:(r + 1) * D],
                                     in_=wq2_d[:, r * 128:(r + 1) * 128],
                                     transpose=True), 16)
            for j in range(NT):
                if j >= 2:
                    sy.wait_ge(bsqx.h, j - 1)
                bx.inc(sy.dma_start(out=xbufB[:, (j % 2) * D:(j % 2 + 1) * D],
                                    in_=x_d[j * 128:(j + 1) * 128, :]), j)
                if j % 2 == 0:
                    if j >= 1:
                        sy.wait_ge(batt.h, 2 * j)
                    for c in range(8):
                        bqt.inc(sy.dma_start(
                            out=qT[:, c * 128:(c + 1) * 128],
                            in_=qh_d[j * 128:(j + 1) * 128, c * 128:(c + 1) * 128],
                            transpose=True), 16)
            sy.wait_ge(bow.h, 16 * NT)

        @block.scalar
        def _(sc):
            smld.inc(sc.dma_start(out=rowsA[:, :], in_=bqkv_d), 16)
            smld.inc(sc.dma_start(out=boutrow[:, :], in_=bout_d), 16)
            smld.inc(sc.dma_start(out=onesb[:, :], in_=onesb_d.ap()), 16)
            smld.inc(sc.dma_start(out=onesf[:, :], in_=onesf_d.ap()), 16)
            smld.inc(sc.dma_start(out=onesrow[:, :], in_=onesrow_d.ap()), 16)
            smld.inc(sc.dma_start(out=identb[:, :], in_=ident_d.ap()), 16)


            def scA_front(j):
                P = (j % 2) * 16
                xs = xbufA[:, (j % 2) * D:(j % 2 + 1) * D]
                sc.wait_ge(ax.h(j % 2), 16 * (j // 2 + 1))
                asq.inc(sc.activation(out=sqscrA[:, :], in_=xs, func=AF.Square,
                                      accum_out=stA[:, P + 1:P + 2]))
                sc.wait_ge(avar.h, j + 1)
                asd.inc(sc.activation(out=stA[:, P + 5:P + 6],
                                      in_=stA[:, P + 4:P + 5], func=AF.Sqrt))
                sc.wait_ge(astat.h, j + 1)
                if j >= 2:
                    sc.wait_ge(ah1w.h(j % 2), 16 * (j // 2))
                ah1.inc(sc.activation(out=h1buf[:, (j % 2) * D:(j % 2 + 1) * D],
                                      in_=xs, func=AF.Identity,
                                      scale=stA[:, P + 6:P + 7],
                                      bias=stA[:, P + 7:P + 8]))
                sc.wait_ge(ah1.h, j + 1)
                ah1w.inc(sc.dma_start(out=h1_d[j * 128:(j + 1) * 128, :],
                                      in_=h1buf[:, (j % 2) * D:(j % 2 + 1) * D]), j)

            def scA_back(u):
                P = (u % 2) * 16
                sc.wait_ge(aevac.h, 6 * u + 2)
                sc.activation(out=sqscrA[:, :], in_=qkvb[:, 0:D], func=AF.Square,
                              accum_out=stA[:, P + 8:P + 9])
                sc.drain()
                aqs.inc(sc.activation(out=stA[:, P + 9:P + 10],
                                      in_=stA[:, P + 8:P + 9], func=AF.Sqrt))
                sc.wait_ge(aevac.h, 6 * u + 4)
                sc.activation(out=sqscrA[:, :], in_=qkvb[:, D:2 * D],
                              func=AF.Square, accum_out=stA[:, P + 10:P + 11])
                sc.drain()
                aks.inc(sc.activation(out=stA[:, P + 11:P + 12],
                                      in_=stA[:, P + 10:P + 11], func=AF.Sqrt))
                sc.wait_ge(aqr.h, u + 1)
                if u >= 2:
                    sc.wait_ge(aqw.h(u % 2), 16 * (u // 2))
                aqh.inc(sc.activation(out=qhat[:, (u % 2) * D:(u % 2 + 1) * D],
                                      in_=qkvb[:, 0:D], func=AF.Identity,
                                      scale=stA[:, P + 12:P + 13]))
                sc.wait_ge(aqh.h, u + 1)
                aqw.inc(sc.dma_start(out=qh_d[u * 128:(u + 1) * 128, :],
                                     in_=qhat[:, (u % 2) * D:(u % 2 + 1) * D]), u)
                sc.wait_ge(akr.h, u + 1)
                if u >= 1:
                    sc.wait_ge(akv.h, u)
                akh.inc(sc.activation(out=khat[:, :], in_=qkvb[:, D:2 * D],
                                      func=AF.Identity, scale=stA[:, P + 13:P + 14]))

            scq_pos = [0]
            quant_loads = sorted(dvq_of_load)
            first_w2q = [True]

            def sc_consume(n, u):
                for _ in range(n):
                    k0 = scq_pos[0]
                    if k0 >= len(quant_loads):
                        return
                    i = quant_loads[k0]
                    kind, ch = wplan[i]
                    if kind == "w1q" and u < min(10, NT - 1):
                        return
                    if kind == "w2q" and u < min(18, NT - 1):
                        return
                    scq_pos[0] = k0 + 1
                    kq = k0
                    sc_emit_tile.append(u)
                    if kq == 0:
                        sc.wait_ge(m1r.h, 1)
                    if kind == "w2q" and first_w2q[0]:
                        sc.wait_ge(m2r.h, 1)
                        first_w2q[0] = False
                    sc.wait_ge(wld.h(i % 5), 16 * (i // 5 + 1))
                    if kq >= 4:
                        sc.wait_ge(dvw.h(kq - 4), 16 * ((kq - 4) // 4 + 1))
                    scol = mcols[:, 1:2] if kind == "w1q" else mcols[:, 3:4]
                    abuf = qring_a(kq)
                    bbuf = qring_b(kq)
                    sc.activation(out=abuf, in_=wslot(i), func=AF.Sign,
                                  scale=scol, bias=mcols[:, 5:6])
                    scq.inc(sc.activation(out=bbuf, in_=wslot(i), func=AF.Sign,
                                          scale=scol, bias=mcols[:, 4:5]))

            for j in range(NT):
                scA_front(j)
                if j >= 3:
                    scA_back(j - 3)
                    sc_consume(4, j - 3)
            for u in range(max(0, NT - 3), NT):
                scA_back(u)
                sc_consume(4, u)
            sc_consume(10 ** 9, NT + 99)
            sc.wait_ge(ah1w.h(0), ah1w.totals()[0])
            sc.wait_ge(ah1w.h(1), ah1w.totals()[1])
            sc.wait_ge(aqw.h(0), aqw.totals()[0])
            sc.wait_ge(aqw.h(1), aqw.totals()[1])
            sc.wait_ge(smld.h, 16 * 6)
            bar.inc(sc.drain())
            sc.wait_ge(bar.h, 5)
            smld.inc(sc.dma_start(out=b2rowB, in_=b2_d), 16)

            def qt_issue(t):
                sc.wait_ge(batt.h, 2 * t)
                for c in range(8):
                    bqt.inc(sc.dma_start(
                        out=qT[:, c * 128:(c + 1) * 128],
                        in_=qh_d[t * 128:(t + 1) * 128, c * 128:(c + 1) * 128],
                        transpose=True), 16)

            def sc_front(t):
                P = (t % 2) * 16
                xr = xring[:, (t % 3) * D:(t % 3 + 1) * D]
                xs = xbufB[:, (t % 2) * D:(t % 2 + 1) * D]
                sc.wait_ge(bxp.h, t + 1)
                sc.activation(out=xs[:, 0:512], in_=xr[:, 0:512], func=AF.Square,
                              accum_out=stB[:, P + 1:P + 2])
                sc.activation(out=xs[:, 512:1024], in_=xr[:, 512:1024],
                              func=AF.Square, accum_out=stB[:, P + 14:P + 15])
                sc.drain()
                bsqx.inc(sc.activation(out=stB[:, P + 1:P + 2],
                                       in_=stB[:, P + 14:P + 15],
                                       func=AF.Identity,
                                       bias=stB[:, P + 1:P + 2]))
                sc.wait_ge(bvar.h, t + 1)
                bsd.inc(sc.activation(out=stB[:, P + 5:P + 6],
                                      in_=stB[:, P + 4:P + 5], func=AF.Sqrt))
                sc.wait_ge(bst.h, t + 1)
                if t >= 1:
                    sc.wait_ge(bzt.h, 2 * t)
                bz1.inc(sc.activation(out=z1buf[:, :], in_=xr, func=AF.Identity,
                                      scale=stB[:, P + 6:P + 7],
                                      bias=stB[:, P + 7:P + 8]))

            def sc_tail(j):
                P = (j % 2) * 16
                sc.wait_ge(bvar2.h, j + 1)
                bsd2.inc(sc.activation(out=stB[:, P + 10:P + 11],
                                       in_=stB[:, P + 9:P + 10], func=AF.Sqrt))
                sc.wait_ge(bfin.h, j + 1)
                bow.inc(sc.dma_start(out=out_d[j * 128:(j + 1) * 128, :],
                                     in_=outbuf[:, :]), 16)

            qt_issue(1)
            sc_front(0)
            sc_front(1)
            for p in range(NT // 2):
                j0, j1 = 2 * p, 2 * p + 1
                if j1 + 2 < NT:
                    qt_issue(j1 + 2)
                if j0 + 2 < NT:
                    sc_front(j0 + 2)
                if p >= 1:
                    sc.wait_ge(bsq4.h, 32 * j0)
                for m in range(32):
                    sc.wait_ge(by1.h, 32 * p + m + 1)
                    bh4.inc(sc.activation(
                        out=h4buf[:, m * 256:(m + 1) * 256],
                        in_=py1[m % 2][:, :], func=AF.Gelu,
                        bias=beta1c[:, m:m + 1]))
                sc_tail(j0)
                if j1 + 2 < NT:
                    sc_front(j1 + 2)
                sc_tail(j1)
            sc.wait_ge(bow.h, 16 * NT)

        @block.gpsimd
        def _(gp):
            for c in range(8):
                wqkvs.inc(gp.dma_start(out=wqkvp[:, c * E3:(c + 1) * E3],
                                       in_=wqkv_d[c * 128:(c + 1) * 128, :]), 16)
            for c in range(8):
                wouts.inc(gp.dma_start(out=woutp[:, c * D:(c + 1) * D],
                                       in_=wout_d[c * 128:(c + 1) * 128, :]), 16)
            with nc.allow_non_contiguous_dma("tiny columnize loads"):
                gmld.inc(gp.dma_start(
                    out=g1c[:, :], in_=g1_d.rearrange("o (c p) -> (o p) c", p=128)), 16)
                gmld.inc(gp.dma_start(
                    out=b1lc[:, :], in_=b1l_d.rearrange("o (c p) -> (o p) c", p=128)), 16)
                gmld.inc(gp.dma_start(
                    out=g2c[:, :], in_=g2_d.rearrange("o (c p) -> (o p) c", p=128)), 16)
                gmld.inc(gp.dma_start(
                    out=b2lc[:, :], in_=b2l_d.rearrange("o (c p) -> (o p) c", p=128)), 16)
                gmld.inc(gp.dma_start(
                    out=b1fc[:, :], in_=b1f_d.rearrange("o (c p) -> (o p) c", p=128)), 16)
            def dv_write(ii):
                kq = dvq_of_load[ii]
                gp.wait_ge(dcons.h, dv_cons_of_load[ii])
                kk, cc = wplan[ii]
                srcb = qring_a(kq)
                if kk == "w1q":
                    dvw.inc(gp.dma_start(
                        out=wq1_d[cc * 128:(cc + 1) * 128, :], in_=srcb), kq)
                else:
                    row, col = cc // 4, cc % 4
                    dvw.inc(gp.dma_start(
                        out=wq2_d[row * 128:(row + 1) * 128,
                                  col * 1024:(col + 1) * 1024], in_=srcb), kq)

            for i, (kind, ch) in enumerate(wplan):
                if i >= 5 and cons_engine[i - 5] == "dv":
                    gp.wait_ge(dcons.h, dv_cons_of_load[i - 5])
                wld.inc(gp.dma_start(out=wslot(i), in_=w_src(kind, ch)), i)
                if i >= 2:
                    if (i - 2) in dvq_of_load:
                        dv_write(i - 2)
            for ii in (NLOADS - 2, NLOADS - 1):
                if ii in dvq_of_load:
                    dv_write(ii)
            gp.wait_ge(wqkvs.h, 128)
            gp.wait_ge(wouts.h, 128)
            gp.wait_ge(gmld.h, 16 * 5)
            for sl in range(5):
                gp.wait_ge(wld.h(sl), wld.totals()[sl])
            for sl in range(4):
                gp.wait_ge(dvw.h(sl), dvw.totals()[sl])
            bar.inc(gp.drain())
            gp.wait_ge(bar.h, 5)

        p_mtot1 = (min(1, NG - 1), 3)
        p_mbc1 = (min(2, NG - 1), 1) if NG >= 3 else p_mtot1
        p_mtot2 = (min(3, NG - 1), 3) if NG >= 4 else p_mbc1
        p_mbc2 = (min(4, NG - 1), 1) if NG >= 5 else p_mtot2

        @block.tensor
        def _(te):
            te.wait_ge(smld.h, 16 * 6)
            for e in range(6):
                if e >= 2:
                    te.wait_ge(binit.h, e - 1)
                abc.inc(te.matmul(out=pq[e % 2][:, :], lhsT=onesrow[:, :],
                                  rhs=rowsA[0:1, e * 512:(e + 1) * 512],
                                  start=True, stop=True))
            for hh in range(2):
                te.wait_ge(binit.h, 5 + hh)
                abc.inc(te.matmul(out=pq[hh][:, :], lhsT=onesrow[:, :],
                                  rhs=boutrow[0:1, hh * 512:(hh + 1) * 512],
                                  start=True, stop=True))
            te.wait_ge(binit.h, 8)
            te.wait_ge(wqkvs.h, 128)
            for g in range(NG):
                for s in range(4):
                    hg = 2 * g + s // 2
                    te.wait_ge(ah1t.h(hg % 4), 128 * (hg // 4 + 1))
                    for e in range(6):
                        i = (4 * g + s) * 6 + e
                        if i >= 4:
                            te.wait_ge(aevac.h, i - 3)
                        for c in range(8):
                            mm = te.matmul(
                                out=pq[i % 4][:, :],
                                lhsT=h1T[:, (g % 2) * 4096 + c * 512 + s * 128:
                                         (g % 2) * 4096 + c * 512 + (s + 1) * 128],
                                rhs=wqkvp[:, c * E3 + e * 512:c * E3 + (e + 1) * 512],
                                start=(c == 0), stop=(c == 7))
                        aqkv.inc(mm)
                    if (g, s) == p_mtot1:
                        te.wait_ge(mpre.h, 1)
                        mtot.inc(te.matmul(out=pone[:, 0:1],
                                           lhsT=stA[:, 30:31], rhs=onesf[:, :],
                                           start=True, stop=True))
                    if (g, s) == p_mbc1:
                        te.wait_ge(mval.h, 1)
                        mbc.inc(te.matmul(out=pmb[:, 0:1], lhsT=onesrow[:, :],
                                          rhs=mrow[0:1, 0:1],
                                          start=True, stop=True))
                    if (g, s) == p_mtot2:
                        te.wait_ge(mpre.h, 2)
                        mtot.inc(te.matmul(out=pone[:, 1:2],
                                           lhsT=stA[:, 31:32], rhs=onesf[:, :],
                                           start=True, stop=True))
                    if (g, s) == p_mbc2:
                        te.wait_ge(mval.h, 2)
                        mbc.inc(te.matmul(out=pmb[:, 1:2], lhsT=onesrow[:, :],
                                          rhs=mrow[0:1, 1:2],
                                          start=True, stop=True))
            te.wait_ge(akv.h, NT)
            te.wait_ge(aevac.h, 6 * NT)
            for c in range(8):
                kvc.inc(te.matmul(out=pkv[:, c:c + 1],
                                  lhsT=kv_acc[:, c * 128:(c + 1) * 128],
                                  rhs=onesf[:, :], start=True, stop=True))
            bar.inc(te.drain())
            te.wait_ge(bar.h, 5)
            te.wait_ge(bhlv.h, 1)
            te.wait_ge(w1t.h, 128)
            for m in range(32):
                if m >= 2:
                    te.wait_ge(bb1ev.h, m - 1)
                for c in range(8):
                    mm = te.matmul(out=pcol[m % 2][:, :],
                                   lhsT=w1p[:, c * H + m * 128:c * H + (m + 1) * 128],
                                   rhs=b1lc[:, c:c + 1],
                                   start=(c == 0), stop=(c == 7))
                bb1.inc(mm)
            te.wait_ge(w2t.h, 16 * 32)
            for r in range(32):
                for hh in range(2):
                    mm = te.matmul(out=prow[0:1, hh * 512:(hh + 1) * 512],
                                   lhsT=b2lc[:, r:r + 1],
                                   rhs=w2p[:, r * D + hh * 512:r * D + (hh + 1) * 512],
                                   start=(r == 0), stop=(r == 31))
            bb2.inc(mm)
            te.wait_ge(bb2r.h, 1)
            for hh in range(2):
                bpb.inc(te.matmul(out=pbc[hh][:, :], lhsT=onesrow[:, :],
                                  rhs=outbuf[0:1, hh * 512:(hh + 1) * 512],
                                  start=True, stop=True))
            te.wait_ge(w2s.h, 1)
            for r in range(32):
                for hh in range(2):
                    mm = te.matmul(out=prow[0:1, hh * 512:(hh + 1) * 512],
                                   lhsT=onesb[:, :],
                                   rhs=w2p[:, r * D + hh * 512:r * D + (hh + 1) * 512],
                                   start=(r == 0), stop=(r == 31))
            bcs.inc(mm)
            te.wait_ge(bcsr.h, 1)
            for hh in range(2):
                bpb.inc(te.matmul(out=pbc[hh][:, :], lhsT=onesrow[:, :],
                                  rhs=outbuf[0:1, hh * 512:(hh + 1) * 512],
                                  start=True, stop=True))
            def attn_mm(t):
                te.wait_ge(bqt.h, 128 * (t + 1))
                te.wait_ge(kvw.h, 1)
                if t >= 1:
                    te.wait_ge(bxp.h, t)
                for hh in range(2):
                    for c in range(8):
                        mm = te.matmul(
                            out=patt[:, hh * 512:(hh + 1) * 512],
                            lhsT=qT[:, c * 128:(c + 1) * 128],
                            rhs=woutp[:, c * D + hh * 512:c * D + (hh + 1) * 512],
                            start=(c == 0), stop=(c == 7))
                    batt.inc(mm)

            def y2_tile(j):
                p = j // 2
                te.wait_ge(bprep.h, 1)
                if j >= 1:
                    te.wait_ge(bfin.h, j)
                    te.wait_ge(bsumrd.h, j)
                for m in range(32):
                    q = 32 * j + m
                    te.wait_ge(bh4.h, 32 * p + m + 1)
                    h4m = h4buf[:, m * 256 + (j % 2) * 128:m * 256 + (j % 2) * 128 + 128]
                    for hh in range(2):
                        mm = te.matmul(out=py2[:, hh * 512:(hh + 1) * 512],
                                       lhsT=h4m,
                                       rhs=w2p[:, m * D + hh * 512:m * D + (hh + 1) * 512],
                                       start=(m == 0), stop=(m == 31))
                        if hh == 1 and m == 31:
                            by2.inc(mm)
                    te.matmul(out=psumSQ[:, m:m + 1], lhsT=h4m, rhs=onesb[:, :],
                              start=True, stop=True)
                    te.wait_ge(bsq.h, q + 1)
                    bsq4.inc(te.matmul(out=psumSQ[:, 32 + m:33 + m],
                                       lhsT=h4sq[:, (m % 4) * 128:(m % 4 + 1) * 128],
                                       rhs=onesb[:, :], start=True, stop=True))

            def zT_round(t, r):
                g = 2 * t + r
                if r == 0:
                    te.wait_ge(bz1.h, t + 1)
                if g >= 1:
                    te.wait_ge(bze.h, g)
                for cc in range(4):
                    c = 4 * r + cc
                    mm = te.matmul(out=pzT[:, cc * 128:(cc + 1) * 128],
                                   lhsT=z1buf[:, c * 128:(c + 1) * 128],
                                   rhs=identb[:, :], start=True, stop=True)
                bzt.inc(mm)

            attn_mm(0)
            attn_mm(1)
            for t in (0, 1):
                zT_round(t, 0)
                zT_round(t, 1)
            for p in range(NT // 2):
                j0, j1 = 2 * p, 2 * p + 1
                if j0 + 2 < NT:
                    attn_mm(j0 + 2)
                te.wait_ge(bze.h, 4 * p + 4)
                te.wait_ge(w1s.h, 1)
                for m in range(32):
                    if 32 * p + m >= 2:
                        te.wait_ge(bh4.h, 32 * p + m - 1)
                    for c in range(8):
                        mm = te.matmul(
                            out=py1[m % 2][:, :],
                            lhsT=w1p[:, c * H + m * 128:c * H + (m + 1) * 128],
                            rhs=z1T[:, c * 256:(c + 1) * 256],
                            start=(c == 0), stop=(c == 7))
                    by1.inc(mm)
                if j1 + 2 < NT:
                    attn_mm(j1 + 2)
                if j0 + 2 < NT:
                    zT_round(j0 + 2, 0)
                    zT_round(j0 + 2, 1)
                y2_tile(j0)
                y2_tile(j1)
                if j1 + 2 < NT:
                    zT_round(j1 + 2, 0)
                    zT_round(j1 + 2, 1)

        @block.vector
        def _(dv):
            dv.memset(kv_acc[:, :], 0.0)
            dv.memset(mcols[:, 4:5], 0.5)
            dv.memset(mcols[:, 5:6], -0.5)
            dv.wait_ge(gmld.h, 16 * 5)
            dv.tensor_scalar(out=b1lc[:, :], in0=b1lc[:, :], scalar1=0.5,
                             scalar2=None, op0=ALU.mult)
            bhlv.inc(dv.tensor_scalar(out=b2lc[:, :], in0=b2lc[:, :],
                                      scalar1=0.5, scalar2=None, op0=ALU.mult))
            for e in range(6):
                dv.wait_ge(abc.h, e + 1)
                binit.inc(dv.tensor_copy(out=bqkvb[:, e * 512:(e + 1) * 512],
                                         in_=pq[e % 2][:, :]))
            for hh in range(2):
                dv.wait_ge(abc.h, 7 + hh)
                binit.inc(dv.tensor_copy(out=boutb[:, hh * 512:(hh + 1) * 512],
                                         in_=pq[hh][:, :]))
            dvq_pos = [0]

            def dv_consume(n, jcur, limit=None, max_i=None):
                for _ in range(n):
                    k0 = dvq_pos[0]
                    if k0 >= NDV or (limit is not None and k0 >= limit):
                        return
                    if max_i is not None and dvq_loads[k0] >= max_i:
                        return
                    i0 = dvq_loads[k0]
                    kq0 = dvq_of_load.get(i0)
                    if kq0 is not None and (kq0 >= len(sc_emit_tile)
                                            or sc_emit_tile[kq0] > jcur - 3):
                        return
                    dvq_pos[0] = k0 + 1
                    i = dvq_loads[k0]
                    kind, ch = wplan[i]
                    if kind in ("w1m", "w2m"):
                        dv.wait_ge(wld.h(i % 5), 16 * (i // 5 + 1))
                        col = ch if kind == "w1m" else NW1 + ch
                        dcons.inc(dv.tensor_reduce(
                            out=wpart[:, col:col + 1], in_=wslot(i),
                            axis=AX.X, op=ALU.add, apply_absolute_value=True))
                        continue
                    kq = dvq_of_load[i]
                    dst = qring_a(kq)
                    srcb = qring_b(kq)
                    dv.wait_ge(scq.h, kq + 1)
                    dcons.inc(dv.tensor_tensor(out=dst, in0=dst, in1=srcb,
                                               op=ALU.add))

            def dvA_front(j):
                P = (j % 2) * 16
                xs = xbufA[:, (j % 2) * D:(j % 2 + 1) * D]
                dv.wait_ge(ax.h(j % 2), 16 * (j // 2 + 1))
                if j >= 2:
                    dv.wait_ge(ah1.h, j - 1)
                    dv.wait_ge(asd.h, j - 1)
                dv.reduce_sum(out=stA[:, P + 0:P + 1], in_=xs, axis=AX.X,
                              negate=True)
                dv.drain()
                dv.tensor_scalar(out=stA[:, P + 2:P + 3], in0=stA[:, P + 0:P + 1],
                                 scalar1=1.0 / D, scalar2=None, op0=ALU.mult)
                dv.wait_ge(asq.h, j + 1)
                dv.tensor_scalar(out=stA[:, P + 3:P + 4], in0=stA[:, P + 1:P + 2],
                                 scalar1=1.0 / D, scalar2=EPS,
                                 op0=ALU.mult, op1=ALU.add)
                dv.drain()
                dv.tensor_tensor(out=stA[:, P + 4:P + 5], in0=stA[:, P + 2:P + 3],
                                 in1=stA[:, P + 2:P + 3], op=ALU.mult)
                dv.drain()
                avar.inc(dv.tensor_tensor(out=stA[:, P + 4:P + 5],
                                          in0=stA[:, P + 3:P + 4],
                                          in1=stA[:, P + 4:P + 5],
                                          op=ALU.subtract))
                dv.wait_ge(asd.h, j + 1)
                dv.reciprocal(out=stA[:, P + 6:P + 7], in_=stA[:, P + 5:P + 6])
                dv.drain()
                astat.inc(dv.tensor_tensor(out=stA[:, P + 7:P + 8],
                                           in0=stA[:, P + 2:P + 3],
                                           in1=stA[:, P + 6:P + 7], op=ALU.mult))
            def dvA_back(u):
                P = (u % 2) * 16
                for e in range(6):
                    i = u * 6 + e
                    dv.wait_ge(aqkv.h, i + 1)
                    if e == 0 and u >= 1:
                        dv.wait_ge(aqh.h, u)
                    if e == 2 and u >= 1:
                        dv.wait_ge(akh.h, u)
                    aevac.inc(dv.tensor_tensor(
                        out=qkvb[:, e * 512:(e + 1) * 512],
                        in0=pq[i % 4][:, :],
                        in1=bqkvb[:, e * 512:(e + 1) * 512], op=ALU.add))
                dv.wait_ge(aqs.h, u + 1)
                aqr.inc(dv.reciprocal(out=stA[:, P + 12:P + 13],
                                      in_=stA[:, P + 9:P + 10]))
                dv.wait_ge(aks.h, u + 1)
                akr.inc(dv.reciprocal(out=stA[:, P + 13:P + 14],
                                      in_=stA[:, P + 11:P + 12]))
                dv.wait_ge(akh.h, u + 1)
                dv.tensor_tensor(out=kvtmp[:, :], in0=khat[:, :],
                                 in1=qkvb[:, 2 * D:3 * D], op=ALU.mult)
                dv.drain()
                akv.inc(dv.tensor_tensor(out=kv_acc[:, :], in0=kv_acc[:, :],
                                         in1=kvtmp[:, :], op=ALU.add))
            def dvA_extra(u):
                if u == min(5, NT - 1):
                    assert dvq_pos[0] >= NW1 or NT < 32, dvq_pos[0]
                    dv.drain()
                    mpre.inc(dv.reduce_sum(out=stA[:, 30:31],
                                           in_=wpart[:, 0:NW1], axis=AX.X))
                if u == min(8, NT - 1):
                    dv.wait_ge(mtot.h, 1)
                    mval.inc(dv.tensor_scalar(out=mrow[0:1, 0:1],
                                              in0=pone[0:1, 0:1],
                                              scalar1=1.0 / (H * D),
                                              scalar2=EPS,
                                              op0=ALU.mult, op1=ALU.max))
                if u == min(10, NT - 1):
                    dv.wait_ge(mbc.h, 1)
                    dv.tensor_copy(out=mcols[:, 0:1], in_=pmb[:, 0:1])
                    dv.drain()
                    m1r.inc(dv.reciprocal(out=mcols[:, 1:2],
                                          in_=mcols[:, 0:1]))
                    dv.drain()
                if u == min(12, NT - 1):
                    assert dvq_pos[0] >= NW1 + NW2 or NT < 32, dvq_pos[0]
                    dv.drain()
                    mpre.inc(dv.reduce_sum(out=stA[:, 31:32],
                                           in_=wpart[:, NW1:NW1 + NW2],
                                           axis=AX.X))
                if u == min(16, NT - 1):
                    dv.wait_ge(mtot.h, 2)
                    mval.inc(dv.tensor_scalar(out=mrow[0:1, 1:2],
                                              in0=pone[0:1, 1:2],
                                              scalar1=1.0 / (H * D),
                                              scalar2=EPS,
                                              op0=ALU.mult, op1=ALU.max))
                if u == min(18, NT - 1):
                    dv.wait_ge(mbc.h, 2)
                    dv.tensor_copy(out=mcols[:, 2:3], in_=pmb[:, 1:2])
                    dv.drain()
                    m2r.inc(dv.reciprocal(out=mcols[:, 3:4],
                                          in_=mcols[:, 2:3]))
                    dv.drain()

            for j in range(NT):
                dvA_front(j)
                if j >= 3:
                    dvA_back(j - 3)
                    dvA_extra(j - 3)
                if NT >= 32:
                    dv_consume(6, j)
                else:
                    dv_consume(64 // NT + 1, j, limit=NW1 + NW2)
            for u in range(max(0, NT - 3), NT):
                dvA_back(u)
                dvA_extra(u)
            dv_consume(10 ** 9, NT + 999)
            dv.wait_ge(kvc.h, 8)
            dv.tensor_copy(out=kvT[:, :], in_=pkv[:, :])
            dv.drain()
            dv.wait_ge(wouts.h, 128)
            for c in range(8):
                dv.tensor_scalar(out=woutp[:, c * D:(c + 1) * D],
                                 in0=woutp[:, c * D:(c + 1) * D],
                                 scalar1=kvT[:, c:c + 1], scalar2=None,
                                 op0=ALU.mult)
            kvw.inc(dv.drain())
            bar.inc(dv.drain())
            dv.wait_ge(bar.h, 5)
            dv.tensor_scalar(out=g1m1c[:, :], in0=g1c[:, :],
                             scalar1=mcols[:, 0:1], scalar2=0.5, op0=ALU.mult,
                             op1=ALU.mult)
            dv.tensor_scalar(out=g2m2c[:, :], in0=g2c[:, :],
                             scalar1=mcols[:, 2:3], scalar2=0.5, op0=ALU.mult,
                             op1=ALU.mult)
            for m in range(32):
                dv.wait_ge(bb1.h, m + 1)
                bb1ev.inc(dv.scalar_tensor_tensor(
                    out=beta1c[:, m:m + 1], in0=pcol[m % 2][:, :],
                    scalar=mcols[:, 0:1], in1=b1fc[:, m:m + 1],
                    op0=ALU.mult, op1=ALU.add))
            dv.drain()
            for c in range(8):
                dv.tensor_scalar(out=w1p[:, c * H:(c + 1) * H],
                                 in0=w1p[:, c * H:(c + 1) * H],
                                 scalar1=g1m1c[:, c:c + 1], scalar2=None,
                                 op0=ALU.mult)
            w1s.inc(dv.drain())
            dv.wait_ge(bb2.h, 1)
            dv.wait_ge(smld.h, 16 * 7)
            bb2r.inc(dv.scalar_tensor_tensor(
                out=b2rowB, in0=prow[0:1, :], scalar=mcols[0:1, 2:3],
                in1=b2rowB, op0=ALU.mult, op1=ALU.add))
            dv.drain()
            for r in range(32):
                dv.tensor_scalar(out=w2p[:, r * D:(r + 1) * D],
                                 in0=w2p[:, r * D:(r + 1) * D],
                                 scalar1=g2m2c[:, r:r + 1], scalar2=None,
                                 op0=ALU.mult)
            w2s.inc(dv.drain())
            dv.wait_ge(bpb.h, 2)
            dv.tensor_copy(out=beta2b[:, 0:512], in_=pbc[0][:, :])
            dv.tensor_copy(out=beta2b[:, 512:1024], in_=pbc[1][:, :])
            dv.wait_ge(bcs.h, 1)
            bcsr.inc(dv.tensor_copy(out=b2rowB, in_=prow[0:1, :]))
            dv.wait_ge(bpb.h, 4)
            dv.tensor_copy(out=cs2b[:, 0:512], in_=pbc[0][:, :])
            bprep.inc(dv.tensor_copy(out=cs2b[:, 512:1024], in_=pbc[1][:, :]))
            def dv_front(t):
                P = (t % 2) * 16
                xs = xbufB[:, (t % 2) * D:(t % 2 + 1) * D]
                xr = xring[:, (t % 3) * D:(t % 3 + 1) * D]
                dv.wait_ge(bx.h(t % 2), 16 * (t // 2 + 1))
                dv.tensor_tensor(out=xs, in0=xs, in1=boutb[:, :], op=ALU.add)
                dv.drain()
                dv.wait_ge(batt.h, 2 * t + 2)
                if t >= 2:
                    dv.wait_ge(bz1.h, t - 1)
                    dv.wait_ge(bsd.h, t - 1)
                bxp.inc(dv.tensor_tensor(out=xr, in0=patt[:, :], in1=xs,
                                         op=ALU.add))
                dv.drain()
                dv.reduce_sum(out=stB[:, P + 0:P + 1], in_=xr, axis=AX.X,
                              negate=True)
                dv.drain()
                dv.tensor_scalar(out=stB[:, P + 2:P + 3], in0=stB[:, P + 0:P + 1],
                                 scalar1=1.0 / D, scalar2=None, op0=ALU.mult)
                dv.wait_ge(bsqx.h, t + 1)
                dv.tensor_scalar(out=stB[:, P + 3:P + 4], in0=stB[:, P + 1:P + 2],
                                 scalar1=(1.0 + EPS) / D, scalar2=EPS * EPS,
                                 op0=ALU.mult, op1=ALU.add)
                dv.drain()
                dv.tensor_tensor(out=stB[:, P + 4:P + 5], in0=stB[:, P + 2:P + 3],
                                 in1=stB[:, P + 2:P + 3], op=ALU.mult)
                dv.drain()
                bvar.inc(dv.scalar_tensor_tensor(
                    out=stB[:, P + 4:P + 5], in0=stB[:, P + 4:P + 5],
                    scalar=-(1.0 + EPS), in1=stB[:, P + 3:P + 4],
                    op0=ALU.mult, op1=ALU.add))
                dv.wait_ge(bsd.h, t + 1)
                dv.reciprocal(out=stB[:, P + 6:P + 7], in_=stB[:, P + 5:P + 6])
                dv.drain()
                bst.inc(dv.tensor_tensor(out=stB[:, P + 7:P + 8],
                                         in0=stB[:, P + 2:P + 3],
                                         in1=stB[:, P + 6:P + 7], op=ALU.mult))

            def dv_h4sq(j):
                p = j // 2
                if j >= 2:
                    dv.wait_ge(bsq4.h, 32 * (j - 1))
                for m in range(32):
                    q = 32 * j + m
                    dv.wait_ge(bh4.h, 32 * p + m + 1)
                    if q >= 4:
                        dv.wait_ge(bsq4.h, q - 3)
                    h4m = h4buf[:, m * 256 + (j % 2) * 128:m * 256 + (j % 2) * 128 + 128]
                    bsq.inc(dv.tensor_tensor(
                        out=h4sq[:, (m % 4) * 128:(m % 4 + 1) * 128],
                        in0=h4m, in1=h4m, op=ALU.mult))

            def dv_tail(j):
                P = (j % 2) * 16
                xr = xring[:, (j % 3) * D:(j % 3 + 1) * D]
                dv.wait_ge(bsq4.h, 32 * (j + 1))
                if j >= 2:
                    dv.wait_ge(bsd2.h, j - 1)
                dv.reduce_sum(out=stB[:, P + 8:P + 9], in_=psumSQ[:, 0:32],
                              axis=AX.X, negate=True)
                bsumrd.inc(dv.reduce_sum(out=stB[:, P + 9:P + 10],
                                         in_=psumSQ[:, 32:64], axis=AX.X))
                dv.drain()
                dv.tensor_scalar(out=stB[:, P + 8:P + 9],
                                 in0=stB[:, P + 8:P + 9],
                                 scalar1=1.0 / H, scalar2=None, op0=ALU.mult)
                dv.tensor_scalar(out=stB[:, P + 9:P + 10],
                                 in0=stB[:, P + 9:P + 10],
                                 scalar1=1.0 / H, scalar2=EPS,
                                 op0=ALU.mult, op1=ALU.add)
                dv.drain()
                dv.tensor_tensor(out=stB[:, P + 11:P + 12],
                                 in0=stB[:, P + 8:P + 9],
                                 in1=stB[:, P + 8:P + 9], op=ALU.mult)
                dv.drain()
                bvar2.inc(dv.tensor_tensor(out=stB[:, P + 9:P + 10],
                                           in0=stB[:, P + 9:P + 10],
                                           in1=stB[:, P + 11:P + 12],
                                           op=ALU.subtract))
                dv.wait_ge(bsd2.h, j + 1)
                dv.reciprocal(out=stB[:, P + 12:P + 13],
                              in_=stB[:, P + 10:P + 11])
                dv.drain()
                dv.tensor_tensor(out=stB[:, P + 13:P + 14],
                                 in0=stB[:, P + 8:P + 9],
                                 in1=stB[:, P + 12:P + 13], op=ALU.mult)
                dv.drain()
                dv.wait_ge(by2.h, j + 1)
                if j >= 1:
                    dv.wait_ge(bow.h, 16 * j)
                dv.scalar_tensor_tensor(out=outbuf[:, :], in0=py2[:, :],
                                        scalar=stB[:, P + 12:P + 13],
                                        in1=beta2b[:, :], op0=ALU.mult,
                                        op1=ALU.add)
                dv.scalar_tensor_tensor(out=xr, in0=cs2b[:, :],
                                        scalar=stB[:, P + 13:P + 14], in1=xr,
                                        op0=ALU.mult, op1=ALU.add)
                dv.drain()
                bfin.inc(dv.tensor_tensor(out=outbuf[:, :], in0=outbuf[:, :],
                                          in1=xr, op=ALU.add))

            def zT_evac(t, r):
                g = 2 * t + r
                dv.wait_ge(bzt.h, g + 1)
                for cc in range(4):
                    c = 4 * r + cc
                    cp = dv.tensor_copy(
                        out=z1T[:, c * 256 + (t % 2) * 128:c * 256 + (t % 2) * 128 + 128],
                        in_=pzT[:, cc * 128:(cc + 1) * 128])
                bze.inc(cp)

            dv_front(0)
            dv_front(1)
            for t in (0, 1):
                zT_evac(t, 0)
                zT_evac(t, 1)
            for p in range(NT // 2):
                j0, j1 = 2 * p, 2 * p + 1
                if j0 + 2 < NT:
                    dv_front(j0 + 2)
                    zT_evac(j0 + 2, 0)
                    zT_evac(j0 + 2, 1)
                dv_h4sq(j0)
                dv_tail(j0)
                if j1 + 2 < NT:
                    dv_front(j1 + 2)
                dv_h4sq(j1)
                if j1 + 2 < NT:
                    zT_evac(j1 + 2, 0)
                    zT_evac(j1 + 2, 1)
                dv_tail(j1)

    expect = {
        "asq": NT, "avar": NT, "asd": NT, "astat": NT, "ah1": NT,
        "aqkv": 6 * NT, "aevac": 6 * NT,
        "aqs": NT, "aqr": NT, "aqh": NT,
        "aks": NT, "akr": NT, "akh": NT, "akv": NT,
        "wqkvs": 128, "wouts": 128, "smld": 16 * 7,
        "gcons": ng, "dcons": nd, "gmld": 16 * 5, "scq": 64, "bhlv": 1,
        "mpre": 2, "mtot": 2, "mval": 2, "mbc": 2, "m1r": 1, "m2r": 1,
        "abc": 8, "binit": 8, "kvc": 8, "kvw": 1, "bar": 5,
        "w1t": 128, "w2t": 512,
        "bb1": 32, "bb1ev": 32, "w1s": 1, "bb2": 1, "bb2r": 1, "w2s": 1,
        "bcs": 1, "bcsr": 1, "bpb": 4, "bprep": 1,
        "batt": 2 * NT, "bxp": NT,
        "bsqx": NT, "bvar": NT, "bsd": NT, "bst": NT, "bz1": NT,
        "by1": 16 * NT, "bh4": 16 * NT, "bsq": 32 * NT, "bsq4": 32 * NT,
        "bsumrd": NT, "by2": NT, "bvar2": NT, "bsd2": NT,
        "bfin": NT, "bow": 16 * NT,
        "bzt": 2 * NT, "bze": 2 * NT,
    }
    dexpect = {"ax": 16 * NT, "ah1w": 16 * NT, "ah1t": 64 * NT,
               "aqw": 16 * NT, "wld": 16 * NLOADS,
               "dvw": 16 * (NW1 + NW2),
               "bx": 16 * NT}
    for ds, tot in [(ax, dexpect["ax"]), (ah1w, dexpect["ah1w"]),
                    (ah1t, dexpect["ah1t"]), (aqw, dexpect["aqw"]),
                    (wld, dexpect["wld"]), (dvw, dexpect["dvw"]),
                    (bx, dexpect["bx"])]:
        assert sum(ds.totals()) == tot, \
            f"dsem {ds.name}: {ds.totals()} != {tot}"
    for s in allsems:
        if s.name in expect:
            assert s.n == expect[s.name], \
                f"sem {s.name}: emitted {s.n}, expected {expect[s.name]}"
    return nc


def _shard_inputs(inputs):
    x = np.ascontiguousarray(np.asarray(inputs["L0_attn_x"], dtype=np.float32))
    common = {
        "wqkv": np.ascontiguousarray(np.asarray(inputs["L0_attn_Wqkv"], np.float32)),
        "bqkv": np.ascontiguousarray(np.asarray(inputs["L0_attn_bqkv"], np.float32).reshape(1, E3)),
        "wout": np.ascontiguousarray(np.asarray(inputs["L0_attn_Wout"], np.float32)),
        "bout": np.ascontiguousarray(np.asarray(inputs["L0_attn_bout"], np.float32).reshape(1, D)),
        "w1": np.ascontiguousarray(np.asarray(inputs["L0_ffn_w1"], np.float32)),
        "b1f": np.ascontiguousarray(np.asarray(inputs["L0_ffn_b1"], np.float32).reshape(1, H)),
        "g1": np.ascontiguousarray(np.asarray(inputs["L0_ffn_ln1_g"], np.float32).reshape(1, D)),
        "b1l": np.ascontiguousarray(np.asarray(inputs["L0_ffn_ln1_b"], np.float32).reshape(1, D)),
        "w2": np.ascontiguousarray(np.asarray(inputs["L0_ffn_w2"], np.float32)),
        "g2": np.ascontiguousarray(np.asarray(inputs["L0_ffn_ln2_g"], np.float32).reshape(1, H)),
        "b2l": np.ascontiguousarray(np.asarray(inputs["L0_ffn_ln2_b"], np.float32).reshape(1, H)),
        "b2": np.ascontiguousarray(np.asarray(inputs["L0_ffn_b2"], np.float32).reshape(1, D)),
    }
    per_core = []
    for b in range(x.shape[0]):
        m = {"x": np.ascontiguousarray(x[b])}
        m.update(common)
        per_core.append(m)
    return per_core


def kernel(**inputs):
    per_core = _shard_inputs(inputs)
    T = per_core[0]["x"].shape[0]
    nc = build_program(T)
    res = run_bass_kernel_spmd(nc, per_core, core_ids=list(range(len(per_core))))
    return np.stack([r["out"] for r in res.results], axis=0)


# version 16 (final)
# speedup vs baseline: 1.1112x; 1.0556x over previous
import numpy as np
import ml_dtypes
from contextlib import ExitStack

import concourse.bass as bass
import concourse.mybir as mybir
from concourse.bass_utils import run_bass_kernel_spmd


F32 = mybir.dt.float32
BF16 = mybir.dt.bfloat16
AF = mybir.ActivationFunctionType
ALU = mybir.AluOpType
AX = mybir.AxisListType

D = 1024
H = 4096
E3 = 3 * D
EPS = 1e-5
N_CORES = 8
NW1 = 32
NW2 = 32


class Sem:
    def __init__(self, nc, es, name):
        self.h = es.enter_context(nc.semaphore(name))
        self.name = name
        self.n = 0

    def inc(self, inst, k=1):
        inst.then_inc(self.h, k)
        self.n += k
        return inst


class DSem:

    def __init__(self, nc, es, name, nslots=2):
        self.s = [Sem(nc, es, f"{name}{i}") for i in range(nslots)]
        self.nslots = nslots
        self.name = name

    def h(self, slot):
        return self.s[slot % self.nslots].h

    def inc(self, inst, slot, k=16):
        inst.then_inc(self.s[slot % self.nslots].h, k)
        self.s[slot % self.nslots].n += k
        return inst

    def totals(self):
        return [x.n for x in self.s]


def build_program(T):
    NT = T // 128
    NG = NT // 4
    nc = bass.Bass("TRN2", target_bir_lowering=False)

    x_d = nc.dram_tensor("x", [T, D], F32, kind="ExternalInput").ap()
    wqkv_d = nc.dram_tensor("wqkv", [D, E3], F32, kind="ExternalInput").ap()
    bqkv_d = nc.dram_tensor("bqkv", [1, E3], F32, kind="ExternalInput").ap()
    wout_d = nc.dram_tensor("wout", [D, D], F32, kind="ExternalInput").ap()
    bout_d = nc.dram_tensor("bout", [1, D], F32, kind="ExternalInput").ap()
    w1_d = nc.dram_tensor("w1", [H, D], F32, kind="ExternalInput").ap()
    b1f_d = nc.dram_tensor("b1f", [1, H], F32, kind="ExternalInput").ap()
    g1_d = nc.dram_tensor("g1", [1, D], F32, kind="ExternalInput").ap()
    b1l_d = nc.dram_tensor("b1l", [1, D], F32, kind="ExternalInput").ap()
    w2_d = nc.dram_tensor("w2", [D, H], F32, kind="ExternalInput").ap()
    g2_d = nc.dram_tensor("g2", [1, H], F32, kind="ExternalInput").ap()
    b2l_d = nc.dram_tensor("b2l", [1, H], F32, kind="ExternalInput").ap()
    b2_d = nc.dram_tensor("b2", [1, D], F32, kind="ExternalInput").ap()
    out_d = nc.dram_tensor("out", [T, D], F32, kind="ExternalOutput").ap()

    h1_d = nc.dram_tensor("h1scr", [T, D], BF16, kind="Internal").ap()
    qh_d = nc.dram_tensor("qhscr", [T, D], BF16, kind="Internal").ap()
    wq1_d = nc.dram_tensor("wq1scr", [H, D], BF16, kind="Internal").ap()
    wq2_d = nc.dram_tensor("wq2scr", [D, H], BF16, kind="Internal").ap()

    onesb_d = nc.inline_tensor(np.ones((128, 1), np.float32).astype(ml_dtypes.bfloat16))
    onesf_d = nc.inline_tensor(np.ones((128, 1), np.float32))
    onesrow_d = nc.inline_tensor(np.ones((1, 128), np.float32))
    ident_d = nc.inline_tensor(np.eye(128, dtype=np.float32).astype(ml_dtypes.bfloat16))

    es = ExitStack()

    def sbR(name, shape, dtype):
        return es.enter_context(nc.sbuf_tensor(name, shape, dtype, side="right"))

    woutp = sbR("woutp", [128, 8 * D], BF16)
    boutb = sbR("boutb", [128, D], F32)
    beta2b = sbR("beta2b", [128, D], F32)
    cs2b = sbR("cs2b", [128, D], F32)
    beta1c = sbR("beta1c", [128, 32], F32)
    g1c = sbR("g1c", [128, 8], F32)
    g2c = sbR("g2c", [128, 32], F32)
    b1lc = sbR("b1lc", [128, 8], BF16)
    b2lc = sbR("b2lc", [128, 32], BF16)
    b1fc = sbR("b1fc", [128, 32], F32)
    g1m1c = sbR("g1m1c", [128, 8], F32)
    g2m2c = sbR("g2m2c", [128, 32], F32)
    onesb = sbR("onesb", [128, 1], BF16)
    onesf = sbR("onesf", [128, 1], F32)
    onesrow = sbR("onesrow", [1, 128], F32)
    identb = sbR("identb", [128, 128], BF16)
    mcols = sbR("mcols", [128, 6], F32)
    mrow = sbR("mrow", [1, 2], F32)
    wpart = sbR("wpart", [128, 64], F32)
    kvT = sbR("kvT", [128, 8], F32)

    allsems = []

    def S(name):
        s = Sem(nc, es, name)
        allsems.append(s)
        return s

    ax = DSem(nc, es, "ax"); asq = S("asq"); avar = S("avar"); asd = S("asd")
    astat = S("astat"); ah1 = S("ah1")
    ah1w = DSem(nc, es, "ah1w"); ah1t = DSem(nc, es, "ah1t", nslots=4)
    aqkv = S("aqkv"); aevac = S("aevac")
    aqs = S("aqs"); aqr = S("aqr"); aqh = S("aqh"); aqw = DSem(nc, es, "aqw")
    aks = S("aks"); akr = S("akr"); akh = S("akh"); akv = S("akv")
    wqkvs = S("wqkvs"); wouts = S("wouts"); smld = S("smld")
    wld = DSem(nc, es, "wld", nslots=5)
    gcons = S("gcons"); dcons = S("dcons"); gmld = S("gmld")
    mpre = S("mpre"); mtot = S("mtot"); mval = S("mval"); mbc = S("mbc")
    m1r = S("m1r"); m2r = S("m2r")
    dvw = DSem(nc, es, "dvw", nslots=4); scq = S("scq"); bhlv = S("bhlv")
    abc = S("abc"); binit = S("binit")
    kvc = S("kvc"); kvw = S("kvw")
    bar = S("bar")
    w1t = S("w1t"); w2t = S("w2t")
    bb1 = S("bb1"); bb1ev = S("bb1ev"); w1s = S("w1s")
    bb2 = S("bb2"); bb2r = S("bb2r"); w2s = S("w2s")
    bcs = S("bcs"); bcsr = S("bcsr"); bpb = S("bpb"); bprep = S("bprep")
    bx = DSem(nc, es, "bx"); bqt = S("bqt")
    batt = S("batt"); bxp = S("bxp")
    bsqx = S("bsqx"); bvar = S("bvar"); bsd = S("bsd"); bst = S("bst")
    bz1 = S("bz1"); bzt = S("bzt"); bze = S("bze")
    by1 = S("by1"); bh4 = S("bh4"); bsq = S("bsq"); bsq4 = S("bsq4")
    bsumrd = S("bsumrd"); by2 = S("by2"); bsum = S("bsum")
    bvar2 = S("bvar2"); bsd2 = S("bsd2")
    bfin = S("bfin"); bow = S("bow")

    esA = ExitStack()

    def sbA(name, shape, dtype, **kw):
        return esA.enter_context(nc.sbuf_tensor(name, shape, dtype, **kw))

    wqkvp = sbA("wqkvp", [128, 8 * E3], BF16)
    bqkvb = sbA("bqkvb", [128, E3], F32)
    xbufA = sbA("xbufA", [128, 2 * D], F32)
    h1buf = sbA("h1buf", [128, 2 * D], BF16)
    h1T = sbA("h1T", [128, 2 * 4096], BF16, is_transpose_dest=True)
    qkvb = sbA("qkvb", [128, E3], F32)
    qhat = sbA("qhat", [128, 2 * D], BF16)
    khat = sbA("khat", [128, D], F32)
    kvtmp = sbA("kvtmp", [128, D], F32)
    kv_acc = sbA("kv_acc", [128, D], F32)
    wstg = sbA("wstg", [128, 5 * 1024], F32)
    qod = sbA("qod", [128, 2 * 1024], BF16)
    qog = sbA("qog", [128, 2 * 1024], BF16)
    qtd = sbA("qtd", [128, 2048], BF16)
    qtg = sbA("qtg", [128, 2048], BF16)
    sqscrA = sbA("sqscrA", [128, D], F32)
    stA = sbA("stA", [128, 32], F32)
    rowsA = sbA("rowsA", [1, E3], F32)
    boutrow = sbA("boutrow", [1, D], F32)

    psA = ExitStack()
    pq = [psA.enter_context(nc.psum_tensor(f"pqkv{i}", [128, 512], F32))
          for i in range(4)]
    pone = psA.enter_context(nc.psum_tensor("pone", [1, 8], F32))
    pmb = psA.enter_context(nc.psum_tensor("pmb", [128, 2], F32))
    pkv = psA.enter_context(nc.psum_tensor("pkv", [128, 8], F32))

    esA.close()
    psA.close()

    esB = ExitStack()

    def sbB(name, shape, dtype, **kw):
        return esB.enter_context(nc.sbuf_tensor(name, shape, dtype, **kw))

    w1p = sbB("w1p", [128, 8 * H], BF16, is_transpose_dest=True)
    w2p = sbB("w2p", [128, 32 * D], BF16, is_transpose_dest=True)
    xbufB = sbB("xbufB", [128, 2 * D], F32)
    xring = sbB("xring", [128, 3 * D], F32)
    qT = sbB("qT", [128, 1024], BF16, is_transpose_dest=True)
    z1buf = sbB("z1buf", [128, D], BF16)
    z1T = sbB("z1T", [128, 2048], BF16, is_transpose_dest=True)
    h4buf = sbB("h4buf", [128, 2 * H], BF16)
    h4sq = sbB("h4sq", [128, 4 * 128], BF16)
    outbuf = sbB("outbuf", [128, D], F32)
    stB = sbB("stB", [128, 32], F32)
    b2rowB = outbuf[0:1, 0:D]

    psB = ExitStack()
    patt = psB.enter_context(nc.psum_tensor("patt", [128, 1024], F32))
    psP = ExitStack()
    pcol = [psP.enter_context(nc.psum_tensor(f"pcol{i}", [128, 1], F32))
            for i in range(2)]
    prow = psP.enter_context(nc.psum_tensor("prow", [1, D], F32))
    pbc = [psP.enter_context(nc.psum_tensor(f"pbc{i}", [128, 512], F32))
           for i in range(2)]
    psP.close()
    py1 = [psB.enter_context(nc.psum_tensor(f"py1_{i}", [128, 256], F32))
           for i in range(2)]
    psumSQ = psB.enter_context(nc.psum_tensor("psumSQ", [128, 64], F32))
    pzT = psB.enter_context(nc.psum_tensor("pzT", [128, 512], F32))
    py2 = psB.enter_context(nc.psum_tensor("py2", [128, 1024], F32))

    wplan = ([("w1m", c) for c in range(NW1)] + [("w2m", c) for c in range(NW2)]
             + [("w1q", c) for c in range(NW1)] + [("w2q", c) for c in range(NW2)])
    cons_engine = []
    for kind, ch in wplan:
        if kind in ("w1m", "w2m"):
            cons_engine.append("dv")
        else:
            cons_engine.append("dv")
    gp_ord = {}
    dv_ord = {}
    ng = nd = 0
    for i, e in enumerate(cons_engine):
        if e == "gp":
            ng += 1
            gp_ord[i] = ng
        else:
            nd += 1
            dv_ord[i] = nd
    NLOADS = len(wplan)
    dvq_loads = [i for i, e in enumerate(cons_engine) if e == "dv"]
    NDV = len(dvq_loads)
    NGP_Q = sum(1 for i, e in enumerate(cons_engine)
                if e == "gp" and wplan[i][0] == "w2q")

    def w_src(kind, ch):
        if kind in ("w1m", "w1q"):
            return w1_d[ch * 128:(ch + 1) * 128, :]
        row, col = ch // 4, ch % 4
        return w2_d[row * 128:(row + 1) * 128, col * 1024:(col + 1) * 1024]

    def wslot(i):
        s = i % 5
        return wstg[:, s * 1024:(s + 1) * 1024]

    dvq_of_load = {}
    for i in dvq_loads:
        kind, ch = wplan[i]
        if kind in ("w1m", "w2m"):
            continue
        dvq_of_load[i] = len(dvq_of_load)
    dv_cons_of_load = {i: k + 1 for k, i in enumerate(dvq_loads)}

    sc_emit_tile = []

    def qring_a(kq):
        base = qod if kq % 4 < 2 else qtd
        s = kq % 2
        return base[:, s * 1024:(s + 1) * 1024]

    def qring_b(kq):
        base = qog if kq % 4 < 2 else qtg
        s = kq % 2
        return base[:, s * 1024:(s + 1) * 1024]

    with es, esB, psB, nc.Block() as block:

        @block.sync
        def _(sy):
            def h1t_tile(jt):
                gt = jt // 4
                st = jt % 4
                sy.wait_ge(ah1w.h(jt % 2), 16 * (jt // 2 + 1))
                if jt >= 4:
                    sy.wait_ge(ah1t.h(jt % 4), 128 * (jt // 4))
                if gt >= 2 and st == 0:
                    sy.wait_ge(aqkv.h, 24 * (gt - 1))
                for c in range(8):
                    ah1t.inc(sy.dma_start(
                        out=h1T[:, (gt % 2) * 4096 + c * 512 + st * 128:
                                (gt % 2) * 4096 + c * 512 + (st + 1) * 128],
                        in_=h1_d[jt * 128:(jt + 1) * 128, c * 128:(c + 1) * 128],
                        transpose=True), jt)

            for g in range(NG):
                for s in range(4):
                    j = 4 * g + s
                    if j >= 2:
                        sy.wait_ge(ah1.h, j - 1)
                    ax.inc(sy.dma_start(
                        out=xbufA[:, (j % 2) * D:(j % 2 + 1) * D],
                        in_=x_d[j * 128:(j + 1) * 128, :]), j)
                    if j - 2 >= 0:
                        h1t_tile(j - 2)
            for jt in (4 * NG - 2, 4 * NG - 1):
                h1t_tile(jt)
            sy.wait_ge(ax.h(0), ax.totals()[0])
            sy.wait_ge(ax.h(1), ax.totals()[1])
            for sl in range(4):
                sy.wait_ge(ah1t.h(sl), ah1t.totals()[sl])
            bar.inc(sy.nop())
            sy.wait_ge(bar.h, 5)
            for sl in range(4):
                sy.wait_ge(dvw.h(sl), 16 * 8)
            for c in range(8):
                w1t.inc(sy.dma_start(out=w1p[:, c * H:(c + 1) * H],
                                     in_=wq1_d[:, c * 128:(c + 1) * 128],
                                     transpose=True), 16)
            for sl in range(4):
                sy.wait_ge(dvw.h(sl), 16 * 16)
            for r in range(32):
                w2t.inc(sy.dma_start(out=w2p[:, r * D:(r + 1) * D],
                                     in_=wq2_d[:, r * 128:(r + 1) * 128],
                                     transpose=True), 16)
            for j in range(NT):
                if j >= 1:
                    sy.wait_ge(batt.h, 2 * j)
                for c in range(8):
                    bqt.inc(sy.dma_start(
                        out=qT[:, c * 128:(c + 1) * 128],
                        in_=qh_d[j * 128:(j + 1) * 128, c * 128:(c + 1) * 128],
                        transpose=True), 16)
                if j >= 2:
                    sy.wait_ge(bsqx.h, j - 1)
                bx.inc(sy.dma_start(out=xbufB[:, (j % 2) * D:(j % 2 + 1) * D],
                                    in_=x_d[j * 128:(j + 1) * 128, :]), j)
            sy.wait_ge(bow.h, 16 * NT)

        @block.scalar
        def _(sc):
            smld.inc(sc.dma_start(out=rowsA[:, :], in_=bqkv_d), 16)
            smld.inc(sc.dma_start(out=boutrow[:, :], in_=bout_d), 16)
            smld.inc(sc.dma_start(out=onesb[:, :], in_=onesb_d.ap()), 16)
            smld.inc(sc.dma_start(out=onesf[:, :], in_=onesf_d.ap()), 16)
            smld.inc(sc.dma_start(out=onesrow[:, :], in_=onesrow_d.ap()), 16)
            smld.inc(sc.dma_start(out=identb[:, :], in_=ident_d.ap()), 16)


            def scA_front(j):
                P = (j % 2) * 16
                xs = xbufA[:, (j % 2) * D:(j % 2 + 1) * D]
                sc.wait_ge(ax.h(j % 2), 16 * (j // 2 + 1))
                asq.inc(sc.activation(out=sqscrA[:, :], in_=xs, func=AF.Square,
                                      accum_out=stA[:, P + 1:P + 2]))
                sc.wait_ge(avar.h, j + 1)
                asd.inc(sc.activation(out=stA[:, P + 5:P + 6],
                                      in_=stA[:, P + 4:P + 5], func=AF.Sqrt))
                sc.wait_ge(astat.h, j + 1)
                if j >= 2:
                    sc.wait_ge(ah1w.h(j % 2), 16 * (j // 2))
                ah1.inc(sc.activation(out=h1buf[:, (j % 2) * D:(j % 2 + 1) * D],
                                      in_=xs, func=AF.Identity,
                                      scale=stA[:, P + 6:P + 7],
                                      bias=stA[:, P + 7:P + 8]))
                sc.wait_ge(ah1.h, j + 1)
                ah1w.inc(sc.dma_start(out=h1_d[j * 128:(j + 1) * 128, :],
                                      in_=h1buf[:, (j % 2) * D:(j % 2 + 1) * D]), j)

            def scA_back(u):
                P = (u % 2) * 16
                sc.wait_ge(aevac.h, 6 * u + 2)
                sc.activation(out=sqscrA[:, :], in_=qkvb[:, 0:D], func=AF.Square,
                              accum_out=stA[:, P + 8:P + 9])
                sc.drain()
                aqs.inc(sc.activation(out=stA[:, P + 9:P + 10],
                                      in_=stA[:, P + 8:P + 9], func=AF.Sqrt))
                sc.wait_ge(aevac.h, 6 * u + 4)
                sc.activation(out=sqscrA[:, :], in_=qkvb[:, D:2 * D],
                              func=AF.Square, accum_out=stA[:, P + 10:P + 11])
                sc.drain()
                aks.inc(sc.activation(out=stA[:, P + 11:P + 12],
                                      in_=stA[:, P + 10:P + 11], func=AF.Sqrt))
                sc.wait_ge(aqr.h, u + 1)
                if u >= 2:
                    sc.wait_ge(aqw.h(u % 2), 16 * (u // 2))
                aqh.inc(sc.activation(out=qhat[:, (u % 2) * D:(u % 2 + 1) * D],
                                      in_=qkvb[:, 0:D], func=AF.Identity,
                                      scale=stA[:, P + 12:P + 13]))
                sc.wait_ge(aqh.h, u + 1)
                aqw.inc(sc.dma_start(out=qh_d[u * 128:(u + 1) * 128, :],
                                     in_=qhat[:, (u % 2) * D:(u % 2 + 1) * D]), u)
                sc.wait_ge(akr.h, u + 1)
                if u >= 1:
                    sc.wait_ge(akv.h, u)
                akh.inc(sc.activation(out=khat[:, :], in_=qkvb[:, D:2 * D],
                                      func=AF.Identity, scale=stA[:, P + 13:P + 14]))

            scq_pos = [0]
            quant_loads = sorted(dvq_of_load)
            first_w2q = [True]

            def sc_consume(n, u):
                for _ in range(n):
                    k0 = scq_pos[0]
                    if k0 >= len(quant_loads):
                        return
                    i = quant_loads[k0]
                    kind, ch = wplan[i]
                    if kind == "w1q" and u < min(10, NT - 1):
                        return
                    if kind == "w2q" and u < min(18, NT - 1):
                        return
                    scq_pos[0] = k0 + 1
                    kq = k0
                    sc_emit_tile.append(u)
                    if kq == 0:
                        sc.wait_ge(m1r.h, 1)
                    if kind == "w2q" and first_w2q[0]:
                        sc.wait_ge(m2r.h, 1)
                        first_w2q[0] = False
                    sc.wait_ge(wld.h(i % 5), 16 * (i // 5 + 1))
                    if kq >= 4:
                        sc.wait_ge(dvw.h(kq - 4), 16 * ((kq - 4) // 4 + 1))
                    scol = mcols[:, 1:2] if kind == "w1q" else mcols[:, 3:4]
                    abuf = qring_a(kq)
                    bbuf = qring_b(kq)
                    sc.activation(out=abuf, in_=wslot(i), func=AF.Sign,
                                  scale=scol, bias=mcols[:, 5:6])
                    scq.inc(sc.activation(out=bbuf, in_=wslot(i), func=AF.Sign,
                                          scale=scol, bias=mcols[:, 4:5]))

            for j in range(NT):
                scA_front(j)
                if j >= 3:
                    scA_back(j - 3)
                    sc_consume(4, j - 3)
            for u in range(max(0, NT - 3), NT):
                scA_back(u)
                sc_consume(4, u)
            sc_consume(10 ** 9, NT + 99)
            sc.wait_ge(ah1w.h(0), ah1w.totals()[0])
            sc.wait_ge(ah1w.h(1), ah1w.totals()[1])
            sc.wait_ge(aqw.h(0), aqw.totals()[0])
            sc.wait_ge(aqw.h(1), aqw.totals()[1])
            sc.wait_ge(smld.h, 16 * 6)
            bar.inc(sc.drain())
            sc.wait_ge(bar.h, 5)
            smld.inc(sc.dma_start(out=b2rowB, in_=b2_d), 16)

            def sc_front(t):
                P = (t % 2) * 16
                xr = xring[:, (t % 3) * D:(t % 3 + 1) * D]
                xs = xbufB[:, (t % 2) * D:(t % 2 + 1) * D]
                sc.wait_ge(bxp.h, t + 1)
                sc.activation(out=xs[:, 0:512], in_=xr[:, 0:512], func=AF.Square,
                              accum_out=stB[:, P + 1:P + 2])
                sc.activation(out=xs[:, 512:1024], in_=xr[:, 512:1024],
                              func=AF.Square, accum_out=stB[:, P + 14:P + 15])
                sc.drain()
                bsqx.inc(sc.activation(out=stB[:, P + 1:P + 2],
                                       in_=stB[:, P + 14:P + 15],
                                       func=AF.Identity,
                                       bias=stB[:, P + 1:P + 2]))
                sc.wait_ge(bvar.h, t + 1)
                bsd.inc(sc.activation(out=stB[:, P + 5:P + 6],
                                      in_=stB[:, P + 4:P + 5], func=AF.Sqrt))
                sc.wait_ge(bst.h, t + 1)
                if t >= 1:
                    sc.wait_ge(bzt.h, 2 * t)
                bz1.inc(sc.activation(out=z1buf[:, :], in_=xr, func=AF.Identity,
                                      scale=stB[:, P + 6:P + 7],
                                      bias=stB[:, P + 7:P + 8]))

            def sc_tail(j):
                P = (j % 2) * 16
                sc.wait_ge(bvar2.h, j + 1)
                bsd2.inc(sc.activation(out=stB[:, P + 10:P + 11],
                                       in_=stB[:, P + 9:P + 10], func=AF.Sqrt))
                sc.wait_ge(bfin.h, j + 1)
                bow.inc(sc.dma_start(out=out_d[j * 128:(j + 1) * 128, :],
                                     in_=outbuf[:, :]), 16)

            sc_front(0)
            sc_front(1)
            for p in range(NT // 2):
                j0, j1 = 2 * p, 2 * p + 1
                if j0 + 2 < NT:
                    sc_front(j0 + 2)
                if p >= 1:
                    sc.wait_ge(bsq4.h, 32 * j0)
                for m in range(32):
                    sc.wait_ge(by1.h, 32 * p + m + 1)
                    bh4.inc(sc.activation(
                        out=h4buf[:, m * 256:(m + 1) * 256],
                        in_=py1[m % 2][:, :], func=AF.Gelu,
                        bias=beta1c[:, m:m + 1]))
                sc_tail(j0)
                if j1 + 2 < NT:
                    sc_front(j1 + 2)
                sc_tail(j1)
            sc.wait_ge(bow.h, 16 * NT)

        @block.gpsimd
        def _(gp):
            for c in range(8):
                wqkvs.inc(gp.dma_start(out=wqkvp[:, c * E3:(c + 1) * E3],
                                       in_=wqkv_d[c * 128:(c + 1) * 128, :]), 16)
            for c in range(8):
                wouts.inc(gp.dma_start(out=woutp[:, c * D:(c + 1) * D],
                                       in_=wout_d[c * 128:(c + 1) * 128, :]), 16)
            with nc.allow_non_contiguous_dma("tiny columnize loads"):
                gmld.inc(gp.dma_start(
                    out=g1c[:, :], in_=g1_d.rearrange("o (c p) -> (o p) c", p=128)), 16)
                gmld.inc(gp.dma_start(
                    out=b1lc[:, :], in_=b1l_d.rearrange("o (c p) -> (o p) c", p=128)), 16)
                gmld.inc(gp.dma_start(
                    out=g2c[:, :], in_=g2_d.rearrange("o (c p) -> (o p) c", p=128)), 16)
                gmld.inc(gp.dma_start(
                    out=b2lc[:, :], in_=b2l_d.rearrange("o (c p) -> (o p) c", p=128)), 16)
                gmld.inc(gp.dma_start(
                    out=b1fc[:, :], in_=b1f_d.rearrange("o (c p) -> (o p) c", p=128)), 16)
            def dv_write(ii):
                kq = dvq_of_load[ii]
                gp.wait_ge(dcons.h, dv_cons_of_load[ii])
                kk, cc = wplan[ii]
                srcb = qring_a(kq)
                if kk == "w1q":
                    dvw.inc(gp.dma_start(
                        out=wq1_d[cc * 128:(cc + 1) * 128, :], in_=srcb), kq)
                else:
                    row, col = cc // 4, cc % 4
                    dvw.inc(gp.dma_start(
                        out=wq2_d[row * 128:(row + 1) * 128,
                                  col * 1024:(col + 1) * 1024], in_=srcb), kq)

            for i, (kind, ch) in enumerate(wplan):
                if i >= 5 and cons_engine[i - 5] == "dv":
                    gp.wait_ge(dcons.h, dv_cons_of_load[i - 5])
                wld.inc(gp.dma_start(out=wslot(i), in_=w_src(kind, ch)), i)
                if i >= 2:
                    if (i - 2) in dvq_of_load:
                        dv_write(i - 2)
            for ii in (NLOADS - 2, NLOADS - 1):
                if ii in dvq_of_load:
                    dv_write(ii)
            gp.wait_ge(wqkvs.h, 128)
            gp.wait_ge(wouts.h, 128)
            gp.wait_ge(gmld.h, 16 * 5)
            for sl in range(5):
                gp.wait_ge(wld.h(sl), wld.totals()[sl])
            for sl in range(4):
                gp.wait_ge(dvw.h(sl), dvw.totals()[sl])
            bar.inc(gp.drain())
            gp.wait_ge(bar.h, 5)

        p_mtot1 = (min(1, NG - 1), 3)
        p_mbc1 = (min(2, NG - 1), 1) if NG >= 3 else p_mtot1
        p_mtot2 = (min(3, NG - 1), 3) if NG >= 4 else p_mbc1
        p_mbc2 = (min(4, NG - 1), 1) if NG >= 5 else p_mtot2

        @block.tensor
        def _(te):
            te.wait_ge(smld.h, 16 * 6)
            for e in range(6):
                if e >= 2:
                    te.wait_ge(binit.h, e - 1)
                abc.inc(te.matmul(out=pq[e % 2][:, :], lhsT=onesrow[:, :],
                                  rhs=rowsA[0:1, e * 512:(e + 1) * 512],
                                  start=True, stop=True))
            for hh in range(2):
                te.wait_ge(binit.h, 5 + hh)
                abc.inc(te.matmul(out=pq[hh][:, :], lhsT=onesrow[:, :],
                                  rhs=boutrow[0:1, hh * 512:(hh + 1) * 512],
                                  start=True, stop=True))
            te.wait_ge(binit.h, 8)
            te.wait_ge(wqkvs.h, 128)
            for g in range(NG):
                for s in range(4):
                    te.wait_ge(ah1t.h(s), 128 * (g + 1))
                    for e in range(6):
                        i = (4 * g + s) * 6 + e
                        if i >= 4:
                            te.wait_ge(aevac.h, i - 3)
                        for c in range(8):
                            mm = te.matmul(
                                out=pq[i % 4][:, :],
                                lhsT=h1T[:, (g % 2) * 4096 + c * 512 + s * 128:
                                         (g % 2) * 4096 + c * 512 + (s + 1) * 128],
                                rhs=wqkvp[:, c * E3 + e * 512:c * E3 + (e + 1) * 512],
                                start=(c == 0), stop=(c == 7))
                        aqkv.inc(mm)
                    if (g, s) == p_mtot1:
                        te.wait_ge(mpre.h, 1)
                        mtot.inc(te.matmul(out=pone[:, 0:1],
                                           lhsT=stA[:, 30:31], rhs=onesf[:, :],
                                           start=True, stop=True))
                    if (g, s) == p_mbc1:
                        te.wait_ge(mval.h, 1)
                        mbc.inc(te.matmul(out=pmb[:, 0:1], lhsT=onesrow[:, :],
                                          rhs=mrow[0:1, 0:1],
                                          start=True, stop=True))
                    if (g, s) == p_mtot2:
                        te.wait_ge(mpre.h, 2)
                        mtot.inc(te.matmul(out=pone[:, 1:2],
                                           lhsT=stA[:, 31:32], rhs=onesf[:, :],
                                           start=True, stop=True))
                    if (g, s) == p_mbc2:
                        te.wait_ge(mval.h, 2)
                        mbc.inc(te.matmul(out=pmb[:, 1:2], lhsT=onesrow[:, :],
                                          rhs=mrow[0:1, 1:2],
                                          start=True, stop=True))
            te.wait_ge(akv.h, NT)
            te.wait_ge(aevac.h, 6 * NT)
            for c in range(8):
                kvc.inc(te.matmul(out=pkv[:, c:c + 1],
                                  lhsT=kv_acc[:, c * 128:(c + 1) * 128],
                                  rhs=onesf[:, :], start=True, stop=True))
            bar.inc(te.drain())
            te.wait_ge(bar.h, 5)
            te.wait_ge(bhlv.h, 1)
            te.wait_ge(w1t.h, 128)
            for m in range(32):
                if m >= 2:
                    te.wait_ge(bb1ev.h, m - 1)
                for c in range(8):
                    mm = te.matmul(out=pcol[m % 2][:, :],
                                   lhsT=w1p[:, c * H + m * 128:c * H + (m + 1) * 128],
                                   rhs=b1lc[:, c:c + 1],
                                   start=(c == 0), stop=(c == 7))
                bb1.inc(mm)
            te.wait_ge(w2t.h, 16 * 32)
            for r in range(32):
                for hh in range(2):
                    mm = te.matmul(out=prow[0:1, hh * 512:(hh + 1) * 512],
                                   lhsT=b2lc[:, r:r + 1],
                                   rhs=w2p[:, r * D + hh * 512:r * D + (hh + 1) * 512],
                                   start=(r == 0), stop=(r == 31))
            bb2.inc(mm)
            te.wait_ge(bb2r.h, 1)
            for hh in range(2):
                bpb.inc(te.matmul(out=pbc[hh][:, :], lhsT=onesrow[:, :],
                                  rhs=outbuf[0:1, hh * 512:(hh + 1) * 512],
                                  start=True, stop=True))
            te.wait_ge(w2s.h, 1)
            for r in range(32):
                for hh in range(2):
                    mm = te.matmul(out=prow[0:1, hh * 512:(hh + 1) * 512],
                                   lhsT=onesb[:, :],
                                   rhs=w2p[:, r * D + hh * 512:r * D + (hh + 1) * 512],
                                   start=(r == 0), stop=(r == 31))
            bcs.inc(mm)
            te.wait_ge(bcsr.h, 1)
            for hh in range(2):
                bpb.inc(te.matmul(out=pbc[hh][:, :], lhsT=onesrow[:, :],
                                  rhs=outbuf[0:1, hh * 512:(hh + 1) * 512],
                                  start=True, stop=True))
            def attn_mm(t):
                te.wait_ge(bqt.h, 128 * (t + 1))
                te.wait_ge(kvw.h, 1)
                if t >= 1:
                    te.wait_ge(bxp.h, t)
                for hh in range(2):
                    for c in range(8):
                        mm = te.matmul(
                            out=patt[:, hh * 512:(hh + 1) * 512],
                            lhsT=qT[:, c * 128:(c + 1) * 128],
                            rhs=woutp[:, c * D + hh * 512:c * D + (hh + 1) * 512],
                            start=(c == 0), stop=(c == 7))
                    batt.inc(mm)

            def y2_tile(j, zt_t=None):
                p = j // 2
                te.wait_ge(bprep.h, 1)
                if j >= 1:
                    te.wait_ge(bfin.h, j)
                    te.wait_ge(bsumrd.h, j)
                for m in range(32):
                    if m == 28 and zt_t is not None:
                        zT_round(zt_t, 0)
                    q = 32 * j + m
                    te.wait_ge(bh4.h, 32 * p + m + 1)
                    h4m = h4buf[:, m * 256 + (j % 2) * 128:m * 256 + (j % 2) * 128 + 128]
                    for hh in range(2):
                        mm = te.matmul(out=py2[:, hh * 512:(hh + 1) * 512],
                                       lhsT=h4m,
                                       rhs=w2p[:, m * D + hh * 512:m * D + (hh + 1) * 512],
                                       start=(m == 0), stop=(m == 31))
                        if hh == 1 and m == 31:
                            by2.inc(mm)
                    te.matmul(out=psumSQ[:, m:m + 1], lhsT=h4m, rhs=onesb[:, :],
                              start=True, stop=True)
                    te.wait_ge(bsq.h, q + 1)
                    bsq4.inc(te.matmul(out=psumSQ[:, 32 + m:33 + m],
                                       lhsT=h4sq[:, (m % 4) * 128:(m % 4 + 1) * 128],
                                       rhs=onesb[:, :], start=True, stop=True))

            def zT_round(t, r):
                g = 2 * t + r
                if r == 0:
                    te.wait_ge(bz1.h, t + 1)
                if g >= 1:
                    te.wait_ge(bze.h, g)
                for cc in range(4):
                    c = 4 * r + cc
                    mm = te.matmul(out=pzT[:, cc * 128:(cc + 1) * 128],
                                   lhsT=z1buf[:, c * 128:(c + 1) * 128],
                                   rhs=identb[:, :], start=True, stop=True)
                bzt.inc(mm)

            attn_mm(0)
            attn_mm(1)
            for t in (0, 1):
                zT_round(t, 0)
                zT_round(t, 1)
            for p in range(NT // 2):
                j0, j1 = 2 * p, 2 * p + 1
                if j0 + 2 < NT:
                    attn_mm(j0 + 2)
                te.wait_ge(bze.h, 4 * p + 4)
                te.wait_ge(w1s.h, 1)
                for m in range(32):
                    if 32 * p + m >= 2:
                        te.wait_ge(bh4.h, 32 * p + m - 1)
                    for c in range(8):
                        mm = te.matmul(
                            out=py1[m % 2][:, :],
                            lhsT=w1p[:, c * H + m * 128:c * H + (m + 1) * 128],
                            rhs=z1T[:, c * 256:(c + 1) * 256],
                            start=(c == 0), stop=(c == 7))
                    by1.inc(mm)
                if j1 + 2 < NT:
                    attn_mm(j1 + 2)
                if j0 + 2 < NT:
                    zT_round(j0 + 2, 0)
                    zT_round(j0 + 2, 1)
                y2_tile(j0)
                y2_tile(j1, zt_t=(j1 + 2 if j1 + 2 < NT else None))
                if j1 + 2 < NT:
                    zT_round(j1 + 2, 1)

        @block.vector
        def _(dv):
            dv.memset(kv_acc[:, :], 0.0)
            dv.memset(mcols[:, 4:5], 0.5)
            dv.memset(mcols[:, 5:6], -0.5)
            dv.wait_ge(gmld.h, 16 * 5)
            dv.tensor_scalar(out=b1lc[:, :], in0=b1lc[:, :], scalar1=0.5,
                             scalar2=None, op0=ALU.mult)
            bhlv.inc(dv.tensor_scalar(out=b2lc[:, :], in0=b2lc[:, :],
                                      scalar1=0.5, scalar2=None, op0=ALU.mult))
            for e in range(6):
                dv.wait_ge(abc.h, e + 1)
                binit.inc(dv.tensor_copy(out=bqkvb[:, e * 512:(e + 1) * 512],
                                         in_=pq[e % 2][:, :]))
            for hh in range(2):
                dv.wait_ge(abc.h, 7 + hh)
                binit.inc(dv.tensor_copy(out=boutb[:, hh * 512:(hh + 1) * 512],
                                         in_=pq[hh][:, :]))
            dvq_pos = [0]

            def dv_consume(n, jcur, limit=None, max_i=None):
                for _ in range(n):
                    k0 = dvq_pos[0]
                    if k0 >= NDV or (limit is not None and k0 >= limit):
                        return
                    if max_i is not None and dvq_loads[k0] >= max_i:
                        return
                    i0 = dvq_loads[k0]
                    kq0 = dvq_of_load.get(i0)
                    if kq0 is not None and (kq0 >= len(sc_emit_tile)
                                            or sc_emit_tile[kq0] > jcur - 3):
                        return
                    dvq_pos[0] = k0 + 1
                    i = dvq_loads[k0]
                    kind, ch = wplan[i]
                    if kind in ("w1m", "w2m"):
                        dv.wait_ge(wld.h(i % 5), 16 * (i // 5 + 1))
                        col = ch if kind == "w1m" else NW1 + ch
                        dcons.inc(dv.tensor_reduce(
                            out=wpart[:, col:col + 1], in_=wslot(i),
                            axis=AX.X, op=ALU.add, apply_absolute_value=True))
                        continue
                    kq = dvq_of_load[i]
                    dst = qring_a(kq)
                    srcb = qring_b(kq)
                    dv.wait_ge(scq.h, kq + 1)
                    dcons.inc(dv.tensor_tensor(out=dst, in0=dst, in1=srcb,
                                               op=ALU.add))

            def dvA_front(j):
                P = (j % 2) * 16
                xs = xbufA[:, (j % 2) * D:(j % 2 + 1) * D]
                dv.wait_ge(ax.h(j % 2), 16 * (j // 2 + 1))
                if j >= 2:
                    dv.wait_ge(ah1.h, j - 1)
                    dv.wait_ge(asd.h, j - 1)
                dv.reduce_sum(out=stA[:, P + 0:P + 1], in_=xs, axis=AX.X,
                              negate=True)
                dv.drain()
                dv.tensor_scalar(out=stA[:, P + 2:P + 3], in0=stA[:, P + 0:P + 1],
                                 scalar1=1.0 / D, scalar2=None, op0=ALU.mult)
                dv.wait_ge(asq.h, j + 1)
                dv.tensor_scalar(out=stA[:, P + 3:P + 4], in0=stA[:, P + 1:P + 2],
                                 scalar1=1.0 / D, scalar2=EPS,
                                 op0=ALU.mult, op1=ALU.add)
                dv.drain()
                dv.tensor_tensor(out=stA[:, P + 4:P + 5], in0=stA[:, P + 2:P + 3],
                                 in1=stA[:, P + 2:P + 3], op=ALU.mult)
                dv.drain()
                avar.inc(dv.tensor_tensor(out=stA[:, P + 4:P + 5],
                                          in0=stA[:, P + 3:P + 4],
                                          in1=stA[:, P + 4:P + 5],
                                          op=ALU.subtract))
                dv.wait_ge(asd.h, j + 1)
                dv.reciprocal(out=stA[:, P + 6:P + 7], in_=stA[:, P + 5:P + 6])
                dv.drain()
                astat.inc(dv.tensor_tensor(out=stA[:, P + 7:P + 8],
                                           in0=stA[:, P + 2:P + 3],
                                           in1=stA[:, P + 6:P + 7], op=ALU.mult))
            def dvA_back(u):
                P = (u % 2) * 16
                for e in range(6):
                    i = u * 6 + e
                    dv.wait_ge(aqkv.h, i + 1)
                    if e == 0 and u >= 1:
                        dv.wait_ge(aqh.h, u)
                    if e == 2 and u >= 1:
                        dv.wait_ge(akh.h, u)
                    aevac.inc(dv.tensor_tensor(
                        out=qkvb[:, e * 512:(e + 1) * 512],
                        in0=pq[i % 4][:, :],
                        in1=bqkvb[:, e * 512:(e + 1) * 512], op=ALU.add))
                dv.wait_ge(aqs.h, u + 1)
                aqr.inc(dv.reciprocal(out=stA[:, P + 12:P + 13],
                                      in_=stA[:, P + 9:P + 10]))
                dv.wait_ge(aks.h, u + 1)
                akr.inc(dv.reciprocal(out=stA[:, P + 13:P + 14],
                                      in_=stA[:, P + 11:P + 12]))
                dv.wait_ge(akh.h, u + 1)
                dv.tensor_tensor(out=kvtmp[:, :], in0=khat[:, :],
                                 in1=qkvb[:, 2 * D:3 * D], op=ALU.mult)
                dv.drain()
                akv.inc(dv.tensor_tensor(out=kv_acc[:, :], in0=kv_acc[:, :],
                                         in1=kvtmp[:, :], op=ALU.add))
            def dvA_extra(u):
                if u == min(5, NT - 1):
                    assert dvq_pos[0] >= NW1 or NT < 32, dvq_pos[0]
                    dv.drain()
                    mpre.inc(dv.reduce_sum(out=stA[:, 30:31],
                                           in_=wpart[:, 0:NW1], axis=AX.X))
                if u == min(8, NT - 1):
                    dv.wait_ge(mtot.h, 1)
                    mval.inc(dv.tensor_scalar(out=mrow[0:1, 0:1],
                                              in0=pone[0:1, 0:1],
                                              scalar1=1.0 / (H * D),
                                              scalar2=EPS,
                                              op0=ALU.mult, op1=ALU.max))
                if u == min(10, NT - 1):
                    dv.wait_ge(mbc.h, 1)
                    dv.tensor_copy(out=mcols[:, 0:1], in_=pmb[:, 0:1])
                    dv.drain()
                    m1r.inc(dv.reciprocal(out=mcols[:, 1:2],
                                          in_=mcols[:, 0:1]))
                    dv.drain()
                if u == min(12, NT - 1):
                    assert dvq_pos[0] >= NW1 + NW2 or NT < 32, dvq_pos[0]
                    dv.drain()
                    mpre.inc(dv.reduce_sum(out=stA[:, 31:32],
                                           in_=wpart[:, NW1:NW1 + NW2],
                                           axis=AX.X))
                if u == min(16, NT - 1):
                    dv.wait_ge(mtot.h, 2)
                    mval.inc(dv.tensor_scalar(out=mrow[0:1, 1:2],
                                              in0=pone[0:1, 1:2],
                                              scalar1=1.0 / (H * D),
                                              scalar2=EPS,
                                              op0=ALU.mult, op1=ALU.max))
                if u == min(18, NT - 1):
                    dv.wait_ge(mbc.h, 2)
                    dv.tensor_copy(out=mcols[:, 2:3], in_=pmb[:, 1:2])
                    dv.drain()
                    m2r.inc(dv.reciprocal(out=mcols[:, 3:4],
                                          in_=mcols[:, 2:3]))
                    dv.drain()

            for j in range(NT):
                dvA_front(j)
                if j >= 3:
                    dvA_back(j - 3)
                    dvA_extra(j - 3)
                if NT >= 32:
                    dv_consume(6, j)
                else:
                    dv_consume(64 // NT + 1, j, limit=NW1 + NW2)
            for u in range(max(0, NT - 3), NT):
                dvA_back(u)
                dvA_extra(u)
            dv_consume(10 ** 9, NT + 999)
            dv.wait_ge(kvc.h, 8)
            dv.tensor_copy(out=kvT[:, :], in_=pkv[:, :])
            dv.drain()
            dv.wait_ge(wouts.h, 128)
            for c in range(8):
                dv.tensor_scalar(out=woutp[:, c * D:(c + 1) * D],
                                 in0=woutp[:, c * D:(c + 1) * D],
                                 scalar1=kvT[:, c:c + 1], scalar2=None,
                                 op0=ALU.mult)
            kvw.inc(dv.drain())
            bar.inc(dv.drain())
            dv.wait_ge(bar.h, 5)
            dv.tensor_scalar(out=g1m1c[:, :], in0=g1c[:, :],
                             scalar1=mcols[:, 0:1], scalar2=0.5, op0=ALU.mult,
                             op1=ALU.mult)
            dv.tensor_scalar(out=g2m2c[:, :], in0=g2c[:, :],
                             scalar1=mcols[:, 2:3], scalar2=0.5, op0=ALU.mult,
                             op1=ALU.mult)
            for m in range(32):
                dv.wait_ge(bb1.h, m + 1)
                bb1ev.inc(dv.scalar_tensor_tensor(
                    out=beta1c[:, m:m + 1], in0=pcol[m % 2][:, :],
                    scalar=mcols[:, 0:1], in1=b1fc[:, m:m + 1],
                    op0=ALU.mult, op1=ALU.add))
            dv.drain()
            for c in range(8):
                dv.tensor_scalar(out=w1p[:, c * H:(c + 1) * H],
                                 in0=w1p[:, c * H:(c + 1) * H],
                                 scalar1=g1m1c[:, c:c + 1], scalar2=None,
                                 op0=ALU.mult)
            w1s.inc(dv.drain())
            dv.wait_ge(bb2.h, 1)
            dv.wait_ge(smld.h, 16 * 7)
            bb2r.inc(dv.scalar_tensor_tensor(
                out=b2rowB, in0=prow[0:1, :], scalar=mcols[0:1, 2:3],
                in1=b2rowB, op0=ALU.mult, op1=ALU.add))
            dv.drain()
            for r in range(32):
                dv.tensor_scalar(out=w2p[:, r * D:(r + 1) * D],
                                 in0=w2p[:, r * D:(r + 1) * D],
                                 scalar1=g2m2c[:, r:r + 1], scalar2=None,
                                 op0=ALU.mult)
            w2s.inc(dv.drain())
            dv.wait_ge(bpb.h, 2)
            dv.tensor_copy(out=beta2b[:, 0:512], in_=pbc[0][:, :])
            dv.tensor_copy(out=beta2b[:, 512:1024], in_=pbc[1][:, :])
            dv.wait_ge(bcs.h, 1)
            bcsr.inc(dv.tensor_copy(out=b2rowB, in_=prow[0:1, :]))
            dv.wait_ge(bpb.h, 4)
            dv.tensor_copy(out=cs2b[:, 0:512], in_=pbc[0][:, :])
            bprep.inc(dv.tensor_copy(out=cs2b[:, 512:1024], in_=pbc[1][:, :]))
            def dv_front(t):
                P = (t % 2) * 16
                xs = xbufB[:, (t % 2) * D:(t % 2 + 1) * D]
                xr = xring[:, (t % 3) * D:(t % 3 + 1) * D]
                dv.wait_ge(bx.h(t % 2), 16 * (t // 2 + 1))
                dv.tensor_tensor(out=xs, in0=xs, in1=boutb[:, :], op=ALU.add)
                dv.drain()
                dv.wait_ge(batt.h, 2 * t + 2)
                if t >= 2:
                    dv.wait_ge(bz1.h, t - 1)
                    dv.wait_ge(bsd.h, t - 1)
                bxp.inc(dv.tensor_tensor(out=xr, in0=patt[:, :], in1=xs,
                                         op=ALU.add))
                dv.drain()
                dv.reduce_sum(out=stB[:, P + 0:P + 1], in_=xr, axis=AX.X,
                              negate=True)
                dv.drain()
                dv.tensor_scalar(out=stB[:, P + 2:P + 3], in0=stB[:, P + 0:P + 1],
                                 scalar1=1.0 / D, scalar2=None, op0=ALU.mult)
                dv.wait_ge(bsqx.h, t + 1)
                dv.tensor_scalar(out=stB[:, P + 3:P + 4], in0=stB[:, P + 1:P + 2],
                                 scalar1=(1.0 + EPS) / D, scalar2=EPS * EPS,
                                 op0=ALU.mult, op1=ALU.add)
                dv.drain()
                dv.tensor_tensor(out=stB[:, P + 4:P + 5], in0=stB[:, P + 2:P + 3],
                                 in1=stB[:, P + 2:P + 3], op=ALU.mult)
                dv.drain()
                bvar.inc(dv.scalar_tensor_tensor(
                    out=stB[:, P + 4:P + 5], in0=stB[:, P + 4:P + 5],
                    scalar=-(1.0 + EPS), in1=stB[:, P + 3:P + 4],
                    op0=ALU.mult, op1=ALU.add))
                dv.wait_ge(bsd.h, t + 1)
                dv.reciprocal(out=stB[:, P + 6:P + 7], in_=stB[:, P + 5:P + 6])
                dv.drain()
                bst.inc(dv.tensor_tensor(out=stB[:, P + 7:P + 8],
                                         in0=stB[:, P + 2:P + 3],
                                         in1=stB[:, P + 6:P + 7], op=ALU.mult))

            def dv_h4sq(j):
                p = j // 2
                if j >= 2:
                    dv.wait_ge(bsq4.h, 32 * (j - 1))
                for m in range(32):
                    q = 32 * j + m
                    dv.wait_ge(bh4.h, 32 * p + m + 1)
                    if q >= 4:
                        dv.wait_ge(bsq4.h, q - 3)
                    h4m = h4buf[:, m * 256 + (j % 2) * 128:m * 256 + (j % 2) * 128 + 128]
                    bsq.inc(dv.tensor_tensor(
                        out=h4sq[:, (m % 4) * 128:(m % 4 + 1) * 128],
                        in0=h4m, in1=h4m, op=ALU.mult))

            def dv_tail(j):
                P = (j % 2) * 16
                xr = xring[:, (j % 3) * D:(j % 3 + 1) * D]
                dv.wait_ge(bsq4.h, 32 * (j + 1))
                if j >= 2:
                    dv.wait_ge(bsd2.h, j - 1)
                dv.reduce_sum(out=stB[:, P + 8:P + 9], in_=psumSQ[:, 0:32],
                              axis=AX.X, negate=True)
                bsumrd.inc(dv.reduce_sum(out=stB[:, P + 9:P + 10],
                                         in_=psumSQ[:, 32:64], axis=AX.X))
                dv.drain()
                dv.tensor_scalar(out=stB[:, P + 8:P + 9],
                                 in0=stB[:, P + 8:P + 9],
                                 scalar1=1.0 / H, scalar2=None, op0=ALU.mult)
                dv.tensor_scalar(out=stB[:, P + 9:P + 10],
                                 in0=stB[:, P + 9:P + 10],
                                 scalar1=1.0 / H, scalar2=EPS,
                                 op0=ALU.mult, op1=ALU.add)
                dv.drain()
                dv.tensor_tensor(out=stB[:, P + 11:P + 12],
                                 in0=stB[:, P + 8:P + 9],
                                 in1=stB[:, P + 8:P + 9], op=ALU.mult)
                dv.drain()
                bvar2.inc(dv.tensor_tensor(out=stB[:, P + 9:P + 10],
                                           in0=stB[:, P + 9:P + 10],
                                           in1=stB[:, P + 11:P + 12],
                                           op=ALU.subtract))
                dv.wait_ge(bsd2.h, j + 1)
                dv.reciprocal(out=stB[:, P + 12:P + 13],
                              in_=stB[:, P + 10:P + 11])
                dv.drain()
                dv.tensor_tensor(out=stB[:, P + 13:P + 14],
                                 in0=stB[:, P + 8:P + 9],
                                 in1=stB[:, P + 12:P + 13], op=ALU.mult)
                dv.drain()
                dv.wait_ge(by2.h, j + 1)
                if j >= 1:
                    dv.wait_ge(bow.h, 16 * j)
                dv.scalar_tensor_tensor(out=outbuf[:, :], in0=py2[:, :],
                                        scalar=stB[:, P + 12:P + 13],
                                        in1=beta2b[:, :], op0=ALU.mult,
                                        op1=ALU.add)
                dv.scalar_tensor_tensor(out=xr, in0=cs2b[:, :],
                                        scalar=stB[:, P + 13:P + 14], in1=xr,
                                        op0=ALU.mult, op1=ALU.add)
                dv.drain()
                bfin.inc(dv.tensor_tensor(out=outbuf[:, :], in0=outbuf[:, :],
                                          in1=xr, op=ALU.add))

            def zT_evac(t, r):
                g = 2 * t + r
                dv.wait_ge(bzt.h, g + 1)
                for cc in range(4):
                    c = 4 * r + cc
                    cp = dv.tensor_copy(
                        out=z1T[:, c * 256 + (t % 2) * 128:c * 256 + (t % 2) * 128 + 128],
                        in_=pzT[:, cc * 128:(cc + 1) * 128])
                bze.inc(cp)

            dv_front(0)
            dv_front(1)
            for t in (0, 1):
                zT_evac(t, 0)
                zT_evac(t, 1)
            for p in range(NT // 2):
                j0, j1 = 2 * p, 2 * p + 1
                if j0 + 2 < NT:
                    dv_front(j0 + 2)
                    zT_evac(j0 + 2, 0)
                    zT_evac(j0 + 2, 1)
                dv_h4sq(j0)
                dv_tail(j0)
                if j1 + 2 < NT:
                    dv_front(j1 + 2)
                dv_h4sq(j1)
                if j1 + 2 < NT:
                    zT_evac(j1 + 2, 0)
                    zT_evac(j1 + 2, 1)
                dv_tail(j1)

    expect = {
        "asq": NT, "avar": NT, "asd": NT, "astat": NT, "ah1": NT,
        "aqkv": 6 * NT, "aevac": 6 * NT,
        "aqs": NT, "aqr": NT, "aqh": NT,
        "aks": NT, "akr": NT, "akh": NT, "akv": NT,
        "wqkvs": 128, "wouts": 128, "smld": 16 * 7,
        "gcons": ng, "dcons": nd, "gmld": 16 * 5, "scq": 64, "bhlv": 1,
        "mpre": 2, "mtot": 2, "mval": 2, "mbc": 2, "m1r": 1, "m2r": 1,
        "abc": 8, "binit": 8, "kvc": 8, "kvw": 1, "bar": 5,
        "w1t": 128, "w2t": 512,
        "bb1": 32, "bb1ev": 32, "w1s": 1, "bb2": 1, "bb2r": 1, "w2s": 1,
        "bcs": 1, "bcsr": 1, "bpb": 4, "bprep": 1,
        "batt": 2 * NT, "bxp": NT,
        "bsqx": NT, "bvar": NT, "bsd": NT, "bst": NT, "bz1": NT,
        "by1": 16 * NT, "bh4": 16 * NT, "bsq": 32 * NT, "bsq4": 32 * NT,
        "bsumrd": NT, "by2": NT, "bvar2": NT, "bsd2": NT,
        "bfin": NT, "bow": 16 * NT,
        "bzt": 2 * NT, "bze": 2 * NT,
    }
    dexpect = {"ax": 16 * NT, "ah1w": 16 * NT, "ah1t": 512 * NG,
               "aqw": 16 * NT, "wld": 16 * NLOADS,
               "dvw": 16 * (NW1 + NW2),
               "bx": 16 * NT}
    for ds, tot in [(ax, dexpect["ax"]), (ah1w, dexpect["ah1w"]),
                    (ah1t, dexpect["ah1t"]), (aqw, dexpect["aqw"]),
                    (wld, dexpect["wld"]), (dvw, dexpect["dvw"]),
                    (bx, dexpect["bx"])]:
        assert sum(ds.totals()) == tot, \
            f"dsem {ds.name}: {ds.totals()} != {tot}"
    for s in allsems:
        if s.name in expect:
            assert s.n == expect[s.name], \
                f"sem {s.name}: emitted {s.n}, expected {expect[s.name]}"
    return nc


def _shard_inputs(inputs):
    x = np.ascontiguousarray(np.asarray(inputs["L0_attn_x"], dtype=np.float32))
    common = {
        "wqkv": np.ascontiguousarray(np.asarray(inputs["L0_attn_Wqkv"], np.float32)),
        "bqkv": np.ascontiguousarray(np.asarray(inputs["L0_attn_bqkv"], np.float32).reshape(1, E3)),
        "wout": np.ascontiguousarray(np.asarray(inputs["L0_attn_Wout"], np.float32)),
        "bout": np.ascontiguousarray(np.asarray(inputs["L0_attn_bout"], np.float32).reshape(1, D)),
        "w1": np.ascontiguousarray(np.asarray(inputs["L0_ffn_w1"], np.float32)),
        "b1f": np.ascontiguousarray(np.asarray(inputs["L0_ffn_b1"], np.float32).reshape(1, H)),
        "g1": np.ascontiguousarray(np.asarray(inputs["L0_ffn_ln1_g"], np.float32).reshape(1, D)),
        "b1l": np.ascontiguousarray(np.asarray(inputs["L0_ffn_ln1_b"], np.float32).reshape(1, D)),
        "w2": np.ascontiguousarray(np.asarray(inputs["L0_ffn_w2"], np.float32)),
        "g2": np.ascontiguousarray(np.asarray(inputs["L0_ffn_ln2_g"], np.float32).reshape(1, H)),
        "b2l": np.ascontiguousarray(np.asarray(inputs["L0_ffn_ln2_b"], np.float32).reshape(1, H)),
        "b2": np.ascontiguousarray(np.asarray(inputs["L0_ffn_b2"], np.float32).reshape(1, D)),
    }
    per_core = []
    for b in range(x.shape[0]):
        m = {"x": np.ascontiguousarray(x[b])}
        m.update(common)
        per_core.append(m)
    return per_core


def kernel(**inputs):
    per_core = _shard_inputs(inputs)
    T = per_core[0]["x"].shape[0]
    nc = build_program(T)
    res = run_bass_kernel_spmd(nc, per_core, core_ids=list(range(len(per_core))))
    return np.stack([r["out"] for r in res.results], axis=0)


# version 17
# speedup vs baseline: 1.1254x; 1.0128x over previous
import numpy as np
import ml_dtypes
from contextlib import ExitStack

import concourse.bass as bass
import concourse.mybir as mybir
from concourse.bass_utils import run_bass_kernel_spmd


F32 = mybir.dt.float32
BF16 = mybir.dt.bfloat16
AF = mybir.ActivationFunctionType
ALU = mybir.AluOpType
AX = mybir.AxisListType

D = 1024
H = 4096
E3 = 3 * D
EPS = 1e-5
N_CORES = 8
NW1 = 32
NW2 = 32


class Sem:
    def __init__(self, nc, es, name):
        self.h = es.enter_context(nc.semaphore(name))
        self.name = name
        self.n = 0

    def inc(self, inst, k=1):
        inst.then_inc(self.h, k)
        self.n += k
        return inst


class DSem:

    def __init__(self, nc, es, name, nslots=2):
        self.s = [Sem(nc, es, f"{name}{i}") for i in range(nslots)]
        self.nslots = nslots
        self.name = name

    def h(self, slot):
        return self.s[slot % self.nslots].h

    def inc(self, inst, slot, k=16):
        inst.then_inc(self.s[slot % self.nslots].h, k)
        self.s[slot % self.nslots].n += k
        return inst

    def totals(self):
        return [x.n for x in self.s]


def build_program(T):
    NT = T // 128
    NG = NT // 4
    nc = bass.Bass("TRN2", target_bir_lowering=False)

    x_d = nc.dram_tensor("x", [T, D], F32, kind="ExternalInput").ap()
    wqkv_d = nc.dram_tensor("wqkv", [D, E3], F32, kind="ExternalInput").ap()
    bqkv_d = nc.dram_tensor("bqkv", [1, E3], F32, kind="ExternalInput").ap()
    wout_d = nc.dram_tensor("wout", [D, D], F32, kind="ExternalInput").ap()
    bout_d = nc.dram_tensor("bout", [1, D], F32, kind="ExternalInput").ap()
    w1_d = nc.dram_tensor("w1", [H, D], F32, kind="ExternalInput").ap()
    b1f_d = nc.dram_tensor("b1f", [1, H], F32, kind="ExternalInput").ap()
    g1_d = nc.dram_tensor("g1", [1, D], F32, kind="ExternalInput").ap()
    b1l_d = nc.dram_tensor("b1l", [1, D], F32, kind="ExternalInput").ap()
    w2_d = nc.dram_tensor("w2", [D, H], F32, kind="ExternalInput").ap()
    g2_d = nc.dram_tensor("g2", [1, H], F32, kind="ExternalInput").ap()
    b2l_d = nc.dram_tensor("b2l", [1, H], F32, kind="ExternalInput").ap()
    b2_d = nc.dram_tensor("b2", [1, D], F32, kind="ExternalInput").ap()
    out_d = nc.dram_tensor("out", [T, D], F32, kind="ExternalOutput").ap()

    h1_d = nc.dram_tensor("h1scr", [T, D], BF16, kind="Internal").ap()
    qh_d = nc.dram_tensor("qhscr", [T, D], BF16, kind="Internal").ap()
    wq1_d = nc.dram_tensor("wq1scr", [H, D], BF16, kind="Internal").ap()
    wq2_d = nc.dram_tensor("wq2scr", [D, H], BF16, kind="Internal").ap()

    onesb_d = nc.inline_tensor(np.ones((128, 1), np.float32).astype(ml_dtypes.bfloat16))
    onesf_d = nc.inline_tensor(np.ones((128, 1), np.float32))
    onesrow_d = nc.inline_tensor(np.ones((1, 128), np.float32))
    ident_d = nc.inline_tensor(np.eye(128, dtype=np.float32).astype(ml_dtypes.bfloat16))

    es = ExitStack()

    def sbR(name, shape, dtype):
        return es.enter_context(nc.sbuf_tensor(name, shape, dtype, side="right"))

    woutp = sbR("woutp", [128, 8 * D], BF16)
    boutb = sbR("boutb", [128, D], F32)
    beta2b = sbR("beta2b", [128, D], F32)
    cs2b = sbR("cs2b", [128, D], F32)
    beta1c = sbR("beta1c", [128, 32], F32)
    g1c = sbR("g1c", [128, 8], F32)
    g2c = sbR("g2c", [128, 32], F32)
    b1lc = sbR("b1lc", [128, 8], BF16)
    b2lc = sbR("b2lc", [128, 32], BF16)
    b1fc = sbR("b1fc", [128, 32], F32)
    g1m1c = sbR("g1m1c", [128, 8], F32)
    g2m2c = sbR("g2m2c", [128, 32], F32)
    onesb = sbR("onesb", [128, 1], BF16)
    onesf = sbR("onesf", [128, 1], F32)
    onesrow = sbR("onesrow", [1, 128], F32)
    identb = sbR("identb", [128, 128], BF16)
    mcols = sbR("mcols", [128, 6], F32)
    mrow = sbR("mrow", [1, 2], F32)
    wpart = sbR("wpart", [128, 64], F32)
    kvT = sbR("kvT", [128, 8], F32)

    allsems = []

    def S(name):
        s = Sem(nc, es, name)
        allsems.append(s)
        return s

    ax = DSem(nc, es, "ax"); asq = S("asq"); avar = S("avar"); asd = S("asd")
    astat = S("astat"); ah1 = S("ah1")
    ah1w = DSem(nc, es, "ah1w"); ah1t = DSem(nc, es, "ah1t", nslots=4)
    aqkv = S("aqkv"); aevac = S("aevac")
    aqs = S("aqs"); aqr = S("aqr"); aqh = S("aqh"); aqw = DSem(nc, es, "aqw")
    aks = S("aks"); akr = S("akr"); akh = S("akh"); akv = S("akv")
    wqkvs = S("wqkvs"); wouts = S("wouts"); smld = S("smld")
    wld = DSem(nc, es, "wld", nslots=5)
    gcons = S("gcons"); dcons = S("dcons"); gmld = S("gmld")
    mpre = S("mpre"); mtot = S("mtot"); mval = S("mval"); mbc = S("mbc")
    m1r = S("m1r"); m2r = S("m2r")
    dvw = DSem(nc, es, "dvw", nslots=4); scq = S("scq"); bhlv = S("bhlv")
    abc = S("abc"); binit = S("binit")
    kvc = S("kvc"); kvw = S("kvw")
    bar = S("bar")
    w1t = S("w1t"); w2t = S("w2t")
    bb1 = S("bb1"); bb1ev = S("bb1ev"); w1s = S("w1s")
    bb2 = S("bb2"); bb2r = S("bb2r"); w2s = S("w2s")
    bcs = S("bcs"); bcsr = S("bcsr"); bpb = S("bpb"); bprep = S("bprep")
    bx = DSem(nc, es, "bx"); bqt = S("bqt")
    batt = S("batt"); bxp = S("bxp")
    bsqx = S("bsqx"); bvar = S("bvar"); bsd = S("bsd"); bst = S("bst")
    bz1 = S("bz1"); bzt = S("bzt"); bze = S("bze")
    by1 = S("by1"); bh4 = S("bh4"); bsq = S("bsq"); bsq4 = S("bsq4")
    bsumrd = S("bsumrd"); by2 = S("by2"); bsum = S("bsum")
    bvar2 = S("bvar2"); bsd2 = S("bsd2")
    bfin = S("bfin"); bow = S("bow")

    esA = ExitStack()

    def sbA(name, shape, dtype, **kw):
        return esA.enter_context(nc.sbuf_tensor(name, shape, dtype, **kw))

    wqkvp = sbA("wqkvp", [128, 8 * E3], BF16)
    bqkvb = sbA("bqkvb", [128, E3], F32)
    xbufA = sbA("xbufA", [128, 2 * D], F32)
    h1buf = sbA("h1buf", [128, 2 * D], BF16)
    h1T = sbA("h1T", [128, 2 * 4096], BF16, is_transpose_dest=True)
    qkvb = sbA("qkvb", [128, E3], F32)
    qhat = sbA("qhat", [128, 2 * D], BF16)
    khat = sbA("khat", [128, D], F32)
    kvtmp = sbA("kvtmp", [128, D], F32)
    kv_acc = sbA("kv_acc", [128, D], F32)
    wstg = sbA("wstg", [128, 5 * 1024], F32)
    qod = sbA("qod", [128, 2 * 1024], BF16)
    qog = sbA("qog", [128, 2 * 1024], BF16)
    qtd = sbA("qtd", [128, 2048], BF16)
    qtg = sbA("qtg", [128, 2048], BF16)
    sqscrA = sbA("sqscrA", [128, D], F32)
    stA = sbA("stA", [128, 32], F32)
    rowsA = sbA("rowsA", [1, E3], F32)
    boutrow = sbA("boutrow", [1, D], F32)

    psA = ExitStack()
    pq = [psA.enter_context(nc.psum_tensor(f"pqkv{i}", [128, 512], F32))
          for i in range(4)]
    pone = psA.enter_context(nc.psum_tensor("pone", [1, 8], F32))
    pmb = psA.enter_context(nc.psum_tensor("pmb", [128, 2], F32))
    pkv = psA.enter_context(nc.psum_tensor("pkv", [128, 8], F32))

    esA.close()
    psA.close()

    esB = ExitStack()

    def sbB(name, shape, dtype, **kw):
        return esB.enter_context(nc.sbuf_tensor(name, shape, dtype, **kw))

    w1p = sbB("w1p", [128, 8 * H], BF16, is_transpose_dest=True)
    w2p = sbB("w2p", [128, 32 * D], BF16, is_transpose_dest=True)
    xbufB = sbB("xbufB", [128, 2 * D], F32)
    xring = sbB("xring", [128, 3 * D], F32)
    qT = sbB("qT", [128, 1024], BF16, is_transpose_dest=True)
    z1buf = sbB("z1buf", [128, D], BF16)
    z1T = sbB("z1T", [128, 2048], BF16, is_transpose_dest=True)
    h4buf = sbB("h4buf", [128, 2 * H], BF16)
    h4sq = sbB("h4sq", [128, 4 * 128], BF16)
    outbuf = sbB("outbuf", [128, D], F32)
    stB = sbB("stB", [128, 32], F32)
    b2rowB = outbuf[0:1, 0:D]

    psB = ExitStack()
    patt = psB.enter_context(nc.psum_tensor("patt", [128, 1024], F32))
    psP = ExitStack()
    pcol = [psP.enter_context(nc.psum_tensor(f"pcol{i}", [128, 1], F32))
            for i in range(2)]
    prow = psP.enter_context(nc.psum_tensor("prow", [1, D], F32))
    pbc = [psP.enter_context(nc.psum_tensor(f"pbc{i}", [128, 512], F32))
           for i in range(2)]
    psP.close()
    py1 = [psB.enter_context(nc.psum_tensor(f"py1_{i}", [128, 256], F32))
           for i in range(2)]
    psumSQ = psB.enter_context(nc.psum_tensor("psumSQ", [128, 64], F32))
    pzT = psB.enter_context(nc.psum_tensor("pzT", [128, 512], F32))
    py2 = psB.enter_context(nc.psum_tensor("py2", [128, 1024], F32))

    wplan = ([("w1m", c) for c in range(NW1)] + [("w2m", c) for c in range(NW2)]
             + [("w1q", c) for c in range(NW1)] + [("w2q", c) for c in range(NW2)])
    cons_engine = []
    for kind, ch in wplan:
        if kind in ("w1m", "w2m"):
            cons_engine.append("dv")
        else:
            cons_engine.append("dv")
    gp_ord = {}
    dv_ord = {}
    ng = nd = 0
    for i, e in enumerate(cons_engine):
        if e == "gp":
            ng += 1
            gp_ord[i] = ng
        else:
            nd += 1
            dv_ord[i] = nd
    NLOADS = len(wplan)
    dvq_loads = [i for i, e in enumerate(cons_engine) if e == "dv"]
    NDV = len(dvq_loads)
    NGP_Q = sum(1 for i, e in enumerate(cons_engine)
                if e == "gp" and wplan[i][0] == "w2q")

    def w_src(kind, ch):
        if kind in ("w1m", "w1q"):
            return w1_d[ch * 128:(ch + 1) * 128, :]
        row, col = ch // 4, ch % 4
        return w2_d[row * 128:(row + 1) * 128, col * 1024:(col + 1) * 1024]

    def wslot(i):
        s = i % 5
        return wstg[:, s * 1024:(s + 1) * 1024]

    dvq_of_load = {}
    for i in dvq_loads:
        kind, ch = wplan[i]
        if kind in ("w1m", "w2m"):
            continue
        dvq_of_load[i] = len(dvq_of_load)
    dv_cons_of_load = {i: k + 1 for k, i in enumerate(dvq_loads)}

    sc_emit_tile = []

    def qring_a(kq):
        base = qod if kq % 4 < 2 else qtd
        s = kq % 2
        return base[:, s * 1024:(s + 1) * 1024]

    def qring_b(kq):
        base = qog if kq % 4 < 2 else qtg
        s = kq % 2
        return base[:, s * 1024:(s + 1) * 1024]

    with es, esB, psB, nc.Block() as block:

        @block.sync
        def _(sy):
            def h1t_tile(jt):
                gt = jt // 4
                st = jt % 4
                sy.wait_ge(ah1w.h(jt % 2), 16 * (jt // 2 + 1))
                if jt >= 4:
                    sy.wait_ge(ah1t.h(jt % 4), 128 * (jt // 4))
                if gt >= 2 and st == 0:
                    sy.wait_ge(aqkv.h, 24 * (gt - 1))
                for c in range(8):
                    ah1t.inc(sy.dma_start(
                        out=h1T[:, (gt % 2) * 4096 + c * 512 + st * 128:
                                (gt % 2) * 4096 + c * 512 + (st + 1) * 128],
                        in_=h1_d[jt * 128:(jt + 1) * 128, c * 128:(c + 1) * 128],
                        transpose=True), jt)

            for g in range(NG):
                for s in range(4):
                    j = 4 * g + s
                    if j >= 2:
                        sy.wait_ge(ah1.h, j - 1)
                    ax.inc(sy.dma_start(
                        out=xbufA[:, (j % 2) * D:(j % 2 + 1) * D],
                        in_=x_d[j * 128:(j + 1) * 128, :]), j)
                    if j - 2 >= 0:
                        h1t_tile(j - 2)
            for jt in (4 * NG - 2, 4 * NG - 1):
                h1t_tile(jt)
            sy.wait_ge(ax.h(0), ax.totals()[0])
            sy.wait_ge(ax.h(1), ax.totals()[1])
            for sl in range(4):
                sy.wait_ge(ah1t.h(sl), ah1t.totals()[sl])
            bar.inc(sy.nop())
            sy.wait_ge(bar.h, 5)
            for sl in range(4):
                sy.wait_ge(dvw.h(sl), 16 * 8)
            for c in range(8):
                w1t.inc(sy.dma_start(out=w1p[:, c * H:(c + 1) * H],
                                     in_=wq1_d[:, c * 128:(c + 1) * 128],
                                     transpose=True), 16)
            for sl in range(4):
                sy.wait_ge(dvw.h(sl), 16 * 16)
            for r in range(32):
                w2t.inc(sy.dma_start(out=w2p[:, r * D:(r + 1) * D],
                                     in_=wq2_d[:, r * 128:(r + 1) * 128],
                                     transpose=True), 16)
            for j in range(NT):
                if j >= 1:
                    sy.wait_ge(batt.h, 2 * j)
                for c in range(8):
                    bqt.inc(sy.dma_start(
                        out=qT[:, c * 128:(c + 1) * 128],
                        in_=qh_d[j * 128:(j + 1) * 128, c * 128:(c + 1) * 128],
                        transpose=True), 16)
                if j >= 2:
                    sy.wait_ge(bsqx.h, j - 1)
                bx.inc(sy.dma_start(out=xbufB[:, (j % 2) * D:(j % 2 + 1) * D],
                                    in_=x_d[j * 128:(j + 1) * 128, :]), j)
            sy.wait_ge(bow.h, 16 * NT)

        @block.scalar
        def _(sc):
            smld.inc(sc.dma_start(out=rowsA[:, :], in_=bqkv_d), 16)
            smld.inc(sc.dma_start(out=boutrow[:, :], in_=bout_d), 16)
            smld.inc(sc.dma_start(out=onesb[:, :], in_=onesb_d.ap()), 16)
            smld.inc(sc.dma_start(out=onesf[:, :], in_=onesf_d.ap()), 16)
            smld.inc(sc.dma_start(out=onesrow[:, :], in_=onesrow_d.ap()), 16)
            smld.inc(sc.dma_start(out=identb[:, :], in_=ident_d.ap()), 16)


            def scA_front(j):
                P = (j % 2) * 16
                xs = xbufA[:, (j % 2) * D:(j % 2 + 1) * D]
                sc.wait_ge(ax.h(j % 2), 16 * (j // 2 + 1))
                asq.inc(sc.activation(out=sqscrA[:, :], in_=xs, func=AF.Square,
                                      accum_out=stA[:, P + 1:P + 2]))
                sc.wait_ge(avar.h, j + 1)
                asd.inc(sc.activation(out=stA[:, P + 5:P + 6],
                                      in_=stA[:, P + 4:P + 5], func=AF.Sqrt))
                sc.wait_ge(astat.h, j + 1)
                if j >= 2:
                    sc.wait_ge(ah1w.h(j % 2), 16 * (j // 2))
                ah1.inc(sc.activation(out=h1buf[:, (j % 2) * D:(j % 2 + 1) * D],
                                      in_=xs, func=AF.Identity,
                                      scale=stA[:, P + 6:P + 7],
                                      bias=stA[:, P + 7:P + 8]))
                sc.wait_ge(ah1.h, j + 1)
                ah1w.inc(sc.dma_start(out=h1_d[j * 128:(j + 1) * 128, :],
                                      in_=h1buf[:, (j % 2) * D:(j % 2 + 1) * D]), j)

            def scA_back(u):
                P = (u % 2) * 16
                sc.wait_ge(aevac.h, 6 * u + 2)
                sc.activation(out=sqscrA[:, :], in_=qkvb[:, 0:D], func=AF.Square,
                              accum_out=stA[:, P + 8:P + 9])
                sc.drain()
                aqs.inc(sc.activation(out=stA[:, P + 9:P + 10],
                                      in_=stA[:, P + 8:P + 9], func=AF.Sqrt))
                sc.wait_ge(aevac.h, 6 * u + 4)
                sc.activation(out=sqscrA[:, :], in_=qkvb[:, D:2 * D],
                              func=AF.Square, accum_out=stA[:, P + 10:P + 11])
                sc.drain()
                aks.inc(sc.activation(out=stA[:, P + 11:P + 12],
                                      in_=stA[:, P + 10:P + 11], func=AF.Sqrt))
                sc.wait_ge(aqr.h, u + 1)
                if u >= 2:
                    sc.wait_ge(aqw.h(u % 2), 16 * (u // 2))
                aqh.inc(sc.activation(out=qhat[:, (u % 2) * D:(u % 2 + 1) * D],
                                      in_=qkvb[:, 0:D], func=AF.Identity,
                                      scale=stA[:, P + 12:P + 13]))
                sc.wait_ge(aqh.h, u + 1)
                aqw.inc(sc.dma_start(out=qh_d[u * 128:(u + 1) * 128, :],
                                     in_=qhat[:, (u % 2) * D:(u % 2 + 1) * D]), u)
                sc.wait_ge(akr.h, u + 1)
                if u >= 1:
                    sc.wait_ge(akv.h, u)
                akh.inc(sc.activation(out=khat[:, :], in_=qkvb[:, D:2 * D],
                                      func=AF.Identity, scale=stA[:, P + 13:P + 14]))

            scq_pos = [0]
            quant_loads = sorted(dvq_of_load)
            first_w2q = [True]

            def sc_consume(n, u):
                for _ in range(n):
                    k0 = scq_pos[0]
                    if k0 >= len(quant_loads):
                        return
                    i = quant_loads[k0]
                    kind, ch = wplan[i]
                    if kind == "w1q" and u < min(10, NT - 1):
                        return
                    if kind == "w2q" and u < min(18, NT - 1):
                        return
                    scq_pos[0] = k0 + 1
                    kq = k0
                    sc_emit_tile.append(u)
                    if kq == 0:
                        sc.wait_ge(m1r.h, 1)
                    if kind == "w2q" and first_w2q[0]:
                        sc.wait_ge(m2r.h, 1)
                        first_w2q[0] = False
                    sc.wait_ge(wld.h(i % 5), 16 * (i // 5 + 1))
                    if kq >= 4:
                        sc.wait_ge(dvw.h(kq - 4), 16 * ((kq - 4) // 4 + 1))
                    scol = mcols[:, 1:2] if kind == "w1q" else mcols[:, 3:4]
                    abuf = qring_a(kq)
                    bbuf = qring_b(kq)
                    sc.activation(out=abuf, in_=wslot(i), func=AF.Sign,
                                  scale=scol, bias=mcols[:, 5:6])
                    scq.inc(sc.activation(out=bbuf, in_=wslot(i), func=AF.Sign,
                                          scale=scol, bias=mcols[:, 4:5]))

            for j in range(NT):
                scA_front(j)
                if j >= 3:
                    scA_back(j - 3)
                    sc_consume(4, j - 3)
            for u in range(max(0, NT - 3), NT):
                scA_back(u)
                sc_consume(4, u)
            sc_consume(10 ** 9, NT + 99)
            sc.wait_ge(ah1w.h(0), ah1w.totals()[0])
            sc.wait_ge(ah1w.h(1), ah1w.totals()[1])
            sc.wait_ge(aqw.h(0), aqw.totals()[0])
            sc.wait_ge(aqw.h(1), aqw.totals()[1])
            sc.wait_ge(smld.h, 16 * 6)
            bar.inc(sc.drain())
            sc.wait_ge(bar.h, 5)
            smld.inc(sc.dma_start(out=b2rowB, in_=b2_d), 16)

            def zT_evac_sc(t, r):
                g = 2 * t + r
                sc.wait_ge(bzt.h, g + 1)
                for cc in range(4):
                    c = 4 * r + cc
                    cp = sc.activation(
                        out=z1T[:, c * 256 + (t % 2) * 128:c * 256 + (t % 2) * 128 + 128],
                        in_=pzT[:, cc * 128:(cc + 1) * 128], func=AF.Copy)
                bze.inc(cp)

            def sc_front(t):
                P = (t % 2) * 16
                xr = xring[:, (t % 3) * D:(t % 3 + 1) * D]
                xs = xbufB[:, (t % 2) * D:(t % 2 + 1) * D]
                sc.wait_ge(bxp.h, t + 1)
                sc.activation(out=xs[:, 0:512], in_=xr[:, 0:512], func=AF.Square,
                              accum_out=stB[:, P + 1:P + 2])
                sc.activation(out=xs[:, 512:1024], in_=xr[:, 512:1024],
                              func=AF.Square, accum_out=stB[:, P + 14:P + 15])
                sc.drain()
                bsqx.inc(sc.activation(out=stB[:, P + 1:P + 2],
                                       in_=stB[:, P + 14:P + 15],
                                       func=AF.Identity,
                                       bias=stB[:, P + 1:P + 2]))
                sc.wait_ge(bvar.h, t + 1)
                bsd.inc(sc.activation(out=stB[:, P + 5:P + 6],
                                      in_=stB[:, P + 4:P + 5], func=AF.Sqrt))
                sc.wait_ge(bst.h, t + 1)
                if t >= 1:
                    sc.wait_ge(bzt.h, 2 * t)
                bz1.inc(sc.activation(out=z1buf[:, :], in_=xr, func=AF.Identity,
                                      scale=stB[:, P + 6:P + 7],
                                      bias=stB[:, P + 7:P + 8]))

            def sc_tail(j):
                P = (j % 2) * 16
                sc.wait_ge(bvar2.h, j + 1)
                bsd2.inc(sc.activation(out=stB[:, P + 10:P + 11],
                                       in_=stB[:, P + 9:P + 10], func=AF.Sqrt))
                sc.wait_ge(bfin.h, j + 1)
                bow.inc(sc.dma_start(out=out_d[j * 128:(j + 1) * 128, :],
                                     in_=outbuf[:, :]), 16)

            sc_front(0)
            sc_front(1)
            for p in range(NT // 2):
                j0, j1 = 2 * p, 2 * p + 1
                if j0 + 2 < NT:
                    sc_front(j0 + 2)
                if p >= 1:
                    sc.wait_ge(bsq4.h, 32 * j0)
                for m in range(32):
                    sc.wait_ge(by1.h, 32 * p + m + 1)
                    bh4.inc(sc.activation(
                        out=h4buf[:, m * 256:(m + 1) * 256],
                        in_=py1[m % 2][:, :], func=AF.Gelu,
                        bias=beta1c[:, m:m + 1]))
                if j0 + 2 < NT:
                    zT_evac_sc(j0 + 2, 0)
                    zT_evac_sc(j0 + 2, 1)
                sc_tail(j0)
                if j1 + 2 < NT:
                    sc_front(j1 + 2)
                sc_tail(j1)
            sc.wait_ge(bow.h, 16 * NT)

        @block.gpsimd
        def _(gp):
            for c in range(8):
                wqkvs.inc(gp.dma_start(out=wqkvp[:, c * E3:(c + 1) * E3],
                                       in_=wqkv_d[c * 128:(c + 1) * 128, :]), 16)
            for c in range(8):
                wouts.inc(gp.dma_start(out=woutp[:, c * D:(c + 1) * D],
                                       in_=wout_d[c * 128:(c + 1) * 128, :]), 16)
            with nc.allow_non_contiguous_dma("tiny columnize loads"):
                gmld.inc(gp.dma_start(
                    out=g1c[:, :], in_=g1_d.rearrange("o (c p) -> (o p) c", p=128)), 16)
                gmld.inc(gp.dma_start(
                    out=b1lc[:, :], in_=b1l_d.rearrange("o (c p) -> (o p) c", p=128)), 16)
                gmld.inc(gp.dma_start(
                    out=g2c[:, :], in_=g2_d.rearrange("o (c p) -> (o p) c", p=128)), 16)
                gmld.inc(gp.dma_start(
                    out=b2lc[:, :], in_=b2l_d.rearrange("o (c p) -> (o p) c", p=128)), 16)
                gmld.inc(gp.dma_start(
                    out=b1fc[:, :], in_=b1f_d.rearrange("o (c p) -> (o p) c", p=128)), 16)
            def dv_write(ii):
                kq = dvq_of_load[ii]
                gp.wait_ge(dcons.h, dv_cons_of_load[ii])
                kk, cc = wplan[ii]
                srcb = qring_a(kq)
                if kk == "w1q":
                    dvw.inc(gp.dma_start(
                        out=wq1_d[cc * 128:(cc + 1) * 128, :], in_=srcb), kq)
                else:
                    row, col = cc // 4, cc % 4
                    dvw.inc(gp.dma_start(
                        out=wq2_d[row * 128:(row + 1) * 128,
                                  col * 1024:(col + 1) * 1024], in_=srcb), kq)

            for i, (kind, ch) in enumerate(wplan):
                if i >= 5 and cons_engine[i - 5] == "dv":
                    gp.wait_ge(dcons.h, dv_cons_of_load[i - 5])
                wld.inc(gp.dma_start(out=wslot(i), in_=w_src(kind, ch)), i)
                if i >= 2:
                    if (i - 2) in dvq_of_load:
                        dv_write(i - 2)
            for ii in (NLOADS - 2, NLOADS - 1):
                if ii in dvq_of_load:
                    dv_write(ii)
            gp.wait_ge(wqkvs.h, 128)
            gp.wait_ge(wouts.h, 128)
            gp.wait_ge(gmld.h, 16 * 5)
            for sl in range(5):
                gp.wait_ge(wld.h(sl), wld.totals()[sl])
            for sl in range(4):
                gp.wait_ge(dvw.h(sl), dvw.totals()[sl])
            bar.inc(gp.drain())
            gp.wait_ge(bar.h, 5)

        p_mtot1 = (min(1, NG - 1), 3)
        p_mbc1 = (min(2, NG - 1), 1) if NG >= 3 else p_mtot1
        p_mtot2 = (min(3, NG - 1), 3) if NG >= 4 else p_mbc1
        p_mbc2 = (min(4, NG - 1), 1) if NG >= 5 else p_mtot2

        @block.tensor
        def _(te):
            te.wait_ge(smld.h, 16 * 6)
            for e in range(6):
                if e >= 2:
                    te.wait_ge(binit.h, e - 1)
                abc.inc(te.matmul(out=pq[e % 2][:, :], lhsT=onesrow[:, :],
                                  rhs=rowsA[0:1, e * 512:(e + 1) * 512],
                                  start=True, stop=True))
            for hh in range(2):
                te.wait_ge(binit.h, 5 + hh)
                abc.inc(te.matmul(out=pq[hh][:, :], lhsT=onesrow[:, :],
                                  rhs=boutrow[0:1, hh * 512:(hh + 1) * 512],
                                  start=True, stop=True))
            te.wait_ge(binit.h, 8)
            te.wait_ge(wqkvs.h, 128)
            for g in range(NG):
                for s in range(4):
                    te.wait_ge(ah1t.h(s), 128 * (g + 1))
                    for e in range(6):
                        i = (4 * g + s) * 6 + e
                        if i >= 4:
                            te.wait_ge(aevac.h, i - 3)
                        for c in range(8):
                            mm = te.matmul(
                                out=pq[i % 4][:, :],
                                lhsT=h1T[:, (g % 2) * 4096 + c * 512 + s * 128:
                                         (g % 2) * 4096 + c * 512 + (s + 1) * 128],
                                rhs=wqkvp[:, c * E3 + e * 512:c * E3 + (e + 1) * 512],
                                start=(c == 0), stop=(c == 7))
                        aqkv.inc(mm)
                    if (g, s) == p_mtot1:
                        te.wait_ge(mpre.h, 1)
                        mtot.inc(te.matmul(out=pone[:, 0:1],
                                           lhsT=stA[:, 30:31], rhs=onesf[:, :],
                                           start=True, stop=True))
                    if (g, s) == p_mbc1:
                        te.wait_ge(mval.h, 1)
                        mbc.inc(te.matmul(out=pmb[:, 0:1], lhsT=onesrow[:, :],
                                          rhs=mrow[0:1, 0:1],
                                          start=True, stop=True))
                    if (g, s) == p_mtot2:
                        te.wait_ge(mpre.h, 2)
                        mtot.inc(te.matmul(out=pone[:, 1:2],
                                           lhsT=stA[:, 31:32], rhs=onesf[:, :],
                                           start=True, stop=True))
                    if (g, s) == p_mbc2:
                        te.wait_ge(mval.h, 2)
                        mbc.inc(te.matmul(out=pmb[:, 1:2], lhsT=onesrow[:, :],
                                          rhs=mrow[0:1, 1:2],
                                          start=True, stop=True))
            te.wait_ge(akv.h, NT)
            te.wait_ge(aevac.h, 6 * NT)
            for c in range(8):
                kvc.inc(te.matmul(out=pkv[:, c:c + 1],
                                  lhsT=kv_acc[:, c * 128:(c + 1) * 128],
                                  rhs=onesf[:, :], start=True, stop=True))
            bar.inc(te.drain())
            te.wait_ge(bar.h, 5)
            te.wait_ge(bhlv.h, 1)
            te.wait_ge(w1t.h, 128)
            for m in range(32):
                if m >= 2:
                    te.wait_ge(bb1ev.h, m - 1)
                for c in range(8):
                    mm = te.matmul(out=pcol[m % 2][:, :],
                                   lhsT=w1p[:, c * H + m * 128:c * H + (m + 1) * 128],
                                   rhs=b1lc[:, c:c + 1],
                                   start=(c == 0), stop=(c == 7))
                bb1.inc(mm)
            te.wait_ge(w2t.h, 16 * 32)
            for r in range(32):
                for hh in range(2):
                    mm = te.matmul(out=prow[0:1, hh * 512:(hh + 1) * 512],
                                   lhsT=b2lc[:, r:r + 1],
                                   rhs=w2p[:, r * D + hh * 512:r * D + (hh + 1) * 512],
                                   start=(r == 0), stop=(r == 31))
            bb2.inc(mm)
            te.wait_ge(bb2r.h, 1)
            for hh in range(2):
                bpb.inc(te.matmul(out=pbc[hh][:, :], lhsT=onesrow[:, :],
                                  rhs=outbuf[0:1, hh * 512:(hh + 1) * 512],
                                  start=True, stop=True))
            te.wait_ge(w2s.h, 1)
            for r in range(32):
                for hh in range(2):
                    mm = te.matmul(out=prow[0:1, hh * 512:(hh + 1) * 512],
                                   lhsT=onesb[:, :],
                                   rhs=w2p[:, r * D + hh * 512:r * D + (hh + 1) * 512],
                                   start=(r == 0), stop=(r == 31))
            bcs.inc(mm)
            te.wait_ge(bcsr.h, 1)
            for hh in range(2):
                bpb.inc(te.matmul(out=pbc[hh][:, :], lhsT=onesrow[:, :],
                                  rhs=outbuf[0:1, hh * 512:(hh + 1) * 512],
                                  start=True, stop=True))
            def attn_mm(t):
                te.wait_ge(bqt.h, 128 * (t + 1))
                te.wait_ge(kvw.h, 1)
                if t >= 1:
                    te.wait_ge(bxp.h, t)
                for hh in range(2):
                    for c in range(8):
                        mm = te.matmul(
                            out=patt[:, hh * 512:(hh + 1) * 512],
                            lhsT=qT[:, c * 128:(c + 1) * 128],
                            rhs=woutp[:, c * D + hh * 512:c * D + (hh + 1) * 512],
                            start=(c == 0), stop=(c == 7))
                    batt.inc(mm)

            def y2_tile(j, zt_t=None):
                p = j // 2
                te.wait_ge(bprep.h, 1)
                if j >= 1:
                    te.wait_ge(bfin.h, j)
                    te.wait_ge(bsumrd.h, j)
                for m in range(32):
                    if m == 28 and zt_t is not None:
                        zT_round(zt_t, 0)
                    q = 32 * j + m
                    te.wait_ge(bh4.h, 32 * p + m + 1)
                    h4m = h4buf[:, m * 256 + (j % 2) * 128:m * 256 + (j % 2) * 128 + 128]
                    for hh in range(2):
                        mm = te.matmul(out=py2[:, hh * 512:(hh + 1) * 512],
                                       lhsT=h4m,
                                       rhs=w2p[:, m * D + hh * 512:m * D + (hh + 1) * 512],
                                       start=(m == 0), stop=(m == 31))
                        if hh == 1 and m == 31:
                            by2.inc(mm)
                    te.matmul(out=psumSQ[:, m:m + 1], lhsT=h4m, rhs=onesb[:, :],
                              start=True, stop=True)
                    te.wait_ge(bsq.h, q + 1)
                    bsq4.inc(te.matmul(out=psumSQ[:, 32 + m:33 + m],
                                       lhsT=h4sq[:, (m % 4) * 128:(m % 4 + 1) * 128],
                                       rhs=onesb[:, :], start=True, stop=True))

            def zT_round(t, r):
                g = 2 * t + r
                if r == 0:
                    te.wait_ge(bz1.h, t + 1)
                if g >= 1:
                    te.wait_ge(bze.h, g)
                for cc in range(4):
                    c = 4 * r + cc
                    mm = te.matmul(out=pzT[:, cc * 128:(cc + 1) * 128],
                                   lhsT=z1buf[:, c * 128:(c + 1) * 128],
                                   rhs=identb[:, :], start=True, stop=True)
                bzt.inc(mm)

            attn_mm(0)
            attn_mm(1)
            for t in (0, 1):
                zT_round(t, 0)
                zT_round(t, 1)
            for p in range(NT // 2):
                j0, j1 = 2 * p, 2 * p + 1
                if j0 + 2 < NT:
                    attn_mm(j0 + 2)
                te.wait_ge(bze.h, 4 * p + 4)
                te.wait_ge(w1s.h, 1)
                for m in range(32):
                    if 32 * p + m >= 2:
                        te.wait_ge(bh4.h, 32 * p + m - 1)
                    for c in range(8):
                        mm = te.matmul(
                            out=py1[m % 2][:, :],
                            lhsT=w1p[:, c * H + m * 128:c * H + (m + 1) * 128],
                            rhs=z1T[:, c * 256:(c + 1) * 256],
                            start=(c == 0), stop=(c == 7))
                    by1.inc(mm)
                if j1 + 2 < NT:
                    attn_mm(j1 + 2)
                if j0 + 2 < NT:
                    zT_round(j0 + 2, 0)
                    zT_round(j0 + 2, 1)
                y2_tile(j0)
                y2_tile(j1, zt_t=(j1 + 2 if j1 + 2 < NT else None))
                if j1 + 2 < NT:
                    zT_round(j1 + 2, 1)

        @block.vector
        def _(dv):
            dv.memset(kv_acc[:, :], 0.0)
            dv.memset(mcols[:, 4:5], 0.5)
            dv.memset(mcols[:, 5:6], -0.5)
            dv.wait_ge(gmld.h, 16 * 5)
            dv.tensor_scalar(out=b1lc[:, :], in0=b1lc[:, :], scalar1=0.5,
                             scalar2=None, op0=ALU.mult)
            bhlv.inc(dv.tensor_scalar(out=b2lc[:, :], in0=b2lc[:, :],
                                      scalar1=0.5, scalar2=None, op0=ALU.mult))
            for e in range(6):
                dv.wait_ge(abc.h, e + 1)
                binit.inc(dv.tensor_copy(out=bqkvb[:, e * 512:(e + 1) * 512],
                                         in_=pq[e % 2][:, :]))
            for hh in range(2):
                dv.wait_ge(abc.h, 7 + hh)
                binit.inc(dv.tensor_copy(out=boutb[:, hh * 512:(hh + 1) * 512],
                                         in_=pq[hh][:, :]))
            dvq_pos = [0]

            def dv_consume(n, jcur, limit=None, max_i=None):
                for _ in range(n):
                    k0 = dvq_pos[0]
                    if k0 >= NDV or (limit is not None and k0 >= limit):
                        return
                    if max_i is not None and dvq_loads[k0] >= max_i:
                        return
                    i0 = dvq_loads[k0]
                    kq0 = dvq_of_load.get(i0)
                    if kq0 is not None and (kq0 >= len(sc_emit_tile)
                                            or sc_emit_tile[kq0] > jcur - 3):
                        return
                    dvq_pos[0] = k0 + 1
                    i = dvq_loads[k0]
                    kind, ch = wplan[i]
                    if kind in ("w1m", "w2m"):
                        dv.wait_ge(wld.h(i % 5), 16 * (i // 5 + 1))
                        col = ch if kind == "w1m" else NW1 + ch
                        dcons.inc(dv.tensor_reduce(
                            out=wpart[:, col:col + 1], in_=wslot(i),
                            axis=AX.X, op=ALU.add, apply_absolute_value=True))
                        continue
                    kq = dvq_of_load[i]
                    dst = qring_a(kq)
                    srcb = qring_b(kq)
                    dv.wait_ge(scq.h, kq + 1)
                    dcons.inc(dv.tensor_tensor(out=dst, in0=dst, in1=srcb,
                                               op=ALU.add))

            def dvA_front(j):
                P = (j % 2) * 16
                xs = xbufA[:, (j % 2) * D:(j % 2 + 1) * D]
                dv.wait_ge(ax.h(j % 2), 16 * (j // 2 + 1))
                if j >= 2:
                    dv.wait_ge(ah1.h, j - 1)
                    dv.wait_ge(asd.h, j - 1)
                dv.reduce_sum(out=stA[:, P + 0:P + 1], in_=xs, axis=AX.X,
                              negate=True)
                dv.drain()
                dv.tensor_scalar(out=stA[:, P + 2:P + 3], in0=stA[:, P + 0:P + 1],
                                 scalar1=1.0 / D, scalar2=None, op0=ALU.mult)
                dv.wait_ge(asq.h, j + 1)
                dv.tensor_scalar(out=stA[:, P + 3:P + 4], in0=stA[:, P + 1:P + 2],
                                 scalar1=1.0 / D, scalar2=EPS,
                                 op0=ALU.mult, op1=ALU.add)
                dv.drain()
                dv.tensor_tensor(out=stA[:, P + 4:P + 5], in0=stA[:, P + 2:P + 3],
                                 in1=stA[:, P + 2:P + 3], op=ALU.mult)
                dv.drain()
                avar.inc(dv.tensor_tensor(out=stA[:, P + 4:P + 5],
                                          in0=stA[:, P + 3:P + 4],
                                          in1=stA[:, P + 4:P + 5],
                                          op=ALU.subtract))
                dv.wait_ge(asd.h, j + 1)
                dv.reciprocal(out=stA[:, P + 6:P + 7], in_=stA[:, P + 5:P + 6])
                dv.drain()
                astat.inc(dv.tensor_tensor(out=stA[:, P + 7:P + 8],
                                           in0=stA[:, P + 2:P + 3],
                                           in1=stA[:, P + 6:P + 7], op=ALU.mult))
            def dvA_back(u):
                P = (u % 2) * 16
                for e in range(6):
                    i = u * 6 + e
                    dv.wait_ge(aqkv.h, i + 1)
                    if e == 0 and u >= 1:
                        dv.wait_ge(aqh.h, u)
                    if e == 2 and u >= 1:
                        dv.wait_ge(akh.h, u)
                    aevac.inc(dv.tensor_tensor(
                        out=qkvb[:, e * 512:(e + 1) * 512],
                        in0=pq[i % 4][:, :],
                        in1=bqkvb[:, e * 512:(e + 1) * 512], op=ALU.add))
                dv.wait_ge(aqs.h, u + 1)
                aqr.inc(dv.reciprocal(out=stA[:, P + 12:P + 13],
                                      in_=stA[:, P + 9:P + 10]))
                dv.wait_ge(aks.h, u + 1)
                akr.inc(dv.reciprocal(out=stA[:, P + 13:P + 14],
                                      in_=stA[:, P + 11:P + 12]))
                dv.wait_ge(akh.h, u + 1)
                dv.tensor_tensor(out=kvtmp[:, :], in0=khat[:, :],
                                 in1=qkvb[:, 2 * D:3 * D], op=ALU.mult)
                dv.drain()
                akv.inc(dv.tensor_tensor(out=kv_acc[:, :], in0=kv_acc[:, :],
                                         in1=kvtmp[:, :], op=ALU.add))
            def dvA_extra(u):
                if u == min(5, NT - 1):
                    assert dvq_pos[0] >= NW1 or NT < 32, dvq_pos[0]
                    dv.drain()
                    mpre.inc(dv.reduce_sum(out=stA[:, 30:31],
                                           in_=wpart[:, 0:NW1], axis=AX.X))
                if u == min(8, NT - 1):
                    dv.wait_ge(mtot.h, 1)
                    mval.inc(dv.tensor_scalar(out=mrow[0:1, 0:1],
                                              in0=pone[0:1, 0:1],
                                              scalar1=1.0 / (H * D),
                                              scalar2=EPS,
                                              op0=ALU.mult, op1=ALU.max))
                if u == min(10, NT - 1):
                    dv.wait_ge(mbc.h, 1)
                    dv.tensor_copy(out=mcols[:, 0:1], in_=pmb[:, 0:1])
                    dv.drain()
                    m1r.inc(dv.reciprocal(out=mcols[:, 1:2],
                                          in_=mcols[:, 0:1]))
                    dv.drain()
                if u == min(12, NT - 1):
                    assert dvq_pos[0] >= NW1 + NW2 or NT < 32, dvq_pos[0]
                    dv.drain()
                    mpre.inc(dv.reduce_sum(out=stA[:, 31:32],
                                           in_=wpart[:, NW1:NW1 + NW2],
                                           axis=AX.X))
                if u == min(16, NT - 1):
                    dv.wait_ge(mtot.h, 2)
                    mval.inc(dv.tensor_scalar(out=mrow[0:1, 1:2],
                                              in0=pone[0:1, 1:2],
                                              scalar1=1.0 / (H * D),
                                              scalar2=EPS,
                                              op0=ALU.mult, op1=ALU.max))
                if u == min(18, NT - 1):
                    dv.wait_ge(mbc.h, 2)
                    dv.tensor_copy(out=mcols[:, 2:3], in_=pmb[:, 1:2])
                    dv.drain()
                    m2r.inc(dv.reciprocal(out=mcols[:, 3:4],
                                          in_=mcols[:, 2:3]))
                    dv.drain()

            for j in range(NT):
                dvA_front(j)
                if j >= 3:
                    dvA_back(j - 3)
                    dvA_extra(j - 3)
                if NT >= 32:
                    dv_consume(6, j)
                else:
                    dv_consume(64 // NT + 1, j, limit=NW1 + NW2)
            for u in range(max(0, NT - 3), NT):
                dvA_back(u)
                dvA_extra(u)
            dv_consume(10 ** 9, NT + 999)
            dv.wait_ge(kvc.h, 8)
            dv.tensor_copy(out=kvT[:, :], in_=pkv[:, :])
            dv.drain()
            dv.wait_ge(wouts.h, 128)
            for c in range(8):
                dv.tensor_scalar(out=woutp[:, c * D:(c + 1) * D],
                                 in0=woutp[:, c * D:(c + 1) * D],
                                 scalar1=kvT[:, c:c + 1], scalar2=None,
                                 op0=ALU.mult)
            kvw.inc(dv.drain())
            bar.inc(dv.drain())
            dv.wait_ge(bar.h, 5)
            dv.tensor_scalar(out=g1m1c[:, :], in0=g1c[:, :],
                             scalar1=mcols[:, 0:1], scalar2=0.5, op0=ALU.mult,
                             op1=ALU.mult)
            dv.tensor_scalar(out=g2m2c[:, :], in0=g2c[:, :],
                             scalar1=mcols[:, 2:3], scalar2=0.5, op0=ALU.mult,
                             op1=ALU.mult)
            for m in range(32):
                dv.wait_ge(bb1.h, m + 1)
                bb1ev.inc(dv.scalar_tensor_tensor(
                    out=beta1c[:, m:m + 1], in0=pcol[m % 2][:, :],
                    scalar=mcols[:, 0:1], in1=b1fc[:, m:m + 1],
                    op0=ALU.mult, op1=ALU.add))
            dv.drain()
            for c in range(8):
                dv.tensor_scalar(out=w1p[:, c * H:(c + 1) * H],
                                 in0=w1p[:, c * H:(c + 1) * H],
                                 scalar1=g1m1c[:, c:c + 1], scalar2=None,
                                 op0=ALU.mult)
            w1s.inc(dv.drain())
            dv.wait_ge(bb2.h, 1)
            dv.wait_ge(smld.h, 16 * 7)
            bb2r.inc(dv.scalar_tensor_tensor(
                out=b2rowB, in0=prow[0:1, :], scalar=mcols[0:1, 2:3],
                in1=b2rowB, op0=ALU.mult, op1=ALU.add))
            dv.drain()
            for r in range(32):
                dv.tensor_scalar(out=w2p[:, r * D:(r + 1) * D],
                                 in0=w2p[:, r * D:(r + 1) * D],
                                 scalar1=g2m2c[:, r:r + 1], scalar2=None,
                                 op0=ALU.mult)
            w2s.inc(dv.drain())
            dv.wait_ge(bpb.h, 2)
            dv.tensor_copy(out=beta2b[:, 0:512], in_=pbc[0][:, :])
            dv.tensor_copy(out=beta2b[:, 512:1024], in_=pbc[1][:, :])
            dv.wait_ge(bcs.h, 1)
            bcsr.inc(dv.tensor_copy(out=b2rowB, in_=prow[0:1, :]))
            dv.wait_ge(bpb.h, 4)
            dv.tensor_copy(out=cs2b[:, 0:512], in_=pbc[0][:, :])
            bprep.inc(dv.tensor_copy(out=cs2b[:, 512:1024], in_=pbc[1][:, :]))
            def dv_front(t):
                P = (t % 2) * 16
                xs = xbufB[:, (t % 2) * D:(t % 2 + 1) * D]
                xr = xring[:, (t % 3) * D:(t % 3 + 1) * D]
                dv.wait_ge(bx.h(t % 2), 16 * (t // 2 + 1))
                dv.tensor_tensor(out=xs, in0=xs, in1=boutb[:, :], op=ALU.add)
                dv.drain()
                dv.wait_ge(batt.h, 2 * t + 2)
                if t >= 2:
                    dv.wait_ge(bz1.h, t - 1)
                    dv.wait_ge(bsd.h, t - 1)
                bxp.inc(dv.tensor_tensor(out=xr, in0=patt[:, :], in1=xs,
                                         op=ALU.add))
                dv.drain()
                dv.reduce_sum(out=stB[:, P + 0:P + 1], in_=xr, axis=AX.X,
                              negate=True)
                dv.drain()
                dv.tensor_scalar(out=stB[:, P + 2:P + 3], in0=stB[:, P + 0:P + 1],
                                 scalar1=1.0 / D, scalar2=None, op0=ALU.mult)
                dv.wait_ge(bsqx.h, t + 1)
                dv.tensor_scalar(out=stB[:, P + 3:P + 4], in0=stB[:, P + 1:P + 2],
                                 scalar1=(1.0 + EPS) / D, scalar2=EPS * EPS,
                                 op0=ALU.mult, op1=ALU.add)
                dv.drain()
                dv.tensor_tensor(out=stB[:, P + 4:P + 5], in0=stB[:, P + 2:P + 3],
                                 in1=stB[:, P + 2:P + 3], op=ALU.mult)
                dv.drain()
                bvar.inc(dv.scalar_tensor_tensor(
                    out=stB[:, P + 4:P + 5], in0=stB[:, P + 4:P + 5],
                    scalar=-(1.0 + EPS), in1=stB[:, P + 3:P + 4],
                    op0=ALU.mult, op1=ALU.add))
                dv.wait_ge(bsd.h, t + 1)
                dv.reciprocal(out=stB[:, P + 6:P + 7], in_=stB[:, P + 5:P + 6])
                dv.drain()
                bst.inc(dv.tensor_tensor(out=stB[:, P + 7:P + 8],
                                         in0=stB[:, P + 2:P + 3],
                                         in1=stB[:, P + 6:P + 7], op=ALU.mult))

            def dv_h4sq(j):
                p = j // 2
                if j >= 2:
                    dv.wait_ge(bsq4.h, 32 * (j - 1))
                for m in range(32):
                    q = 32 * j + m
                    dv.wait_ge(bh4.h, 32 * p + m + 1)
                    if q >= 4:
                        dv.wait_ge(bsq4.h, q - 3)
                    h4m = h4buf[:, m * 256 + (j % 2) * 128:m * 256 + (j % 2) * 128 + 128]
                    bsq.inc(dv.tensor_tensor(
                        out=h4sq[:, (m % 4) * 128:(m % 4 + 1) * 128],
                        in0=h4m, in1=h4m, op=ALU.mult))

            def dv_tail(j):
                P = (j % 2) * 16
                xr = xring[:, (j % 3) * D:(j % 3 + 1) * D]
                dv.wait_ge(bsq4.h, 32 * (j + 1))
                if j >= 2:
                    dv.wait_ge(bsd2.h, j - 1)
                dv.reduce_sum(out=stB[:, P + 8:P + 9], in_=psumSQ[:, 0:32],
                              axis=AX.X, negate=True)
                bsumrd.inc(dv.reduce_sum(out=stB[:, P + 9:P + 10],
                                         in_=psumSQ[:, 32:64], axis=AX.X))
                dv.drain()
                dv.tensor_scalar(out=stB[:, P + 8:P + 9],
                                 in0=stB[:, P + 8:P + 9],
                                 scalar1=1.0 / H, scalar2=None, op0=ALU.mult)
                dv.tensor_scalar(out=stB[:, P + 9:P + 10],
                                 in0=stB[:, P + 9:P + 10],
                                 scalar1=1.0 / H, scalar2=EPS,
                                 op0=ALU.mult, op1=ALU.add)
                dv.drain()
                dv.tensor_tensor(out=stB[:, P + 11:P + 12],
                                 in0=stB[:, P + 8:P + 9],
                                 in1=stB[:, P + 8:P + 9], op=ALU.mult)
                dv.drain()
                bvar2.inc(dv.tensor_tensor(out=stB[:, P + 9:P + 10],
                                           in0=stB[:, P + 9:P + 10],
                                           in1=stB[:, P + 11:P + 12],
                                           op=ALU.subtract))
                dv.wait_ge(bsd2.h, j + 1)
                dv.reciprocal(out=stB[:, P + 12:P + 13],
                              in_=stB[:, P + 10:P + 11])
                dv.drain()
                dv.tensor_tensor(out=stB[:, P + 13:P + 14],
                                 in0=stB[:, P + 8:P + 9],
                                 in1=stB[:, P + 12:P + 13], op=ALU.mult)
                dv.drain()
                dv.wait_ge(by2.h, j + 1)
                if j >= 1:
                    dv.wait_ge(bow.h, 16 * j)
                dv.scalar_tensor_tensor(out=outbuf[:, :], in0=py2[:, :],
                                        scalar=stB[:, P + 12:P + 13],
                                        in1=beta2b[:, :], op0=ALU.mult,
                                        op1=ALU.add)
                dv.scalar_tensor_tensor(out=xr, in0=cs2b[:, :],
                                        scalar=stB[:, P + 13:P + 14], in1=xr,
                                        op0=ALU.mult, op1=ALU.add)
                dv.drain()
                bfin.inc(dv.tensor_tensor(out=outbuf[:, :], in0=outbuf[:, :],
                                          in1=xr, op=ALU.add))

            def zT_evac(t, r):
                g = 2 * t + r
                dv.wait_ge(bzt.h, g + 1)
                for cc in range(4):
                    c = 4 * r + cc
                    cp = dv.tensor_copy(
                        out=z1T[:, c * 256 + (t % 2) * 128:c * 256 + (t % 2) * 128 + 128],
                        in_=pzT[:, cc * 128:(cc + 1) * 128])
                bze.inc(cp)

            dv_front(0)
            dv_front(1)
            for t in (0, 1):
                zT_evac(t, 0)
                zT_evac(t, 1)
            for p in range(NT // 2):
                j0, j1 = 2 * p, 2 * p + 1
                if j0 + 2 < NT:
                    dv_front(j0 + 2)
                dv_h4sq(j0)
                dv_tail(j0)
                if j1 + 2 < NT:
                    dv_front(j1 + 2)
                dv_h4sq(j1)
                if j1 + 2 < NT:
                    zT_evac(j1 + 2, 0)
                    zT_evac(j1 + 2, 1)
                dv_tail(j1)

    expect = {
        "asq": NT, "avar": NT, "asd": NT, "astat": NT, "ah1": NT,
        "aqkv": 6 * NT, "aevac": 6 * NT,
        "aqs": NT, "aqr": NT, "aqh": NT,
        "aks": NT, "akr": NT, "akh": NT, "akv": NT,
        "wqkvs": 128, "wouts": 128, "smld": 16 * 7,
        "gcons": ng, "dcons": nd, "gmld": 16 * 5, "scq": 64, "bhlv": 1,
        "mpre": 2, "mtot": 2, "mval": 2, "mbc": 2, "m1r": 1, "m2r": 1,
        "abc": 8, "binit": 8, "kvc": 8, "kvw": 1, "bar": 5,
        "w1t": 128, "w2t": 512,
        "bb1": 32, "bb1ev": 32, "w1s": 1, "bb2": 1, "bb2r": 1, "w2s": 1,
        "bcs": 1, "bcsr": 1, "bpb": 4, "bprep": 1,
        "batt": 2 * NT, "bxp": NT,
        "bsqx": NT, "bvar": NT, "bsd": NT, "bst": NT, "bz1": NT,
        "by1": 16 * NT, "bh4": 16 * NT, "bsq": 32 * NT, "bsq4": 32 * NT,
        "bsumrd": NT, "by2": NT, "bvar2": NT, "bsd2": NT,
        "bfin": NT, "bow": 16 * NT,
        "bzt": 2 * NT, "bze": 2 * NT,
    }
    dexpect = {"ax": 16 * NT, "ah1w": 16 * NT, "ah1t": 512 * NG,
               "aqw": 16 * NT, "wld": 16 * NLOADS,
               "dvw": 16 * (NW1 + NW2),
               "bx": 16 * NT}
    for ds, tot in [(ax, dexpect["ax"]), (ah1w, dexpect["ah1w"]),
                    (ah1t, dexpect["ah1t"]), (aqw, dexpect["aqw"]),
                    (wld, dexpect["wld"]), (dvw, dexpect["dvw"]),
                    (bx, dexpect["bx"])]:
        assert sum(ds.totals()) == tot, \
            f"dsem {ds.name}: {ds.totals()} != {tot}"
    for s in allsems:
        if s.name in expect:
            assert s.n == expect[s.name], \
                f"sem {s.name}: emitted {s.n}, expected {expect[s.name]}"
    return nc


def _shard_inputs(inputs):
    x = np.ascontiguousarray(np.asarray(inputs["L0_attn_x"], dtype=np.float32))
    common = {
        "wqkv": np.ascontiguousarray(np.asarray(inputs["L0_attn_Wqkv"], np.float32)),
        "bqkv": np.ascontiguousarray(np.asarray(inputs["L0_attn_bqkv"], np.float32).reshape(1, E3)),
        "wout": np.ascontiguousarray(np.asarray(inputs["L0_attn_Wout"], np.float32)),
        "bout": np.ascontiguousarray(np.asarray(inputs["L0_attn_bout"], np.float32).reshape(1, D)),
        "w1": np.ascontiguousarray(np.asarray(inputs["L0_ffn_w1"], np.float32)),
        "b1f": np.ascontiguousarray(np.asarray(inputs["L0_ffn_b1"], np.float32).reshape(1, H)),
        "g1": np.ascontiguousarray(np.asarray(inputs["L0_ffn_ln1_g"], np.float32).reshape(1, D)),
        "b1l": np.ascontiguousarray(np.asarray(inputs["L0_ffn_ln1_b"], np.float32).reshape(1, D)),
        "w2": np.ascontiguousarray(np.asarray(inputs["L0_ffn_w2"], np.float32)),
        "g2": np.ascontiguousarray(np.asarray(inputs["L0_ffn_ln2_g"], np.float32).reshape(1, H)),
        "b2l": np.ascontiguousarray(np.asarray(inputs["L0_ffn_ln2_b"], np.float32).reshape(1, H)),
        "b2": np.ascontiguousarray(np.asarray(inputs["L0_ffn_b2"], np.float32).reshape(1, D)),
    }
    per_core = []
    for b in range(x.shape[0]):
        m = {"x": np.ascontiguousarray(x[b])}
        m.update(common)
        per_core.append(m)
    return per_core


def kernel(**inputs):
    per_core = _shard_inputs(inputs)
    T = per_core[0]["x"].shape[0]
    nc = build_program(T)
    res = run_bass_kernel_spmd(nc, per_core, core_ids=list(range(len(per_core))))
    return np.stack([r["out"] for r in res.results], axis=0)
